# version 15
# baseline (speedup 1.0000x reference)
import contextlib
import numpy as np

import concourse.bass as bass
import concourse.mybir as mybir

F32 = mybir.dt.float32
BF16 = mybir.dt.bfloat16
AF = mybir.ActivationFunctionType
ALU = mybir.AluOpType
AX = mybir.AxisListType

P = 128
D = 1024
HID = 4096
DC = D // 128
C32 = 1.5 * 2.0 ** 23
C52 = 1.5 * 2.0 ** 52
LN16 = float(np.log(16.0))
EPS = 1e-5
ISQ = 1.0 / 32.0
ENGN = ("sync", "gpsimd", "vector", "scalar", "tensor")


class _S:
    __slots__ = ("h", "v")

    def __init__(self, h):
        self.h = h
        self.v = 0


class Plan:
    def __init__(self, nc, es):
        self.nc = nc
        self.es = es
        self.fns = {e: [] for e in ENGN}
        self.sems = {}
        self.mu = {}
        for e in ENGN:
            self.mu[e] = self.sem("mu_" + e)

    def sem(self, name):
        s = _S(self.es.enter_context(self.nc.semaphore(name)))
        self.sems[name] = s
        return s

    vec_pending = None

    def emit(self, engine, build, waits=(), incs=()):
        for s, n in incs:
            s.v += n
        wl = [(s.h, int(v)) for s, v in waits if v > 0]
        il = [(s.h, n) for s, n in incs]

        def fn(eng, build=build, wl=wl, il=il):
            for h, v in wl:
                eng.wait_ge(h, v)
            instr = build(eng)
            for h, n in il:
                instr.then_inc(h, n)
        if engine == "vector" and self.vec_pending is not None:
            self.vec_pending.append(fn)
        else:
            self.fns[engine].append(fn)

    def pump(self, n):
        while n > 0 and self.vec_pending:
            self.fns["vector"].append(self.vec_pending.pop(0))
            n -= 1

    def pump_all(self):
        if self.vec_pending:
            self.fns["vector"] += self.vec_pending
            self.vec_pending = []

    def op(self, engine, build, after=(), extra=()):
        extra = list(extra)
        if extra:
            assert len(extra) == 1
            self.emit(engine, build, waits=after, incs=extra)
            return (extra[0][0], extra[0][0].v)
        m = self.mu[engine]
        self.emit(engine, build, waits=after, incs=[(m, 1)])
        return (m, m.v)

    def waits_only(self, engine, waits):
        wl = [(s.h, int(v)) for s, v in waits if v > 0]

        def fn(eng, wl=wl):
            for h, v in wl:
                eng.wait_ge(h, v)
        self.fns[engine].append(fn)

    def snapshot(self):
        return {k: s.v for k, s in self.sems.items()}

    def barrier(self):
        cur = self.snapshot()
        waits = [(self.sems[k], v) for k, v in cur.items() if v > 0]
        for e in ENGN:
            self.waits_only(e, waits)


def build(seq=8192):
    assert seq % 2048 == 0
    NSLOT = seq // 1024
    NOWN = NSLOT * 256
    NRNG = seq // 1024
    NORNG = max(1, NOWN // 1024)
    ORNG = NOWN // NORNG
    NHT = NOWN // P
    NT8 = NOWN // 256
    L = [8 * s + 8 for s in range(NSLOT)]
    W1TILES = DC
    W2TILES = HID // P
    NWT = W1TILES + W2TILES
    NHC = HID // 256

    nc = bass.Bass("TRN2", target_bir_lowering=False)

    ap_x = nc.dram_tensor("x", [seq, D], F32, kind="ExternalInput").ap()
    ap_xo = nc.dram_tensor("x_own", [NOWN, D], F32, kind="ExternalInput").ap()
    ap_qpos = nc.dram_tensor("qpos", [P, NSLOT], F32, kind="ExternalInput").ap()
    ap_wqT = nc.dram_tensor("wqT", [D, D], F32, kind="ExternalInput").ap()
    ap_wkT = nc.dram_tensor("wkT", [D, D], F32, kind="ExternalInput").ap()
    ap_wvT = nc.dram_tensor("wvT", [D, D], F32, kind="ExternalInput").ap()
    ap_w1T = nc.dram_tensor("w1T", [D, HID], F32, kind="ExternalInput").ap()
    ap_w2T = nc.dram_tensor("w2T", [HID, D], F32, kind="ExternalInput").ap()
    ap_b1t = nc.dram_tensor("b1t", [P, HID // P], F32, kind="ExternalInput").ap()
    ap_b2bc = nc.dram_tensor("b2bc", [P, D], F32, kind="ExternalInput").ap()
    s_lnx = nc.dram_tensor("s_lnx", [seq, D], BF16).ap()
    s_lnxo = nc.dram_tensor("s_lnxo", [NOWN, D], BF16).ap()
    s_kT = nc.dram_tensor("s_kT", [D, seq], BF16).ap()
    s_v = nc.dram_tensor("s_v", [seq, D], BF16).ap()
    s_lnat = nc.dram_tensor("s_lnat", [NOWN, D], BF16).ap()
    s_w1q = nc.dram_tensor("s_w1q", [HID // 256, P, DC, 256], BF16).ap()
    s_w2q = nc.dram_tensor("s_w2q", [HID // 256, P, 2, D], BF16).ap()
    s_xq = nc.dram_tensor("s_xq", [NOWN, D], BF16).ap()
    ap_out = nc.dram_tensor("out", [NOWN, D], F32, kind="ExternalOutput").ap()

    es = contextlib.ExitStack()
    pl = Plan(nc, es)

    def sb(st, name, shape, dtype=F32):
        return st.enter_context(nc.sbuf_tensor(name, shape, dtype))

    pb = [es.enter_context(nc.psum_tensor(f"pb{i}", [P, 512], F32))
          for i in range(8)]

    iotaD = sb(es, "iotaD", [P, 256], F32)
    iota64 = sb(es, "iota64", [P, NSLOT * 8], F32)
    thr = sb(es, "thr", [P, NSLOT, 8], F32)
    alpha = sb(es, "alpha", [P, 1], F32)
    betat = sb(es, "betat", [P, 1], F32)
    dsum = sb(es, "dsum", [P, 16], F32)
    epst = sb(es, "epst", [P, 1], F32)
    ln16t = sb(es, "ln16t", [P, 1], F32)
    onesc = sb(es, "onesc", [P, 1], BF16)
    qposb = sb(es, "qposb", [P, NSLOT], F32)
    b1q16 = sb(es, "b1q16", [P, HID // P], F32)
    b1raw = sb(es, "b1raw", [P, HID // P], F32)
    b2tile = sb(es, "b2tile", [P, D], F32)
    stat = sb(es, "stat", [P, 2, 32], F32)
    sq_scr = sb(es, "sq_scr", [P, 4, D], BF16)

    S = {n: pl.sem(n) for n in
         ["sP0", "sP0g", "sXa", "sXb", "sLN", "sLXa", "sLXb", "sTPa", "sTPb", "sKM", "sKE", "sKVw",
          "sKVLa", "sKVLb", "sKVF", "sS", "sE", "sMG", "sMSK", "sAV",
          "sAVE", "sOE", "sLAW", "sWL", "sWQw",
          "sX2", "sLG", "sX2c", "sXQW", "sXQT", "sWL2a", "sWL2b", "sWL2c",
          "sWL2d", "sWL2e", "sWL2f", "sHM",
          "sRL", "sRQ", "sY", "sYB", "sOUT", "sXE", "sLE"]}

    def ln_stats_group(xtiles, par, rs_bias, stt=None, sqt=None):
        nt = len(xtiles)
        st = (stt if stt is not None else stat)[:, par, :]
        sqb = sqt if sqt is not None else sq_scr
        red_mu = None
        for j, (xap, w) in enumerate(xtiles):
            red_mu = pl.op("vector", lambda e, xap=xap, j=j: e.reduce_sum(
                st[:, j:j + 1], xap, axis=AX.X), after=w)
        sq_mu = None
        for j, (xap, w) in enumerate(xtiles):
            sq_mu = pl.op("scalar", lambda e, xap=xap, j=j: e.activation(
                sqb[:, j, :], xap, AF.Square,
                accum_out=st[:, 4 + j:5 + j]), after=w)
        m1 = pl.op("vector", lambda e, nt=nt, st=st: e.tensor_scalar(
            st[:, 8:8 + nt], st[:, 0:nt], -1.0 / D, None, ALU.mult),
            after=[red_mu])
        m2 = pl.op("vector", lambda e, nt=nt, st=st: e.tensor_scalar(
            st[:, 12:12 + nt], st[:, 4:4 + nt], 1.0 / D, None, ALU.mult),
            after=[sq_mu, m1])
        m3 = pl.op("vector", lambda e, nt=nt, st=st: e.tensor_mul(
            st[:, 16:16 + nt], st[:, 8:8 + nt], st[:, 8:8 + nt]),
            after=[m2])
        m4 = pl.op("vector", lambda e, nt=nt, st=st: e.tensor_sub(
            st[:, 20:20 + nt], st[:, 12:12 + nt], st[:, 16:16 + nt]),
            after=[m3])
        s1 = pl.op("scalar", lambda e, nt=nt, st=st: e.activation(
            st[:, 24:24 + nt], st[:, 20:20 + nt], AF.Ln, bias=epst[:, 0:1]),
            after=[m4])
        rb = ln16t[:, 0:1] if rs_bias else 0.0
        s2 = pl.op("scalar", lambda e, nt=nt, st=st, rb=rb: e.activation(
            st[:, 28:28 + nt], st[:, 24:24 + nt], AF.Exp, bias=rb,
            scale=-0.5), after=[s1])
        return m1, s2

    def ln_norm(xap, par, j, out_ap, after, extra=(), stt=None):
        st = (stt if stt is not None else stat)[:, par, :]
        return pl.op("vector", lambda e, st=st, j=j: e.tensor_scalar(
            out_ap, xap, st[:, 8 + j:9 + j], st[:, 28 + j:29 + j],
            ALU.add, ALU.mult), after=after, extra=extra)

    def quant_chain(in_ap, out_ap, width, after, pre_scale=None,
                    post_scale=None, y1t=None, zt=None, extra=(),
                    eng="vector"):
        y1 = y1t[:, 0:width]
        z = zt[:, 0:width]
        if pre_scale is None:
            c = pl.op(eng, lambda e: e.tensor_scalar(
                y1, in_ap, C32, C32, ALU.add, ALU.subtract), after=after)
        else:
            c = pl.op(eng, lambda e: e.tensor_scalar(
                in_ap, in_ap, pre_scale, None, ALU.mult), after=after)
            c = pl.op(eng, lambda e: e.tensor_scalar(
                y1, in_ap, C32, C32, ALU.add, ALU.subtract), after=[c])
        c = pl.op(eng, lambda e: e.tensor_scalar(
            z, y1, C52, C52, ALU.add, ALU.subtract), after=[c])
        if eng == "gpsimd":
            c = pl.op(eng, lambda e: e.tensor_scalar(
                y1, y1, betat[:, 0:1], None, ALU.mult), after=[c])
            c = pl.op(eng, lambda e: e.tensor_add(z, z, y1), after=[c])
        else:
            c = pl.op(eng, lambda e: e.scalar_tensor_tensor(
                z, y1, betat[:, 0:1], z, ALU.mult, ALU.add), after=[c])
        if post_scale is None:
            c = pl.op(eng, lambda e: e.tensor_scalar(
                out_ap, z, -128.0, 127.0, ALU.max, ALU.min), after=[c],
                extra=extra)
        else:
            c = pl.op(eng, lambda e: e.tensor_scalar(
                z, z, -128.0, 127.0, ALU.max, ALU.min), after=[c])
            c = pl.op(eng, lambda e: e.tensor_scalar(
                out_ap, z, post_scale, None, ALU.mult), after=[c],
                extra=extra)
        return c

    pl.emit("sync", lambda e: e.dma_start(out=qposb[:], in_=ap_qpos[:, :]),
            incs=[(S["sP0"], 16)])
    pl.emit("sync", lambda e: e.dma_start(out=b1raw[:], in_=ap_b1t[:, :]),
            incs=[(S["sP0"], 16)])
    pl.emit("sync", lambda e: e.dma_start(out=b2tile[:], in_=ap_b2bc[:, :]),
            incs=[(S["sP0"], 16)])
    gi1 = pl.op("gpsimd", lambda e: e.iota(
        iotaD[:, :], [[1, 256]], channel_multiplier=-1,
        allow_small_or_imprecise_dtypes=True))
    gi2 = pl.op("gpsimd", lambda e: e.iota(
        iota64[:, :], [[128, NSLOT * 8]], channel_multiplier=0,
        allow_small_or_imprecise_dtypes=True), after=[gi1])
    a0 = pl.op("vector", lambda e: e.memset(alpha[:], 2.0 ** -30))
    a1 = pl.op("vector", lambda e: e.tensor_scalar(
        alpha[:], alpha[:], 1.0, 1.0, ALU.add, ALU.subtract), after=[a0])
    a2 = pl.op("vector", lambda e: e.tensor_scalar(
        alpha[:], alpha[:], 2.0 ** 30, None, ALU.mult), after=[a1])
    a3 = pl.op("vector", lambda e: e.tensor_scalar(
        betat[:], alpha[:], -1.0, 1.0, ALU.mult, ALU.add), after=[a2])
    a3 = pl.op("vector", lambda e: e.memset(dsum[:], 0.0), after=[a3])
    e0 = pl.op("vector", lambda e: e.memset(epst[:], EPS), after=[a3])
    e0 = pl.op("vector", lambda e: e.memset(ln16t[:], LN16), after=[e0])
    o0 = pl.op("vector", lambda e: e.memset(onesc[:], 1.0), after=[e0])
    tm = [o0, gi2, (S["sP0"], 48)]
    for s in range(NSLOT):
        tm = [pl.op("vector", lambda e, s=s: e.tensor_scalar(
            thr[:, s, :], iota64[:, 8 * s:8 * s + 8], qposb[:, s:s + 1],
            None, ALU.subtract), after=tm)]
    stAB = contextlib.ExitStack()
    qt = sb(stAB, "qt", [P, DC, NOWN], BF16)
    stA = contextlib.ExitStack()
    xbuf = sb(stA, "xbuf", [P, 8, D], F32)
    qcb = quant_chain(b1raw[:, :], b1q16[:, :], HID // P,
                      after=tm, pre_scale=64.0, post_scale=0.25,
                      y1t=xbuf[:, 0, :], zt=xbuf[:, 1, :])
    qcb = quant_chain(b2tile[:, :], b2tile[:, :], D,
                      after=[qcb], pre_scale=64.0, post_scale=1.0 / 64.0,
                      y1t=xbuf[:, 0, :], zt=xbuf[:, 1, :])
    lnstage = sb(stA, "lnstage", [P, 8, D], BF16)
    lnxT = sb(stA, "lnxT", [P, 2, DC, 1024], BF16)
    kstage = sb(stA, "kstage", [P, DC, 1024], BF16)
    vstage = sb(stA, "vstage", [P, 8, D], BF16)
    wT_sb = {}
    for nm, srcw in (("q", ap_wqT), ("k", ap_wkT), ("v", ap_wvT)):
        wT_sb[nm] = sb(stA, f"w{nm}T_sb", [P, DC, D], BF16)
        pl.emit("gpsimd", lambda e, dst=wT_sb[nm], srcw=srcw: e.dma_start(
            out=dst[:, :, :], in_=srcw.rearrange("(a p) o -> p a o", p=P)),
            incs=[(S["sP0g"], 16)])
    W0 = S["sP0"].v

    PA_RT = ([(r * 1024, 1024, False) for r in range(NRNG)]
             + [(r * ORNG, ORNG, True) for r in range(NORNG)])

    Tg = 0
    bank_gate = {0: 0, 1: 0, 2: 0, 3: 0}
    kv_store_done = 0
    km_before_prev = 0
    lx_after = {}
    for ri, (base, rtok, own) in enumerate(PA_RT):
        ntile = rtok // P
        srcx = ap_xo if own else ap_x
        km_before = S["sKM"].v
        for gi0 in range(0, ntile, 4):
            nt = min(4, ntile - gi0)
            G4 = Tg // 4
            sl = (G4 % 2) * 4
            w = [(S["sLN"], Tg - 4)] if Tg >= 8 else []
            if Tg == 0:
                w = [qcb]
            xsem = S["sXa"] if G4 % 2 == 0 else S["sXb"]
            pl.emit("sync", lambda e, gi0=gi0, nt=nt, base=base, srcx=srcx,
                    sl=sl: e.dma_start(
                        out=xbuf[:, sl:sl + nt, :],
                        in_=srcx.rearrange("(a p) d -> p a d", p=P)[
                            :, (base + gi0 * P) // P:
                            (base + gi0 * P) // P + nt, :]),
                    waits=w, incs=[(xsem, 16)])
            Tg += nt
        srcl = s_lnxo if own else s_lnx
        obase = base
        tp_after_half = {}
        for h in range(rtok // 512):
            hs = S["sLXa"] if h == 0 else S["sLXb"]
            ts_ = S["sTPa"] if h == 0 else S["sTPb"]
            for d in range(DC):
                w = [(hs, 16 * (ri + 1))]
                if d == 0:
                    w.append((ts_, ts_.v))
                if ri > 1 and h == 0 and d == 0:
                    w.append((S["sKM"], km_before_prev))
                pl.emit("sync", lambda e, d=d, h=h, obase=obase, srcl=srcl,
                        ri=ri: e.dma_start_transpose(
                            lnxT[:, ri % 2, d, 512 * h:512 * (h + 1)],
                            srcl[obase + 512 * h:obase + 512 * (h + 1),
                                 d * P:(d + 1) * P]),
                        waits=w, incs=[(ts_, 16)])
            tp_after_half[h] = (ts_, ts_.v)
        km_before_prev = km_before
        t0g = Tg - ntile
        for gi0 in range(0, ntile, 4):
            nt = min(4, ntile - gi0)
            G4 = (t0g + gi0) // 4
            par = G4 % 2
            sl = (G4 % 2) * 4
            xsem = S["sXa"] if G4 % 2 == 0 else S["sXb"]
            glw = (xsem, 16 * (G4 // 2 + 1))
            tiles = []
            for j in range(nt):
                T = t0g + gi0 + j
                w = [glw]
                if j == 0 and T >= 8:
                    w.append((S["sLN"], T - 4))
                tiles.append((xbuf[:, sl + j, :], w))
            mneg, srs = ln_stats_group(tiles, par, False)
            hh = gi0 // 4
            hs = S["sLXa"] if hh == 0 else S["sLXb"]
            lxw_gate = lx_after.get((ri - 1, hh), 0) if ri else 0
            for j in range(nt):
                ln_norm(xbuf[:, sl + j, :], par, j, lnstage[:, gi0 + j, :],
                        [srs, (hs, lxw_gate)],
                        extra=[(S["sLN"], 1)])
        dstl = s_lnxo if own else s_lnx
        for hh in range(0, ntile, 4):
            hn = min(4, ntile - hh)
            hs = S["sLXa"] if hh == 0 else S["sLXb"]
            pl.emit("gpsimd", lambda e, base=base, hh=hh, hn=hn, dstl=dstl:
                    e.dma_start(
                        out=dstl.rearrange("(a p) d -> p a d", p=P)[
                            :, base // P + hh:base // P + hh + hn, :],
                        in_=lnstage[:, hh:hh + hn, :]),
                    waits=[(S["sLN"], Tg - ntile + hh + hn)],
                    incs=[(hs, 16)])
            lx_after[(ri, hh // 4)] = hs.v
        if own:
            groups = [("q", oc, tsub) for tsub in range(rtok // 512)
                      for oc in range(DC)]
        else:
            groups = []
            for h in range(rtok // 512):
                kg = [("k", oc, h) for oc in range(DC)]
                vg = [("v", t8, oh) for oh in range(2)
                      for t8 in range(4 * h, 4 * h + 4)]
                groups += [g for pair in zip(kg, vg) for g in pair]
        for gidx, (kind, ga, gb) in enumerate(groups):
            ghalf = gb if kind != "v" else ga // 4
            bank = (0 if kind != "v" else 2) + gidx % 2
            wts = [tp_after_half[ghalf], (S["sKE"], bank_gate[bank])]
            if ri == 0 and gidx == 0:
                wts.append((S["sP0"], W0))
                wts.append((S["sP0g"], 48))
            for dc in range(DC):
                if kind == "v":
                    bf = (lambda e, dc=dc, t8=ga, oh=gb, bank=bank, ri=ri:
                          e.matmul(pb[bank][:, 0:512],
                                   lnxT[:, ri % 2, dc, t8 * P:(t8 + 1) * P],
                                   wT_sb["v"][:, dc, oh * 512:(oh + 1) * 512],
                                   start=(dc == 0), stop=(dc == DC - 1)))
                else:
                    wsb = wT_sb["k" if kind == "k" else "q"]
                    bf = (lambda e, dc=dc, oc=ga, tsub=gb, bank=bank, wsb=wsb,
                          ri=ri:
                          e.matmul(pb[bank][:, 0:512],
                                   wsb[:, dc, oc * P:(oc + 1) * P],
                                   lnxT[:, ri % 2, dc,
                                        tsub * 512:(tsub + 1) * 512],
                                   start=(dc == 0), stop=(dc == DC - 1)))
                pl.emit("tensor", bf, waits=wts if dc == 0 else (),
                        incs=[(S["sKM"], 1)] if dc == DC - 1 else ())
            kmv = S["sKM"].v
            if kind == "k":
                ev = (lambda e, oc=ga, tsub=gb, bank=bank: e.activation(
                    kstage[:, oc, tsub * 512:(tsub + 1) * 512],
                    pb[bank][:, 0:512], AF.Copy))
            elif kind == "v":
                ev = (lambda e, t8=ga, oh=gb, bank=bank: e.activation(
                    vstage[:, t8, oh * 512:(oh + 1) * 512],
                    pb[bank][:, 0:512], AF.Copy))
            else:
                ev = (lambda e, oc=ga, tsub=gb, bank=bank, base=base:
                      e.activation(
                          qt[:, oc, base + tsub * 512:base + (tsub + 1) * 512],
                          pb[bank][:, 0:512], AF.Copy))
            evw = [(S["sKM"], kmv)]
            if not own:
                evw.append((S["sKVw"], kv_store_done))
            pl.op("scalar", ev, after=evw, extra=[(S["sKE"], 1)])
            bank_gate[bank] = S["sKE"].v
        if not own:
            kev = S["sKE"].v
            pl.emit("gpsimd", lambda e, base=base, rtok=rtok: e.dma_start(
                out=s_kT.rearrange("(a p) t -> p a t", p=P)[
                    :, :, base:base + rtok],
                in_=kstage[:, :, 0:rtok]),
                waits=[(S["sKE"], kev)], incs=[(S["sKVw"], 16)])
            pl.emit("gpsimd", lambda e, base=base, ntile=ntile: e.dma_start(
                out=s_v.rearrange("(a p) d -> p a d", p=P)[
                    :, base // P:base // P + ntile, :],
                in_=vstage[:, 0:ntile, :]),
                waits=[(S["sKE"], kev)], incs=[(S["sKVw"], 16)])
            kv_store_done = S["sKVw"].v

    pl.barrier()
    stA.close()

    stB = contextlib.ExitStack()
    xbuf2 = sb(stB, "xbuf2", [P, 4, D], F32)
    latbuf = sb(stB, "latbuf", [P, 4, D], BF16)
    t16b = sb(stB, "t16b", [P, 2, D], F32)
    xqstage = sb(stB, "xqstage", [P, 4, D], BF16)
    stat2 = sb(stB, "stat2", [P, 2, 32], F32)
    sq2 = sb(stB, "sq2", [P, 4, D], BF16)
    qc1 = sb(stB, "qc1", [P, D], F32)
    qc2 = sb(stB, "qc2", [P, D], F32)
    kgrp = sb(stB, "kgrp", [P, 2, DC, 512], BF16)
    vgrp = sb(stB, "vgrp", [P, 2, 4, D], BF16)
    ptb = sb(stB, "ptb", [P, 8, 256], BF16)
    maskb = sb(stB, "maskb", [P, 8, 256], BF16)
    rcp = sb(stB, "rcp", [P, 16], F32)
    oacc = sb(stB, "oacc", [P, 4, 2, D], F32)
    lnatst = sb(stB, "lnatst", [P, 2, D], BF16)
    wstage = sb(stB, "wstage", [P, 2, HID // 2], F32)
    wqout = sb(stB, "wqout", [P, HID // 2], BF16)
    wy1 = sb(stB, "wy1", [P, HID // 2], BF16)
    wz = sb(stB, "wz", [P, HID // 2], BF16)

    chain_mu = {}

    def emit_part1_loads(gg):
        gi0 = gg * 4
        law_need = 16 * min(2 * gg + 2, NSLOT)
        w = [(S["sLAW"], law_need)]
        if gi0 >= 4:
            w.append((S["sX2c"], gi0))
        pl.emit("gpsimd", lambda e, gi0=gi0: e.dma_start(
            out=xbuf2[:, 0:4, :],
            in_=ap_xo.rearrange("(a p) d -> p a d", p=P)[:, gi0:gi0 + 4, :]),
            waits=w, incs=[(S["sX2"], 16)])
        pl.emit("gpsimd", lambda e, gi0=gi0: e.dma_start(
            out=latbuf[:, 0:4, :],
            in_=s_lnat.rearrange("(a p) d -> p a d", p=P)[:, gi0:gi0 + 4, :]),
            waits=w, incs=[(S["sLG"], 16)])

    def emit_part1_group(gg):
        gi0 = gg * 4
        tiles = []
        for j in range(4):
            ww = [(S["sX2"], 16 * (gg + 1)), (S["sLG"], 16 * (gg + 1))]
            xmu = pl.op("vector", lambda e, j=j: e.tensor_add(
                xbuf2[:, j, :], xbuf2[:, j, :], latbuf[:, j, :]), after=ww)
            tiles.append((xbuf2[:, j, :], [xmu]))
        par = gg % 2
        mneg, srs = ln_stats_group(tiles, par, True, stt=stat2, sqt=sq2)
        for j in range(4):
            i = gi0 + j
            after = [srs]
            if i >= 2 and (i - 2) in chain_mu:
                after.append(chain_mu[i - 2])
            nrm = ln_norm(xbuf2[:, j, :], par, j, t16b[:, i % 2, :],
                          after, extra=[(S["sX2c"], 1)], stt=stat2)
            cw = [nrm]
            if i >= 4:
                cw.append((S["sXQW"], 16 * (i // 4)))
            chain_mu[i] = quant_chain(
                t16b[:, i % 2, :], xqstage[:, i % 4, :], D, cw,
                y1t=qc1, zt=qc2)
        pl.emit("gpsimd", lambda e, gi0=gi0: e.dma_start(
            out=s_xq.rearrange("(a p) d -> p a d", p=P)[
                :, gi0:gi0 + 4, :],
            in_=xqstage[:, 0:4, :]),
            waits=[chain_mu[gi0 + 3]], incs=[(S["sXQW"], 16)])

    wq_units = []
    for wi in range(W1TILES):
        for h in range(2):
            src = ap_w1T[wi * P:(wi + 1) * P,
                         h * (HID // 2):(h + 1) * (HID // 2)]
            wout = s_w1q.rearrange("a p c m -> p a c m")[
                :, 8 * h:8 * (h + 1), wi, :]
            wq_units.append((src, HID // 2, wout, 256))
    for w2i in range(W2TILES):
        src = ap_w2T[w2i * P:(w2i + 1) * P, :]
        wout = s_w2q.rearrange("a p c o -> p a c o")[:, w2i // 2, w2i % 2, :]
        wq_units.append((src, D, wout, None))
    wq_mu = {}
    wqst = {"load": 0, "chain": 0}

    def emit_wq_unit():
        while wqst["load"] < min(len(wq_units), wqst["chain"] + 2):
            u = wqst["load"]
            src, width, _, _ = wq_units[u]
            lw = [wq_mu[u - 2]] if u >= 2 else []
            pl.emit("gpsimd", lambda e, src=src, width=width, u=u:
                    e.dma_start(out=wstage[:, u % 2, 0:width], in_=src),
                    waits=lw, incs=[(S["sWL"], 16)])
            wqst["load"] += 1
        u = wqst["chain"]
        if u >= len(wq_units):
            return
        src, width, wout, m = wq_units[u]
        ch = quant_chain(wstage[:, u % 2, 0:width], wqout[:, 0:width], width,
                         after=[(S["sWL"], 16 * (u + 1)),
                                (S["sWQw"], 16 * u)],
                         pre_scale=64.0, y1t=wy1, zt=wz, eng="gpsimd")
        wq_mu[u] = ch
        wsrc = wqout[:, 0:width]
        if m is not None:
            wsrc = wsrc.rearrange("p (a m) -> p a m", m=m)
        pl.emit("gpsimd", lambda e, wout=wout, wsrc=wsrc: e.dma_start(
            out=wout, in_=wsrc), waits=[ch], incs=[(S["sWQw"], 16)])
        wqst["chain"] += 1

    pstate = {"par": 0}
    lnat_nrm = {}
    lane_free = {}

    def emit_slot_epilogue(s, lane, last_ev):
        sv = pl.vec_pending
        pl.vec_pending = None
        rcp_mu = pl.op("vector", lambda e, s=s: e.reciprocal(
            rcp[:, 2 * s:2 * s + 2], dsum[:, 2 * s:2 * s + 2]),
            after=[last_ev])
        ev_mus = []
        for qs2 in range(2):
            for oh in range(2):
                mu = pl.op("scalar", lambda e, qs2=qs2, oh=oh, lane=lane,
                           s=s: e.activation(
                               oacc[:, lane, qs2, oh * 512:(oh + 1) * 512],
                               oacc[:, lane, qs2, oh * 512:(oh + 1) * 512],
                               AF.Copy, scale=rcp[:, 2 * s + qs2:
                                                  2 * s + qs2 + 1]),
                           after=[rcp_mu, last_ev])
                ev_mus.append(mu)
        par = pstate["par"]
        tiles = [(oacc[:, lane, 0, :], [ev_mus[0], ev_mus[1]]),
                 (oacc[:, lane, 1, :], [ev_mus[2], ev_mus[3]])]
        if s >= 2:
            tiles[0][1].append(lnat_nrm[s - 2])
        mneg, srs = ln_stats_group(tiles, par, False)
        nrm_mu = None
        for j in range(2):
            after = [srs, (S["sLAW"], 16 * s)] if nrm_mu is None \
                else [srs, nrm_mu]
            nrm_mu = ln_norm(oacc[:, lane, j, :], par, j,
                             lnatst[:, j, :], after)
        lnat_nrm[s] = nrm_mu
        lane_free[lane] = nrm_mu
        pstate["par"] ^= 1
        pl.emit("gpsimd", lambda e, s=s: e.dma_start(
            out=s_lnat.rearrange("(a p) d -> p a d", p=P)[
                :, 2 * s:2 * s + 2, :],
            in_=lnatst[:, 0:2, :]),
            waits=[nrm_mu], incs=[(S["sLAW"], 16)])
        pl.vec_pending = sv

    I = 0
    B = 0
    G = 0
    msk = 0
    av_pend = []
    p1_after = {1: [0], 3: [1], 5: [2], 7: [3]}
    pl.vec_pending = []
    for q2 in range(2):
        GQ = 8 if q2 == 0 else 16
        slots4 = list(range(4 * q2, 4 * q2 + 4))
        for g in range(GQ):
            gw = [(S["sKVF"], G - 1)] if G >= 2 else []
            k0 = 512 * g
            kvsem = S["sKVLa"] if G % 2 == 0 else S["sKVLb"]
            gpar = G % 2
            pl.emit("sync", lambda e, k0=k0, gpar=gpar: e.dma_start(
                out=kgrp[:, gpar, :, :],
                in_=s_kT.rearrange("(a p) t -> p a t", p=P)[
                    :, :, k0:k0 + 512]),
                waits=gw, incs=[(kvsem, 16)])
            pl.emit("sync", lambda e, k0=k0, gpar=gpar: e.dma_start(
                out=vgrp[:, gpar, :, :],
                in_=s_v.rearrange("(a p) d -> p a d", p=P)[
                    :, k0 // P:k0 // P + 4, :]),
                incs=[(kvsem, 16)])
            G += 1
            kvl_wait = (kvsem, kvsem.v)
            act = [s for s in slots4 if g < 2 * s + 2]
            for si, s in enumerate(act):
                lane = s % 4
                last_of_group = (si == len(act) - 1)
                slot_last = (g == 2 * s + 1)
                first_burst = (g == 0)
                if B % 3 == 0:
                    emit_wq_unit()
                    emit_wq_unit()
                if g == 2 * s:
                    sv = pl.vec_pending
                    pl.vec_pending = None
                    for m in range(8):
                        w = [(S["sMSK"], 8 * s)]
                        if s == 0 and m == 0:
                            w.append(tm[0])
                        pl.op("vector", lambda e, s=s, m=m: e.tensor_scalar(
                            maskb[:, m, :], iotaD[:, :], thr[:, s, m:m + 1],
                            None, ALU.is_ge), after=w,
                            extra=[(S["sMG"], 1)])
                    pl.vec_pending = sv
                iter_info = []
                for kk in range(4):
                    kb = 4 * g + kk
                    sbank = 4 + I % 2
                    for dc in range(DC):
                        w = []
                        if dc == 0:
                            w = [kvl_wait]
                            if I >= 2:
                                w.append((S["sE"], I - 1))
                        pl.emit("tensor", lambda e, dc=dc, kk=kk, s=s,
                                gpar=gpar, sbank=sbank: e.matmul(
                                    pb[sbank][:, 0:256],
                                    kgrp[:, gpar, dc, kk * P:(kk + 1) * P],
                                    qt[:, dc, s * 256:(s + 1) * 256],
                                    start=(dc == 0), stop=(dc == DC - 1)),
                                waits=w if dc == 0 else (),
                                incs=[(S["sS"], 1)] if dc == DC - 1 else ())
                    w = [(S["sS"], I + 1)]
                    if I >= 8:
                        w.append((S["sAV"], I - 7))
                    pl.op("scalar", lambda e, I=I, sbank=sbank: e.activation(
                        ptb[:, I % 8, :], pb[sbank][:, 0:256], AF.Exp,
                        scale=ISQ), after=w, extra=[(S["sE"], 1)])
                    masked = kb >= 8 * s
                    if masked:
                        m = kb - 8 * s
                        sv = pl.vec_pending
                        pl.vec_pending = None
                        pl.op("vector", lambda e, I=I, m=m: e.tensor_mul(
                            ptb[:, I % 8, :], ptb[:, I % 8, :],
                            maskb[:, m, :]),
                            after=[(S["sE"], I + 1),
                                   (S["sMG"], 8 * s + m + 1)],
                            extra=[(S["sMSK"], 1)])
                        pl.vec_pending = sv
                        msk += 1
                    first_w = [(S["sMSK"], msk)] if masked \
                        else [(S["sE"], I + 1)]
                    iter_info.append((I, kb, tuple(first_w)))
                    pl.pump(3)
                    I += 1

                def make_av_block(s=s, lane=lane, B=B,
                                  iter_info=tuple(iter_info), gpar=gpar,
                                  first_burst=first_burst,
                                  last_of_group=last_of_group):
                    def emit_block():
                        for j, (I0, kb, first_w) in enumerate(iter_info):
                            kk = kb % 4
                            fw = list(first_w)
                            if j == 0 and B >= 1:
                                fw.append((S["sAVE"], 5 * B))
                            nmm = 0
                            for qs2 in range(2):
                                for oh in range(2):
                                    avincs = []
                                    if nmm == 3 and j == 3 and last_of_group:
                                        avincs.append((S["sKVF"], 1))
                                    pl.emit("tensor", lambda e, I0=I0,
                                            qs2=qs2, oh=oh, gpar=gpar, kk=kk:
                                            e.matmul(
                                                pb[2 * qs2 + oh][:, 0:512],
                                                ptb[:, I0 % 8,
                                                    qs2 * P:(qs2 + 1) * P],
                                                vgrp[:, gpar, kk,
                                                     oh * 512:(oh + 1) * 512],
                                                start=(kk == 0),
                                                stop=(kk == 3)),
                                            waits=fw if nmm == 0 else (),
                                            incs=avincs)
                                    nmm += 1
                                pl.emit("tensor", lambda e, I0=I0, qs2=qs2,
                                        j=j: e.matmul(
                                            pb[7][:, qs2:qs2 + 1],
                                            ptb[:, I0 % 8,
                                                qs2 * P:(qs2 + 1) * P],
                                            onesc[:, 0:1],
                                            start=(j == 0 and qs2 == 0),
                                            stop=(j == 3 and qs2 == 1)),
                                        incs=[(S["sAV"], 1)]
                                        if qs2 == 1 else ())
                        sAV_t = iter_info[-1][0] + 1
                        sv = pl.vec_pending
                        pl.vec_pending = None
                        ew = [(S["sAV"], sAV_t)]
                        if first_burst and lane in lane_free:
                            ew.append(lane_free[lane])
                        last = pl.op("vector", lambda e, s=s: e.tensor_add(
                            dsum[:, 2 * s:2 * s + 2],
                            dsum[:, 2 * s:2 * s + 2], pb[7][:, 0:2]),
                            after=ew, extra=[(S["sAVE"], 1)])
                        for qs2 in range(2):
                            for oh in range(2):
                                if first_burst:
                                    last = pl.op(
                                        "vector",
                                        lambda e, qs2=qs2, oh=oh, lane=lane:
                                        e.tensor_copy(
                                            oacc[:, lane, qs2,
                                                 oh * 512:(oh + 1) * 512],
                                            pb[2 * qs2 + oh][:, 0:512]),
                                        extra=[(S["sAVE"], 1)])
                                else:
                                    last = pl.op(
                                        "vector",
                                        lambda e, qs2=qs2, oh=oh, lane=lane:
                                        e.tensor_add(
                                            oacc[:, lane, qs2,
                                                 oh * 512:(oh + 1) * 512],
                                            oacc[:, lane, qs2,
                                                 oh * 512:(oh + 1) * 512],
                                            pb[2 * qs2 + oh][:, 0:512]),
                                        extra=[(S["sAVE"], 1)])
                        pl.vec_pending = sv
                        return last
                    return emit_block
                if av_pend:
                    av_pend.pop(0)()
                av_pend.append(make_av_block())
                B += 1
                if slot_last:
                    last_ev = None
                    while av_pend:
                        last_ev = av_pend.pop(0)()
                    emit_slot_epilogue(s, lane, last_ev)
                    for gg in p1_after.get(s, []):
                        emit_part1_loads(gg)
                        emit_part1_group(gg)
    while wqst["chain"] < len(wq_units):
        emit_wq_unit()
    pl.pump_all()
    pl.vec_pending = None
    pl.barrier()
    stB.close()
    stAB.close()

    stC = contextlib.ExitStack()
    xqT = sb(stC, "xqT", [P, DC, NOWN], BF16)
    for d in range(DC):
        pl.emit("sync", lambda e, d=d: e.dma_start_transpose(
            xqT[:, d, :], s_xq[:, d * P:(d + 1) * P]),
            waits=[(S["sXQW"], 16 * (NHT // 4))],
            incs=[(S["sXQT"], 16)])
    WRING = 6
    w1ch = sb(stC, "w1ch", [P, WRING, DC, 256], BF16)
    w2ch = sb(stC, "w2ch", [P, WRING, 2, D], BF16)
    yacc = sb(stC, "yacc", [P, NT8, 2, D], F32)
    usb = sb(stC, "usb", [P, 6, 256], F32)
    rbuf = sb(stC, "rbuf", [P, 6, 256], BF16)
    ysb = sb(stC, "ysb", [P, 2, D], F32)
    outst = sb(stC, "outst", [P, 2, D], F32)
    xet = sb(stC, "xet", [P, 2, D], F32)
    late = sb(stC, "late", [P, 2, D], F32)
    qc1c = sb(stC, "qc1c", [P, 256], F32)
    qc2c = sb(stC, "qc2c", [P, 256], F32)

    wlsems = [S["sWL2a"], S["sWL2b"], S["sWL2c"], S["sWL2d"],
              S["sWL2e"], S["sWL2f"]]
    for hc in range(NHC):
        lw = []
        if hc >= WRING:
            lw = [(S["sY"], 64 * ((hc - WRING) // 4 + 1))]
        wsem = wlsems[hc % WRING]
        pl.emit("sync", lambda e, hc=hc: e.dma_start(
            out=w1ch[:, hc % WRING, :, :],
            in_=s_w1q.rearrange("a p c m -> p a c m")[:, hc, :, :]),
            waits=lw, incs=[(wsem, 16)])
        pl.emit("sync", lambda e, hc=hc: e.dma_start(
            out=w2ch[:, hc % WRING, :, :],
            in_=s_w2q.rearrange("a p c o -> p a c o")[:, hc, :, :]),
            incs=[(wsem, 16)])

    idx = 0
    MM2_SKEW = 4
    mm2_pend = []

    def emit_mm2(idx, hc, h2, step, t8q):
        mw = [(S["sRQ"], idx + 1)]
        if step == 0:
            mw.append((S["sYB"], 4 * t8q))
        nmm = 0
        for tsub in range(2):
            for oh in range(2):
                pl.emit("tensor", lambda e, idx=idx, tsub=tsub,
                        oh=oh, hc=hc, h2=h2, step=step: e.matmul(
                            pb[2 * tsub + oh][:, 0:512],
                            rbuf[:, idx % 6, tsub * P:(tsub + 1) * P],
                            w2ch[:, hc % WRING, h2,
                                 oh * 512:(oh + 1) * 512],
                            start=(step == 0), stop=(step == 7)),
                        waits=mw if nmm == 0 else (),
                        incs=[(S["sY"], 1)] if nmm == 3 else ())
                nmm += 1

    ep_state = {"lny_mu_prev": None, "out_mu_prev": None}
    lny_nrm = {}
    out_add_mu = {}

    def emit_epilogue(t8):
        ew = []
        if t8 >= 1:
            ew = [out_add_mu[t8 - 1]]
        pl.emit("sync", lambda e, t8=t8: e.dma_start(
            out=xet[:, 0:2, :],
            in_=ap_xo.rearrange("(a p) d -> p a d", p=P)[
                :, 2 * t8:2 * t8 + 2, :]),
            waits=ew, incs=[(S["sXE"], 16)])
        pl.emit("gpsimd", lambda e, t8=t8: e.dma_start(
            out=late[:, 0:2, :],
            in_=s_lnat.rearrange("(a p) d -> p a d", p=P)[
                :, 2 * t8:2 * t8 + 2, :]),
            waits=ew, incs=[(S["sLE"], 16)])
        yb_mu = None
        for j in range(2):
            w = [(S["sYB"], 4 * (24 + t8 + 1))]
            if ep_state["lny_mu_prev"] is not None and yb_mu is None:
                w.append(ep_state["lny_mu_prev"])
            yb_mu = pl.op("vector", lambda e, j=j, t8=t8:
                          e.scalar_tensor_tensor(
                              ysb[:, j, :], yacc[:, t8, j, :], 1.0 / 1024.0,
                              b2tile[:, :], ALU.mult, ALU.add),
                          after=w)
        tiles = [(ysb[:, j, :], [yb_mu]) for j in range(2)]
        if t8 >= 2:
            tiles[0][1].append(lny_nrm[t8 - 2])
        par = t8 % 2
        mneg, srs = ln_stats_group(tiles, par, False)
        x2a_mu = None
        for j in range(2):
            aw = [(S["sXE"], 16 * (t8 + 1)), (S["sLE"], 16 * (t8 + 1))]
            if x2a_mu is not None:
                aw = [x2a_mu]
            x2a_mu = pl.op("vector", lambda e, j=j: e.tensor_add(
                xet[:, j, :], xet[:, j, :], late[:, j, :]), after=aw)
        out_mu = None
        lny_mu_prev = None
        for j in range(2):
            after = [srs] if lny_mu_prev is None else [srs, lny_mu_prev]
            if j == 0 and ep_state["out_mu_prev"] is not None:
                after.append(ep_state["out_mu_prev"])
            nrm = ln_norm(ysb[:, j, :], par, j, ysb[:, j, :], after)
            lny_mu_prev = nrm
            ow = [nrm, (S["sOUT"], 16 * t8)]
            if j == 0:
                ow.append(x2a_mu)
            out_mu = pl.op("vector", lambda e, t8=t8, j=j: e.tensor_add(
                outst[:, j, :], xet[:, j, :], ysb[:, j, :]),
                after=ow)
        lny_nrm[t8] = lny_mu_prev
        ep_state["lny_mu_prev"] = lny_mu_prev
        ep_state["out_mu_prev"] = out_mu
        out_add_mu[t8] = out_mu
        pl.emit("gpsimd", lambda e, t8=t8: e.dma_start(
            out=ap_out.rearrange("(a p) d -> p a d", p=P)[
                :, 2 * t8:2 * t8 + 2, :],
            in_=outst[:, 0:2, :]),
            waits=[out_mu], incs=[(S["sOUT"], 16)])

    for q in range(4):
        for t8 in range(NT8):
            for step in range(8):
                hc = 4 * q + step // 2
                h2 = step % 2
                hbank = 4 + idx % 4
                w = [(wlsems[hc % WRING], 32 * (hc // WRING + 1)),
                     (S["sXQT"], 16 * DC)]
                if idx >= 4:
                    w.append((S["sRL"], idx - 3))
                for dc in range(DC):
                    pl.emit("tensor", lambda e, dc=dc, h2=h2, t8=t8, hc=hc,
                            hbank=hbank: e.matmul(
                                pb[hbank][:, 0:256],
                                w1ch[:, hc % WRING, dc, h2 * P:(h2 + 1) * P],
                                xqT[:, dc, t8 * 256:(t8 + 1) * 256],
                                start=(dc == 0), stop=(dc == DC - 1)),
                            waits=w if dc == 0 else (),
                            incs=[(S["sHM"], 1)] if dc == DC - 1 else ())
                rw = [(S["sHM"], idx + 1)]
                if idx >= 6:
                    rw.append((S["sRQ"], idx - 5))
                hcol = 2 * hc + h2
                pl.op("scalar", lambda e, idx=idx, hcol=hcol, hbank=hbank:
                      e.activation(usb[:, idx % 6, :], pb[hbank][:, 0:256],
                                   AF.Relu, bias=b1q16[:, hcol:hcol + 1],
                                   scale=1.0 / 64.0),
                      after=rw, extra=[(S["sRL"], 1)])
                cw = [(S["sRL"], idx + 1)]
                if idx >= 6:
                    cw.append((S["sY"], idx - 5))
                quant_chain(usb[:, idx % 6, :], rbuf[:, idx % 6, :], 256,
                            cw, extra=[(S["sRQ"], 1)], y1t=qc1c, zt=qc2c)
                mm2_pend.append((idx, hc, h2, step, q * 8 + t8))
                if len(mm2_pend) > MM2_SKEW:
                    emit_mm2(*mm2_pend.pop(0))
                idx += 1
            for tsub in range(2):
                for oh in range(2):
                    w = [(S["sY"], 64 * q + 8 * (t8 + 1))]
                    if q >= 1:
                        w.append((S["sYB"], 4 * ((q - 1) * 8 + t8 + 1)))
                    if q == 0:
                        pl.op("scalar", lambda e, tsub=tsub, oh=oh, t8=t8:
                              e.activation(
                                  yacc[:, t8, tsub, oh * 512:(oh + 1) * 512],
                                  pb[2 * tsub + oh][:, 0:512], AF.Copy),
                              after=w, extra=[(S["sYB"], 1)])
                    else:
                        pl.op("vector", lambda e, tsub=tsub, oh=oh, t8=t8:
                              e.tensor_add(
                                  yacc[:, t8, tsub, oh * 512:(oh + 1) * 512],
                                  yacc[:, t8, tsub, oh * 512:(oh + 1) * 512],
                                  pb[2 * tsub + oh][:, 0:512]),
                              after=w, extra=[(S["sYB"], 1)])
            if q == 3:
                emit_epilogue(t8)
    while mm2_pend:
        emit_mm2(*mm2_pend.pop(0))

    pl.waits_only("gpsimd", [(S["sOUT"], 16 * NT8)])
    stC.close()

    with es, nc.Block() as block:
        for ename in ENGN:
            dec = getattr(block, ename)

            def body(eng, fns=pl.fns[ename]):
                for fn in fns:
                    fn(eng)
            dec(body)
    return nc


def own_tiles(c, seq):
    nt = seq // 256
    return [j for j in range(nt) if j % 8 in (c, 7 - c)]


def make_inmaps(seq, x_full, wq, wk, wv, w1, b1, w2, b2):
    f32 = np.float32
    shared = dict(
        wqT=np.ascontiguousarray(wq.T, dtype=f32),
        wkT=np.ascontiguousarray(wk.T, dtype=f32),
        wvT=np.ascontiguousarray(wv.T, dtype=f32),
        w1T=np.ascontiguousarray(w1.T, dtype=f32),
        w2T=np.ascontiguousarray(w2.T, dtype=f32),
        b1t=np.ascontiguousarray(b1.reshape(HID // P, P).T, dtype=f32),
        b2bc=np.ascontiguousarray(np.broadcast_to(b2[None, :], (P, D)),
                                  dtype=f32),
    )
    in_maps = []
    tiles_by_core = []
    for core in range(8):
        b, c = divmod(core, 4)
        tiles = own_tiles(c, seq)
        xb = np.ascontiguousarray(x_full[b], dtype=f32)
        x_own = np.ascontiguousarray(
            np.concatenate([xb[j * 256:(j + 1) * 256] for j in tiles], 0))
        qpos = np.broadcast_to(
            np.array([j * 256 for j in tiles], dtype=f32)[None, :],
            (P, len(tiles)))
        in_maps.append(dict(x=xb, x_own=x_own,
                            qpos=np.ascontiguousarray(qpos), **shared))
        tiles_by_core.append(tiles)
    return in_maps, tiles_by_core


def assemble(seq, results, tiles_by_core):
    B = 2
    out = np.zeros((B, seq, D), dtype=np.float32)
    for core in range(8):
        b = core // 4
        r = np.asarray(results[core]["out"])
        for s, j in enumerate(tiles_by_core[core]):
            out[b, j * 256:(j + 1) * 256] = r[s * 256:(s + 1) * 256]
    return out

from concourse.bass_utils import run_bass_kernel_spmd


def kernel(L0_attn_inputs, L0_attn_Wq, L0_attn_Wk, L0_attn_Wv,
           L0_ffn_w1, L0_ffn_b1, L0_ffn_w2, L0_ffn_b2):
    seq = 8192
    x = np.asarray(L0_attn_inputs, dtype=np.float32)
    in_maps, tiles = make_inmaps(seq, x,
                                 np.asarray(L0_attn_Wq, dtype=np.float32),
                                 np.asarray(L0_attn_Wk, dtype=np.float32),
                                 np.asarray(L0_attn_Wv, dtype=np.float32),
                                 np.asarray(L0_ffn_w1, dtype=np.float32),
                                 np.asarray(L0_ffn_b1, dtype=np.float32),
                                 np.asarray(L0_ffn_w2, dtype=np.float32),
                                 np.asarray(L0_ffn_b2, dtype=np.float32))
    nc = build(seq)
    res = run_bass_kernel_spmd(nc, in_maps, core_ids=list(range(8)))
    return assemble(seq, res.results, tiles)


# version 16
# speedup vs baseline: 2.4770x; 2.4770x over previous
import contextlib
import numpy as np

import concourse.bass as bass
import concourse.mybir as mybir

F32 = mybir.dt.float32
BF16 = mybir.dt.bfloat16
AF = mybir.ActivationFunctionType
ALU = mybir.AluOpType
AX = mybir.AxisListType

P = 128
D = 1024
HID = 4096
DC = D // 128
C32 = 1.5 * 2.0 ** 23
C52 = 1.5 * 2.0 ** 52
LN16 = float(np.log(16.0))
EPS = 1e-5
ISQ = 1.0 / 32.0
ENGN = ("sync", "gpsimd", "vector", "scalar", "tensor")


class _S:
    __slots__ = ("h", "v")

    def __init__(self, h):
        self.h = h
        self.v = 0


class Plan:
    def __init__(self, nc, es):
        self.nc = nc
        self.es = es
        self.fns = {e: [] for e in ENGN}
        self.sems = {}
        self.mu = {}
        for e in ENGN:
            self.mu[e] = self.sem("mu_" + e)

    def sem(self, name):
        s = _S(self.es.enter_context(self.nc.semaphore(name)))
        self.sems[name] = s
        return s

    vec_pending = None

    def emit(self, engine, build, waits=(), incs=()):
        for s, n in incs:
            s.v += n
        wl = [(s.h, int(v)) for s, v in waits if v > 0]
        il = [(s.h, n) for s, n in incs]

        def fn(eng, build=build, wl=wl, il=il):
            for h, v in wl:
                eng.wait_ge(h, v)
            instr = build(eng)
            for h, n in il:
                instr.then_inc(h, n)
        if engine == "vector" and self.vec_pending is not None:
            self.vec_pending.append(fn)
        else:
            self.fns[engine].append(fn)

    def pump(self, n):
        while n > 0 and self.vec_pending:
            self.fns["vector"].append(self.vec_pending.pop(0))
            n -= 1

    def pump_all(self):
        if self.vec_pending:
            self.fns["vector"] += self.vec_pending
            self.vec_pending = []

    def op(self, engine, build, after=(), extra=()):
        extra = list(extra)
        if extra:
            assert len(extra) == 1
            self.emit(engine, build, waits=after, incs=extra)
            return (extra[0][0], extra[0][0].v)
        m = self.mu[engine]
        self.emit(engine, build, waits=after, incs=[(m, 1)])
        return (m, m.v)

    def waits_only(self, engine, waits):
        wl = [(s.h, int(v)) for s, v in waits if v > 0]

        def fn(eng, wl=wl):
            for h, v in wl:
                eng.wait_ge(h, v)
        self.fns[engine].append(fn)

    def snapshot(self):
        return {k: s.v for k, s in self.sems.items()}

    def barrier(self):
        cur = self.snapshot()
        waits = [(self.sems[k], v) for k, v in cur.items() if v > 0]
        for e in ENGN:
            self.waits_only(e, waits)


def build(seq=8192):
    assert seq % 2048 == 0
    NSLOT = seq // 1024
    NOWN = NSLOT * 256
    NRNG = seq // 1024
    NORNG = max(1, NOWN // 1024)
    ORNG = NOWN // NORNG
    NHT = NOWN // P
    NT8 = NOWN // 256
    L = [8 * s + 8 for s in range(NSLOT)]
    W1TILES = DC
    W2TILES = HID // P
    NWT = W1TILES + W2TILES
    NHC = HID // 256

    nc = bass.Bass("TRN2", target_bir_lowering=False)

    ap_x = nc.dram_tensor("x", [seq, D], F32, kind="ExternalInput").ap()
    ap_xo = nc.dram_tensor("x_own", [NOWN, D], F32, kind="ExternalInput").ap()
    ap_qpos = nc.dram_tensor("qpos", [P, NSLOT], F32, kind="ExternalInput").ap()
    ap_wqT = nc.dram_tensor("wqT", [D, D], F32, kind="ExternalInput").ap()
    ap_wkT = nc.dram_tensor("wkT", [D, D], F32, kind="ExternalInput").ap()
    ap_wvT = nc.dram_tensor("wvT", [D, D], F32, kind="ExternalInput").ap()
    ap_w1T = nc.dram_tensor("w1T", [D, HID], F32, kind="ExternalInput").ap()
    ap_w2T = nc.dram_tensor("w2T", [HID, D], F32, kind="ExternalInput").ap()
    ap_b1t = nc.dram_tensor("b1t", [P, HID // P], F32, kind="ExternalInput").ap()
    ap_b2bc = nc.dram_tensor("b2bc", [P, D], F32, kind="ExternalInput").ap()
    s_lnx = nc.dram_tensor("s_lnx", [seq, D], BF16).ap()
    s_lnxo = nc.dram_tensor("s_lnxo", [NOWN, D], BF16).ap()
    s_kT = nc.dram_tensor("s_kT", [D, seq], BF16).ap()
    s_v = nc.dram_tensor("s_v", [seq, D], BF16).ap()
    s_lnat = nc.dram_tensor("s_lnat", [NOWN, D], BF16).ap()
    s_w1q = nc.dram_tensor("s_w1q", [HID // 256, P, DC, 256], BF16).ap()
    s_w2q = nc.dram_tensor("s_w2q", [HID // 256, P, 2, D], BF16).ap()
    s_xq = nc.dram_tensor("s_xq", [NOWN, D], BF16).ap()
    ap_out = nc.dram_tensor("out", [NOWN, D], F32, kind="ExternalOutput").ap()

    es = contextlib.ExitStack()
    pl = Plan(nc, es)

    def sb(st, name, shape, dtype=F32):
        return st.enter_context(nc.sbuf_tensor(name, shape, dtype))

    pb = [es.enter_context(nc.psum_tensor(f"pb{i}", [P, 512], F32))
          for i in range(8)]

    iotaD = sb(es, "iotaD", [P, 256], F32)
    iota64 = sb(es, "iota64", [P, NSLOT * 8], F32)
    thr = sb(es, "thr", [P, NSLOT, 8], F32)
    alpha = sb(es, "alpha", [P, 1], F32)
    betat = sb(es, "betat", [P, 1], F32)
    dsum = sb(es, "dsum", [P, 16], F32)
    epst = sb(es, "epst", [P, 1], F32)
    ln16t = sb(es, "ln16t", [P, 1], F32)
    onesc = sb(es, "onesc", [P, 1], BF16)
    qposb = sb(es, "qposb", [P, NSLOT], F32)
    b1q16 = sb(es, "b1q16", [P, HID // P], F32)
    b1raw = sb(es, "b1raw", [P, HID // P], F32)
    b2tile = sb(es, "b2tile", [P, D], F32)
    stat = sb(es, "stat", [P, 2, 32], F32)
    sq_scr = sb(es, "sq_scr", [P, 4, D], BF16)

    S = {n: pl.sem(n) for n in
         ["sP0", "sP0g", "sXa", "sXb", "sLN", "sLXa", "sLXb", "sTPa", "sTPb", "sKM", "sKE", "sKVw",
          "sKVLa", "sKVLb", "sKVF", "sS", "sE", "sMG", "sMSK", "sAV",
          "sAVE", "sOE", "sLAW", "sWL", "sWQw",
          "sX2", "sLG", "sX2c", "sXQW", "sXQT", "sWL2a", "sWL2b", "sWL2c",
          "sWL2d", "sWL2e", "sWL2f", "sHM",
          "sRL", "sRQ", "sY", "sYB", "sOUT", "sXE", "sLE"]}

    def ln_stats_group(xtiles, par, rs_bias, stt=None, sqt=None):
        nt = len(xtiles)
        st = (stt if stt is not None else stat)[:, par, :]
        sqb = sqt if sqt is not None else sq_scr
        red_mu = None
        for j, (xap, w) in enumerate(xtiles):
            red_mu = pl.op("vector", lambda e, xap=xap, j=j: e.reduce_sum(
                st[:, j:j + 1], xap, axis=AX.X), after=w)
        sq_mu = None
        for j, (xap, w) in enumerate(xtiles):
            sq_mu = pl.op("scalar", lambda e, xap=xap, j=j: e.activation(
                sqb[:, j, :], xap, AF.Square,
                accum_out=st[:, 4 + j:5 + j]), after=w)
        m1 = pl.op("vector", lambda e, nt=nt, st=st: e.tensor_scalar(
            st[:, 8:8 + nt], st[:, 0:nt], -1.0 / D, None, ALU.mult),
            after=[red_mu])
        m2 = pl.op("vector", lambda e, nt=nt, st=st: e.tensor_scalar(
            st[:, 12:12 + nt], st[:, 4:4 + nt], 1.0 / D, None, ALU.mult),
            after=[sq_mu, m1])
        m3 = pl.op("vector", lambda e, nt=nt, st=st: e.tensor_mul(
            st[:, 16:16 + nt], st[:, 8:8 + nt], st[:, 8:8 + nt]),
            after=[m2])
        m4 = pl.op("vector", lambda e, nt=nt, st=st: e.tensor_sub(
            st[:, 20:20 + nt], st[:, 12:12 + nt], st[:, 16:16 + nt]),
            after=[m3])
        s1 = pl.op("scalar", lambda e, nt=nt, st=st: e.activation(
            st[:, 24:24 + nt], st[:, 20:20 + nt], AF.Ln, bias=epst[:, 0:1]),
            after=[m4])
        rb = ln16t[:, 0:1] if rs_bias else 0.0
        s2 = pl.op("scalar", lambda e, nt=nt, st=st, rb=rb: e.activation(
            st[:, 28:28 + nt], st[:, 24:24 + nt], AF.Exp, bias=rb,
            scale=-0.5), after=[s1])
        return m1, s2

    def ln_norm(xap, par, j, out_ap, after, extra=(), stt=None):
        st = (stt if stt is not None else stat)[:, par, :]
        return pl.op("vector", lambda e, st=st, j=j: e.tensor_scalar(
            out_ap, xap, st[:, 8 + j:9 + j], st[:, 28 + j:29 + j],
            ALU.add, ALU.mult), after=after, extra=extra)

    def quant_chain(in_ap, out_ap, width, after, pre_scale=None,
                    post_scale=None, y1t=None, zt=None, extra=(),
                    eng="vector"):
        y1 = y1t[:, 0:width]
        z = zt[:, 0:width]
        if pre_scale is None:
            c = pl.op(eng, lambda e: e.tensor_scalar(
                y1, in_ap, C32, C32, ALU.add, ALU.subtract), after=after)
        else:
            c = pl.op(eng, lambda e: e.tensor_scalar(
                in_ap, in_ap, pre_scale, None, ALU.mult), after=after)
            c = pl.op(eng, lambda e: e.tensor_scalar(
                y1, in_ap, C32, C32, ALU.add, ALU.subtract), after=[c])
        c = pl.op(eng, lambda e: e.tensor_scalar(
            z, y1, C52, C52, ALU.add, ALU.subtract), after=[c])
        if eng == "gpsimd":
            c = pl.op(eng, lambda e: e.tensor_scalar(
                y1, y1, betat[:, 0:1], None, ALU.mult), after=[c])
            c = pl.op(eng, lambda e: e.tensor_add(z, z, y1), after=[c])
        else:
            c = pl.op(eng, lambda e: e.scalar_tensor_tensor(
                z, y1, betat[:, 0:1], z, ALU.mult, ALU.add), after=[c])
        if post_scale is None:
            c = pl.op(eng, lambda e: e.tensor_scalar(
                out_ap, z, -128.0, 127.0, ALU.max, ALU.min), after=[c],
                extra=extra)
        else:
            c = pl.op(eng, lambda e: e.tensor_scalar(
                z, z, -128.0, 127.0, ALU.max, ALU.min), after=[c])
            c = pl.op(eng, lambda e: e.tensor_scalar(
                out_ap, z, post_scale, None, ALU.mult), after=[c],
                extra=extra)
        return c

    pl.emit("sync", lambda e: e.dma_start(out=qposb[:], in_=ap_qpos[:, :]),
            incs=[(S["sP0"], 16)])
    pl.emit("sync", lambda e: e.dma_start(out=b1raw[:], in_=ap_b1t[:, :]),
            incs=[(S["sP0"], 16)])
    pl.emit("sync", lambda e: e.dma_start(out=b2tile[:], in_=ap_b2bc[:, :]),
            incs=[(S["sP0"], 16)])
    gi1 = pl.op("gpsimd", lambda e: e.iota(
        iotaD[:, :], [[1, 256]], channel_multiplier=-1,
        allow_small_or_imprecise_dtypes=True))
    gi2 = pl.op("gpsimd", lambda e: e.iota(
        iota64[:, :], [[128, NSLOT * 8]], channel_multiplier=0,
        allow_small_or_imprecise_dtypes=True), after=[gi1])
    a0 = pl.op("vector", lambda e: e.memset(alpha[:], 2.0 ** -30))
    a1 = pl.op("vector", lambda e: e.tensor_scalar(
        alpha[:], alpha[:], 1.0, 1.0, ALU.add, ALU.subtract), after=[a0])
    a2 = pl.op("vector", lambda e: e.tensor_scalar(
        alpha[:], alpha[:], 2.0 ** 30, None, ALU.mult), after=[a1])
    a3 = pl.op("vector", lambda e: e.tensor_scalar(
        betat[:], alpha[:], -1.0, 1.0, ALU.mult, ALU.add), after=[a2])
    a3 = pl.op("vector", lambda e: e.memset(dsum[:], 0.0), after=[a3])
    e0 = pl.op("vector", lambda e: e.memset(epst[:], EPS), after=[a3])
    e0 = pl.op("vector", lambda e: e.memset(ln16t[:], LN16), after=[e0])
    o0 = pl.op("vector", lambda e: e.memset(onesc[:], 1.0), after=[e0])
    tm = [o0, gi2, (S["sP0"], 48)]
    for s in range(NSLOT):
        tm = [pl.op("vector", lambda e, s=s: e.tensor_scalar(
            thr[:, s, :], iota64[:, 8 * s:8 * s + 8], qposb[:, s:s + 1],
            None, ALU.subtract), after=tm)]
    stAB = contextlib.ExitStack()
    qt = sb(stAB, "qt", [P, DC, NOWN], BF16)
    stA = contextlib.ExitStack()
    xbuf = sb(stA, "xbuf", [P, 8, D], F32)
    qcb = quant_chain(b1raw[:, :], b1q16[:, :], HID // P,
                      after=tm, pre_scale=64.0, post_scale=0.25,
                      y1t=xbuf[:, 0, :], zt=xbuf[:, 1, :])
    qcb = quant_chain(b2tile[:, :], b2tile[:, :], D,
                      after=[qcb], pre_scale=64.0, post_scale=1.0 / 64.0,
                      y1t=xbuf[:, 0, :], zt=xbuf[:, 1, :])
    lnstage = sb(stA, "lnstage", [P, 8, D], BF16)
    lnxT = sb(stA, "lnxT", [P, 2, DC, 1024], BF16)
    kstage = sb(stA, "kstage", [P, DC, 1024], BF16)
    vstage = sb(stA, "vstage", [P, 8, D], BF16)
    wT_sb = {}
    for nm, srcw in (("q", ap_wqT), ("k", ap_wkT), ("v", ap_wvT)):
        wT_sb[nm] = sb(stA, f"w{nm}T_sb", [P, DC, D], BF16)
        pl.emit("gpsimd", lambda e, dst=wT_sb[nm], srcw=srcw: e.dma_start(
            out=dst[:, :, :], in_=srcw.rearrange("(a p) o -> p a o", p=P)),
            incs=[(S["sP0g"], 16)])
    W0 = S["sP0"].v

    PA_RT = ([(r * 1024, 1024, False) for r in range(NRNG)]
             + [(r * ORNG, ORNG, True) for r in range(NORNG)])

    Tg = 0
    bank_gate = {0: 0, 1: 0, 2: 0, 3: 0}
    kv_store_done = 0
    km_before_prev = 0
    lx_after = {}
    for ri, (base, rtok, own) in enumerate(PA_RT):
        ntile = rtok // P
        srcx = ap_xo if own else ap_x
        km_before = S["sKM"].v
        for gi0 in range(0, ntile, 4):
            nt = min(4, ntile - gi0)
            G4 = Tg // 4
            sl = (G4 % 2) * 4
            w = [(S["sLN"], Tg - 4)] if Tg >= 8 else []
            if Tg == 0:
                w = [qcb]
            xsem = S["sXa"] if G4 % 2 == 0 else S["sXb"]
            pl.emit("sync", lambda e, gi0=gi0, nt=nt, base=base, srcx=srcx,
                    sl=sl: e.dma_start(
                        out=xbuf[:, sl:sl + nt, :],
                        in_=srcx.rearrange("(a p) d -> p a d", p=P)[
                            :, (base + gi0 * P) // P:
                            (base + gi0 * P) // P + nt, :]),
                    waits=w, incs=[(xsem, 16)])
            Tg += nt
        srcl = s_lnxo if own else s_lnx
        obase = base
        tp_after_half = {}
        for h in range(rtok // 512):
            hs = S["sLXa"] if h == 0 else S["sLXb"]
            ts_ = S["sTPa"] if h == 0 else S["sTPb"]
            for d in range(DC):
                w = [(hs, 16 * (ri + 1))]
                if d == 0:
                    w.append((ts_, ts_.v))
                if ri > 1 and h == 0 and d == 0:
                    w.append((S["sKM"], km_before_prev))
                pl.emit("sync", lambda e, d=d, h=h, obase=obase, srcl=srcl,
                        ri=ri: e.dma_start_transpose(
                            lnxT[:, ri % 2, d, 512 * h:512 * (h + 1)],
                            srcl[obase + 512 * h:obase + 512 * (h + 1),
                                 d * P:(d + 1) * P]),
                        waits=w, incs=[(ts_, 16)])
            tp_after_half[h] = (ts_, ts_.v)
        km_before_prev = km_before
        t0g = Tg - ntile
        for gi0 in range(0, ntile, 4):
            nt = min(4, ntile - gi0)
            G4 = (t0g + gi0) // 4
            par = G4 % 2
            sl = (G4 % 2) * 4
            xsem = S["sXa"] if G4 % 2 == 0 else S["sXb"]
            glw = (xsem, 16 * (G4 // 2 + 1))
            tiles = []
            for j in range(nt):
                T = t0g + gi0 + j
                w = [glw]
                if j == 0 and T >= 8:
                    w.append((S["sLN"], T - 4))
                tiles.append((xbuf[:, sl + j, :], w))
            mneg, srs = ln_stats_group(tiles, par, False)
            hh = gi0 // 4
            hs = S["sLXa"] if hh == 0 else S["sLXb"]
            lxw_gate = lx_after.get((ri - 1, hh), 0) if ri else 0
            for j in range(nt):
                ln_norm(xbuf[:, sl + j, :], par, j, lnstage[:, gi0 + j, :],
                        [srs, (hs, lxw_gate)],
                        extra=[(S["sLN"], 1)])
        dstl = s_lnxo if own else s_lnx
        for hh in range(0, ntile, 4):
            hn = min(4, ntile - hh)
            hs = S["sLXa"] if hh == 0 else S["sLXb"]
            pl.emit("gpsimd", lambda e, base=base, hh=hh, hn=hn, dstl=dstl:
                    e.dma_start(
                        out=dstl.rearrange("(a p) d -> p a d", p=P)[
                            :, base // P + hh:base // P + hh + hn, :],
                        in_=lnstage[:, hh:hh + hn, :]),
                    waits=[(S["sLN"], Tg - ntile + hh + hn)],
                    incs=[(hs, 16)])
            lx_after[(ri, hh // 4)] = hs.v
        if own:
            groups = [("q", oc, tsub) for tsub in range(rtok // 512)
                      for oc in range(DC)]
        else:
            groups = []
            for h in range(rtok // 512):
                kg = [("k", oc, h) for oc in range(DC)]
                vg = [("v", t8, oh) for oh in range(2)
                      for t8 in range(4 * h, 4 * h + 4)]
                groups += [g for pair in zip(kg, vg) for g in pair]
        for gidx, (kind, ga, gb) in enumerate(groups):
            ghalf = gb if kind != "v" else ga // 4
            bank = (0 if kind != "v" else 2) + gidx % 2
            wts = [tp_after_half[ghalf], (S["sKE"], bank_gate[bank])]
            if ri == 0 and gidx == 0:
                wts.append((S["sP0"], W0))
                wts.append((S["sP0g"], 48))
            for dc in range(DC):
                if kind == "v":
                    bf = (lambda e, dc=dc, t8=ga, oh=gb, bank=bank, ri=ri:
                          e.matmul(pb[bank][:, 0:512],
                                   lnxT[:, ri % 2, dc, t8 * P:(t8 + 1) * P],
                                   wT_sb["v"][:, dc, oh * 512:(oh + 1) * 512],
                                   start=(dc == 0), stop=(dc == DC - 1)))
                else:
                    wsb = wT_sb["k" if kind == "k" else "q"]
                    bf = (lambda e, dc=dc, oc=ga, tsub=gb, bank=bank, wsb=wsb,
                          ri=ri:
                          e.matmul(pb[bank][:, 0:512],
                                   wsb[:, dc, oc * P:(oc + 1) * P],
                                   lnxT[:, ri % 2, dc,
                                        tsub * 512:(tsub + 1) * 512],
                                   start=(dc == 0), stop=(dc == DC - 1)))
                pl.emit("tensor", bf, waits=wts if dc == 0 else (),
                        incs=[(S["sKM"], 1)] if dc == DC - 1 else ())
            kmv = S["sKM"].v
            if kind == "k":
                ev = (lambda e, oc=ga, tsub=gb, bank=bank: e.activation(
                    kstage[:, oc, tsub * 512:(tsub + 1) * 512],
                    pb[bank][:, 0:512], AF.Copy))
            elif kind == "v":
                ev = (lambda e, t8=ga, oh=gb, bank=bank: e.activation(
                    vstage[:, t8, oh * 512:(oh + 1) * 512],
                    pb[bank][:, 0:512], AF.Copy))
            else:
                ev = (lambda e, oc=ga, tsub=gb, bank=bank, base=base:
                      e.activation(
                          qt[:, oc, base + tsub * 512:base + (tsub + 1) * 512],
                          pb[bank][:, 0:512], AF.Copy))
            evw = [(S["sKM"], kmv)]
            if not own:
                evw.append((S["sKVw"], kv_store_done))
            pl.op("scalar", ev, after=evw, extra=[(S["sKE"], 1)])
            bank_gate[bank] = S["sKE"].v
        if not own:
            kev = S["sKE"].v
            pl.emit("gpsimd", lambda e, base=base, rtok=rtok: e.dma_start(
                out=s_kT.rearrange("(a p) t -> p a t", p=P)[
                    :, :, base:base + rtok],
                in_=kstage[:, :, 0:rtok]),
                waits=[(S["sKE"], kev)], incs=[(S["sKVw"], 16)])
            pl.emit("gpsimd", lambda e, base=base, ntile=ntile: e.dma_start(
                out=s_v.rearrange("(a p) d -> p a d", p=P)[
                    :, base // P:base // P + ntile, :],
                in_=vstage[:, 0:ntile, :]),
                waits=[(S["sKE"], kev)], incs=[(S["sKVw"], 16)])
            kv_store_done = S["sKVw"].v

    pl.barrier()
    stA.close()

    stB = contextlib.ExitStack()
    xbuf2 = sb(stB, "xbuf2", [P, 4, D], F32)
    latbuf = sb(stB, "latbuf", [P, 4, D], BF16)
    t16b = sb(stB, "t16b", [P, 2, D], F32)
    xqstage = sb(stB, "xqstage", [P, 4, D], BF16)
    stat2 = sb(stB, "stat2", [P, 2, 32], F32)
    sq2 = sb(stB, "sq2", [P, 4, D], BF16)
    qc1 = sb(stB, "qc1", [P, D], F32)
    qc2 = sb(stB, "qc2", [P, D], F32)
    kgrp = sb(stB, "kgrp", [P, 2, DC, 512], BF16)
    vgrp = sb(stB, "vgrp", [P, 2, 4, D], BF16)
    ptb = sb(stB, "ptb", [P, 8, 256], BF16)
    maskb = sb(stB, "maskb", [P, 8, 256], BF16)
    rcp = sb(stB, "rcp", [P, 16], F32)
    oacc = sb(stB, "oacc", [P, 4, 2, D], F32)
    lnatst = sb(stB, "lnatst", [P, 2, D], BF16)
    wstage = sb(stB, "wstage", [P, 2, HID // 2], F32)
    wqout = sb(stB, "wqout", [P, HID // 2], BF16)
    wy1 = sb(stB, "wy1", [P, HID // 2], BF16)
    wz = sb(stB, "wz", [P, HID // 2], BF16)

    chain_mu = {}

    def emit_part1_loads(gg):
        gi0 = gg * 4
        law_need = 16 * min(2 * gg + 2, NSLOT)
        w = [(S["sLAW"], law_need)]
        if gi0 >= 4:
            w.append((S["sX2c"], gi0))
        pl.emit("gpsimd", lambda e, gi0=gi0: e.dma_start(
            out=xbuf2[:, 0:4, :],
            in_=ap_xo.rearrange("(a p) d -> p a d", p=P)[:, gi0:gi0 + 4, :]),
            waits=w, incs=[(S["sX2"], 16)])
        pl.emit("gpsimd", lambda e, gi0=gi0: e.dma_start(
            out=latbuf[:, 0:4, :],
            in_=s_lnat.rearrange("(a p) d -> p a d", p=P)[:, gi0:gi0 + 4, :]),
            waits=w, incs=[(S["sLG"], 16)])

    def emit_part1_group(gg):
        gi0 = gg * 4
        tiles = []
        for j in range(4):
            ww = [(S["sX2"], 16 * (gg + 1)), (S["sLG"], 16 * (gg + 1))]
            xmu = pl.op("vector", lambda e, j=j: e.tensor_add(
                xbuf2[:, j, :], xbuf2[:, j, :], latbuf[:, j, :]), after=ww)
            tiles.append((xbuf2[:, j, :], [xmu]))
        par = gg % 2
        mneg, srs = ln_stats_group(tiles, par, True, stt=stat2, sqt=sq2)
        for j in range(4):
            i = gi0 + j
            after = [srs]
            if i >= 2 and (i - 2) in chain_mu:
                after.append(chain_mu[i - 2])
            nrm = ln_norm(xbuf2[:, j, :], par, j, t16b[:, i % 2, :],
                          after, extra=[(S["sX2c"], 1)], stt=stat2)
            cw = [nrm]
            if i >= 4:
                cw.append((S["sXQW"], 16 * (i // 4)))
            chain_mu[i] = quant_chain(
                t16b[:, i % 2, :], xqstage[:, i % 4, :], D, cw,
                y1t=qc1, zt=qc2)
        pl.emit("gpsimd", lambda e, gi0=gi0: e.dma_start(
            out=s_xq.rearrange("(a p) d -> p a d", p=P)[
                :, gi0:gi0 + 4, :],
            in_=xqstage[:, 0:4, :]),
            waits=[chain_mu[gi0 + 3]], incs=[(S["sXQW"], 16)])

    wq_units = []
    for wi in range(W1TILES):
        for h in range(2):
            src = ap_w1T[wi * P:(wi + 1) * P,
                         h * (HID // 2):(h + 1) * (HID // 2)]
            wout = s_w1q.rearrange("a p c m -> p a c m")[
                :, 8 * h:8 * (h + 1), wi, :]
            wq_units.append((src, HID // 2, wout, 256))
    for w2i in range(W2TILES):
        src = ap_w2T[w2i * P:(w2i + 1) * P, :]
        wout = s_w2q.rearrange("a p c o -> p a c o")[:, w2i // 2, w2i % 2, :]
        wq_units.append((src, D, wout, None))
    wq_mu = {}
    wqst = {"load": 0, "chain": 0}

    def emit_wq_unit():
        while wqst["load"] < min(len(wq_units), wqst["chain"] + 2):
            u = wqst["load"]
            src, width, _, _ = wq_units[u]
            lw = [wq_mu[u - 2]] if u >= 2 else []
            pl.emit("gpsimd", lambda e, src=src, width=width, u=u:
                    e.dma_start(out=wstage[:, u % 2, 0:width], in_=src),
                    waits=lw, incs=[(S["sWL"], 16)])
            wqst["load"] += 1
        u = wqst["chain"]
        if u >= len(wq_units):
            return
        src, width, wout, m = wq_units[u]
        ch = quant_chain(wstage[:, u % 2, 0:width], wqout[:, 0:width], width,
                         after=[(S["sWL"], 16 * (u + 1)),
                                (S["sWQw"], 16 * u)],
                         pre_scale=64.0, y1t=wy1, zt=wz)
        wq_mu[u] = ch
        wsrc = wqout[:, 0:width]
        if m is not None:
            wsrc = wsrc.rearrange("p (a m) -> p a m", m=m)
        pl.emit("gpsimd", lambda e, wout=wout, wsrc=wsrc: e.dma_start(
            out=wout, in_=wsrc), waits=[ch], incs=[(S["sWQw"], 16)])
        wqst["chain"] += 1

    pstate = {"par": 0}
    lnat_nrm = {}
    lane_free = {}

    def emit_slot_epilogue(s, lane, last_ev):
        sv = pl.vec_pending
        pl.vec_pending = None
        rcp_mu = pl.op("vector", lambda e, s=s: e.reciprocal(
            rcp[:, 2 * s:2 * s + 2], dsum[:, 2 * s:2 * s + 2]),
            after=[last_ev])
        ev_mus = []
        for qs2 in range(2):
            for oh in range(2):
                mu = pl.op("scalar", lambda e, qs2=qs2, oh=oh, lane=lane,
                           s=s: e.activation(
                               oacc[:, lane, qs2, oh * 512:(oh + 1) * 512],
                               oacc[:, lane, qs2, oh * 512:(oh + 1) * 512],
                               AF.Copy, scale=rcp[:, 2 * s + qs2:
                                                  2 * s + qs2 + 1]),
                           after=[rcp_mu, last_ev])
                ev_mus.append(mu)
        par = pstate["par"]
        tiles = [(oacc[:, lane, 0, :], [ev_mus[0], ev_mus[1]]),
                 (oacc[:, lane, 1, :], [ev_mus[2], ev_mus[3]])]
        if s >= 2:
            tiles[0][1].append(lnat_nrm[s - 2])
        mneg, srs = ln_stats_group(tiles, par, False)
        nrm_mu = None
        for j in range(2):
            after = [srs, (S["sLAW"], 16 * s)] if nrm_mu is None \
                else [srs, nrm_mu]
            nrm_mu = ln_norm(oacc[:, lane, j, :], par, j,
                             lnatst[:, j, :], after)
        lnat_nrm[s] = nrm_mu
        lane_free[lane] = nrm_mu
        pstate["par"] ^= 1
        pl.emit("gpsimd", lambda e, s=s: e.dma_start(
            out=s_lnat.rearrange("(a p) d -> p a d", p=P)[
                :, 2 * s:2 * s + 2, :],
            in_=lnatst[:, 0:2, :]),
            waits=[nrm_mu], incs=[(S["sLAW"], 16)])
        pl.vec_pending = sv

    I = 0
    B = 0
    G = 0
    msk = 0
    av_pend = []
    p1_after = {1: [0], 3: [1], 5: [2], 7: [3]}
    pl.vec_pending = []
    for q2 in range(2):
        GQ = 8 if q2 == 0 else 16
        slots4 = list(range(4 * q2, 4 * q2 + 4))
        for g in range(GQ):
            gw = [(S["sKVF"], G - 1)] if G >= 2 else []
            k0 = 512 * g
            kvsem = S["sKVLa"] if G % 2 == 0 else S["sKVLb"]
            gpar = G % 2
            pl.emit("sync", lambda e, k0=k0, gpar=gpar: e.dma_start(
                out=kgrp[:, gpar, :, :],
                in_=s_kT.rearrange("(a p) t -> p a t", p=P)[
                    :, :, k0:k0 + 512]),
                waits=gw, incs=[(kvsem, 16)])
            pl.emit("sync", lambda e, k0=k0, gpar=gpar: e.dma_start(
                out=vgrp[:, gpar, :, :],
                in_=s_v.rearrange("(a p) d -> p a d", p=P)[
                    :, k0 // P:k0 // P + 4, :]),
                incs=[(kvsem, 16)])
            G += 1
            kvl_wait = (kvsem, kvsem.v)
            act = [s for s in slots4 if g < 2 * s + 2]
            for si, s in enumerate(act):
                lane = s % 4
                last_of_group = (si == len(act) - 1)
                slot_last = (g == 2 * s + 1)
                first_burst = (g == 0)
                if B % 3 == 0:
                    emit_wq_unit()
                    emit_wq_unit()
                if g == 2 * s:
                    sv = pl.vec_pending
                    pl.vec_pending = None
                    for m in range(8):
                        w = [(S["sMSK"], 8 * s)]
                        if s == 0 and m == 0:
                            w.append(tm[0])
                        pl.op("vector", lambda e, s=s, m=m: e.tensor_scalar(
                            maskb[:, m, :], iotaD[:, :], thr[:, s, m:m + 1],
                            None, ALU.is_ge), after=w,
                            extra=[(S["sMG"], 1)])
                    pl.vec_pending = sv
                iter_info = []
                for kk in range(4):
                    kb = 4 * g + kk
                    sbank = 4 + I % 2
                    for dc in range(DC):
                        w = []
                        if dc == 0:
                            w = [kvl_wait]
                            if I >= 2:
                                w.append((S["sE"], I - 1))
                        pl.emit("tensor", lambda e, dc=dc, kk=kk, s=s,
                                gpar=gpar, sbank=sbank: e.matmul(
                                    pb[sbank][:, 0:256],
                                    kgrp[:, gpar, dc, kk * P:(kk + 1) * P],
                                    qt[:, dc, s * 256:(s + 1) * 256],
                                    start=(dc == 0), stop=(dc == DC - 1)),
                                waits=w if dc == 0 else (),
                                incs=[(S["sS"], 1)] if dc == DC - 1 else ())
                    w = [(S["sS"], I + 1)]
                    if I >= 8:
                        w.append((S["sAV"], I - 7))
                    pl.op("scalar", lambda e, I=I, sbank=sbank: e.activation(
                        ptb[:, I % 8, :], pb[sbank][:, 0:256], AF.Exp,
                        scale=ISQ), after=w, extra=[(S["sE"], 1)])
                    masked = kb >= 8 * s
                    if masked:
                        m = kb - 8 * s
                        sv = pl.vec_pending
                        pl.vec_pending = None
                        pl.op("vector", lambda e, I=I, m=m: e.tensor_mul(
                            ptb[:, I % 8, :], ptb[:, I % 8, :],
                            maskb[:, m, :]),
                            after=[(S["sE"], I + 1),
                                   (S["sMG"], 8 * s + m + 1)],
                            extra=[(S["sMSK"], 1)])
                        pl.vec_pending = sv
                        msk += 1
                    first_w = [(S["sMSK"], msk)] if masked \
                        else [(S["sE"], I + 1)]
                    iter_info.append((I, kb, tuple(first_w)))
                    pl.pump(3)
                    I += 1

                def make_av_block(s=s, lane=lane, B=B,
                                  iter_info=tuple(iter_info), gpar=gpar,
                                  first_burst=first_burst,
                                  last_of_group=last_of_group):
                    def emit_block():
                        for j, (I0, kb, first_w) in enumerate(iter_info):
                            kk = kb % 4
                            fw = list(first_w)
                            if j == 0 and B >= 1:
                                fw.append((S["sAVE"], 5 * B))
                            nmm = 0
                            for qs2 in range(2):
                                for oh in range(2):
                                    avincs = []
                                    if nmm == 3 and j == 3 and last_of_group:
                                        avincs.append((S["sKVF"], 1))
                                    pl.emit("tensor", lambda e, I0=I0,
                                            qs2=qs2, oh=oh, gpar=gpar, kk=kk:
                                            e.matmul(
                                                pb[2 * qs2 + oh][:, 0:512],
                                                ptb[:, I0 % 8,
                                                    qs2 * P:(qs2 + 1) * P],
                                                vgrp[:, gpar, kk,
                                                     oh * 512:(oh + 1) * 512],
                                                start=(kk == 0),
                                                stop=(kk == 3)),
                                            waits=fw if nmm == 0 else (),
                                            incs=avincs)
                                    nmm += 1
                                pl.emit("tensor", lambda e, I0=I0, qs2=qs2,
                                        j=j: e.matmul(
                                            pb[7][:, qs2:qs2 + 1],
                                            ptb[:, I0 % 8,
                                                qs2 * P:(qs2 + 1) * P],
                                            onesc[:, 0:1],
                                            start=(j == 0 and qs2 == 0),
                                            stop=(j == 3 and qs2 == 1)),
                                        incs=[(S["sAV"], 1)]
                                        if qs2 == 1 else ())
                        sAV_t = iter_info[-1][0] + 1
                        sv = pl.vec_pending
                        pl.vec_pending = None
                        ew = [(S["sAV"], sAV_t)]
                        if first_burst and lane in lane_free:
                            ew.append(lane_free[lane])
                        last = pl.op("vector", lambda e, s=s: e.tensor_add(
                            dsum[:, 2 * s:2 * s + 2],
                            dsum[:, 2 * s:2 * s + 2], pb[7][:, 0:2]),
                            after=ew, extra=[(S["sAVE"], 1)])
                        for qs2 in range(2):
                            for oh in range(2):
                                if first_burst:
                                    last = pl.op(
                                        "vector",
                                        lambda e, qs2=qs2, oh=oh, lane=lane:
                                        e.tensor_copy(
                                            oacc[:, lane, qs2,
                                                 oh * 512:(oh + 1) * 512],
                                            pb[2 * qs2 + oh][:, 0:512]),
                                        extra=[(S["sAVE"], 1)])
                                else:
                                    last = pl.op(
                                        "vector",
                                        lambda e, qs2=qs2, oh=oh, lane=lane:
                                        e.tensor_add(
                                            oacc[:, lane, qs2,
                                                 oh * 512:(oh + 1) * 512],
                                            oacc[:, lane, qs2,
                                                 oh * 512:(oh + 1) * 512],
                                            pb[2 * qs2 + oh][:, 0:512]),
                                        extra=[(S["sAVE"], 1)])
                        pl.vec_pending = sv
                        return last
                    return emit_block
                if av_pend:
                    av_pend.pop(0)()
                av_pend.append(make_av_block())
                B += 1
                if slot_last:
                    last_ev = None
                    while av_pend:
                        last_ev = av_pend.pop(0)()
                    emit_slot_epilogue(s, lane, last_ev)
                    for gg in p1_after.get(s, []):
                        emit_part1_loads(gg)
                        emit_part1_group(gg)
    while wqst["chain"] < len(wq_units):
        emit_wq_unit()
    pl.pump_all()
    pl.vec_pending = None
    pl.barrier()
    stB.close()
    stAB.close()

    stC = contextlib.ExitStack()
    xqT = sb(stC, "xqT", [P, DC, NOWN], BF16)
    for d in range(DC):
        pl.emit("sync", lambda e, d=d: e.dma_start_transpose(
            xqT[:, d, :], s_xq[:, d * P:(d + 1) * P]),
            waits=[(S["sXQW"], 16 * (NHT // 4))],
            incs=[(S["sXQT"], 16)])
    WRING = 6
    w1ch = sb(stC, "w1ch", [P, WRING, DC, 256], BF16)
    w2ch = sb(stC, "w2ch", [P, WRING, 2, D], BF16)
    yacc = sb(stC, "yacc", [P, NT8, 2, D], F32)
    usb = sb(stC, "usb", [P, 6, 256], F32)
    rbuf = sb(stC, "rbuf", [P, 6, 256], BF16)
    ysb = sb(stC, "ysb", [P, 2, D], F32)
    outst = sb(stC, "outst", [P, 2, D], F32)
    xet = sb(stC, "xet", [P, 2, D], F32)
    late = sb(stC, "late", [P, 2, D], F32)
    qc1c = sb(stC, "qc1c", [P, 256], F32)
    qc2c = sb(stC, "qc2c", [P, 256], F32)

    wlsems = [S["sWL2a"], S["sWL2b"], S["sWL2c"], S["sWL2d"],
              S["sWL2e"], S["sWL2f"]]
    for hc in range(NHC):
        lw = []
        if hc >= WRING:
            lw = [(S["sY"], 64 * ((hc - WRING) // 4 + 1))]
        wsem = wlsems[hc % WRING]
        pl.emit("sync", lambda e, hc=hc: e.dma_start(
            out=w1ch[:, hc % WRING, :, :],
            in_=s_w1q.rearrange("a p c m -> p a c m")[:, hc, :, :]),
            waits=lw, incs=[(wsem, 16)])
        pl.emit("sync", lambda e, hc=hc: e.dma_start(
            out=w2ch[:, hc % WRING, :, :],
            in_=s_w2q.rearrange("a p c o -> p a c o")[:, hc, :, :]),
            incs=[(wsem, 16)])

    idx = 0
    MM2_SKEW = 4
    mm2_pend = []

    def emit_mm2(idx, hc, h2, step, t8q):
        mw = [(S["sRQ"], idx + 1)]
        if step == 0:
            mw.append((S["sYB"], 4 * t8q))
        nmm = 0
        for tsub in range(2):
            for oh in range(2):
                pl.emit("tensor", lambda e, idx=idx, tsub=tsub,
                        oh=oh, hc=hc, h2=h2, step=step: e.matmul(
                            pb[2 * tsub + oh][:, 0:512],
                            rbuf[:, idx % 6, tsub * P:(tsub + 1) * P],
                            w2ch[:, hc % WRING, h2,
                                 oh * 512:(oh + 1) * 512],
                            start=(step == 0), stop=(step == 7)),
                        waits=mw if nmm == 0 else (),
                        incs=[(S["sY"], 1)] if nmm == 3 else ())
                nmm += 1

    ep_state = {"lny_mu_prev": None, "out_mu_prev": None}
    lny_nrm = {}
    out_add_mu = {}

    def emit_epilogue(t8):
        ew = []
        if t8 >= 1:
            ew = [out_add_mu[t8 - 1]]
        pl.emit("sync", lambda e, t8=t8: e.dma_start(
            out=xet[:, 0:2, :],
            in_=ap_xo.rearrange("(a p) d -> p a d", p=P)[
                :, 2 * t8:2 * t8 + 2, :]),
            waits=ew, incs=[(S["sXE"], 16)])
        pl.emit("gpsimd", lambda e, t8=t8: e.dma_start(
            out=late[:, 0:2, :],
            in_=s_lnat.rearrange("(a p) d -> p a d", p=P)[
                :, 2 * t8:2 * t8 + 2, :]),
            waits=ew, incs=[(S["sLE"], 16)])
        yb_mu = None
        for j in range(2):
            w = [(S["sYB"], 4 * (24 + t8 + 1))]
            if ep_state["lny_mu_prev"] is not None and yb_mu is None:
                w.append(ep_state["lny_mu_prev"])
            yb_mu = pl.op("vector", lambda e, j=j, t8=t8:
                          e.scalar_tensor_tensor(
                              ysb[:, j, :], yacc[:, t8, j, :], 1.0 / 1024.0,
                              b2tile[:, :], ALU.mult, ALU.add),
                          after=w)
        tiles = [(ysb[:, j, :], [yb_mu]) for j in range(2)]
        if t8 >= 2:
            tiles[0][1].append(lny_nrm[t8 - 2])
        par = t8 % 2
        mneg, srs = ln_stats_group(tiles, par, False)
        x2a_mu = None
        for j in range(2):
            aw = [(S["sXE"], 16 * (t8 + 1)), (S["sLE"], 16 * (t8 + 1))]
            if x2a_mu is not None:
                aw = [x2a_mu]
            x2a_mu = pl.op("vector", lambda e, j=j: e.tensor_add(
                xet[:, j, :], xet[:, j, :], late[:, j, :]), after=aw)
        out_mu = None
        lny_mu_prev = None
        for j in range(2):
            after = [srs] if lny_mu_prev is None else [srs, lny_mu_prev]
            if j == 0 and ep_state["out_mu_prev"] is not None:
                after.append(ep_state["out_mu_prev"])
            nrm = ln_norm(ysb[:, j, :], par, j, ysb[:, j, :], after)
            lny_mu_prev = nrm
            ow = [nrm, (S["sOUT"], 16 * t8)]
            if j == 0:
                ow.append(x2a_mu)
            out_mu = pl.op("vector", lambda e, t8=t8, j=j: e.tensor_add(
                outst[:, j, :], xet[:, j, :], ysb[:, j, :]),
                after=ow)
        lny_nrm[t8] = lny_mu_prev
        ep_state["lny_mu_prev"] = lny_mu_prev
        ep_state["out_mu_prev"] = out_mu
        out_add_mu[t8] = out_mu
        pl.emit("gpsimd", lambda e, t8=t8: e.dma_start(
            out=ap_out.rearrange("(a p) d -> p a d", p=P)[
                :, 2 * t8:2 * t8 + 2, :],
            in_=outst[:, 0:2, :]),
            waits=[out_mu], incs=[(S["sOUT"], 16)])

    for q in range(4):
        for t8 in range(NT8):
            for step in range(8):
                hc = 4 * q + step // 2
                h2 = step % 2
                hbank = 4 + idx % 4
                w = [(wlsems[hc % WRING], 32 * (hc // WRING + 1)),
                     (S["sXQT"], 16 * DC)]
                if idx >= 4:
                    w.append((S["sRL"], idx - 3))
                for dc in range(DC):
                    pl.emit("tensor", lambda e, dc=dc, h2=h2, t8=t8, hc=hc,
                            hbank=hbank: e.matmul(
                                pb[hbank][:, 0:256],
                                w1ch[:, hc % WRING, dc, h2 * P:(h2 + 1) * P],
                                xqT[:, dc, t8 * 256:(t8 + 1) * 256],
                                start=(dc == 0), stop=(dc == DC - 1)),
                            waits=w if dc == 0 else (),
                            incs=[(S["sHM"], 1)] if dc == DC - 1 else ())
                rw = [(S["sHM"], idx + 1)]
                if idx >= 6:
                    rw.append((S["sRQ"], idx - 5))
                hcol = 2 * hc + h2
                pl.op("scalar", lambda e, idx=idx, hcol=hcol, hbank=hbank:
                      e.activation(usb[:, idx % 6, :], pb[hbank][:, 0:256],
                                   AF.Relu, bias=b1q16[:, hcol:hcol + 1],
                                   scale=1.0 / 64.0),
                      after=rw, extra=[(S["sRL"], 1)])
                cw = [(S["sRL"], idx + 1)]
                if idx >= 6:
                    cw.append((S["sY"], idx - 5))
                quant_chain(usb[:, idx % 6, :], rbuf[:, idx % 6, :], 256,
                            cw, extra=[(S["sRQ"], 1)], y1t=qc1c, zt=qc2c)
                mm2_pend.append((idx, hc, h2, step, q * 8 + t8))
                if len(mm2_pend) > MM2_SKEW:
                    emit_mm2(*mm2_pend.pop(0))
                idx += 1
            for tsub in range(2):
                for oh in range(2):
                    w = [(S["sY"], 64 * q + 8 * (t8 + 1))]
                    if q >= 1:
                        w.append((S["sYB"], 4 * ((q - 1) * 8 + t8 + 1)))
                    if q == 0:
                        pl.op("scalar", lambda e, tsub=tsub, oh=oh, t8=t8:
                              e.activation(
                                  yacc[:, t8, tsub, oh * 512:(oh + 1) * 512],
                                  pb[2 * tsub + oh][:, 0:512], AF.Copy),
                              after=w, extra=[(S["sYB"], 1)])
                    else:
                        pl.op("vector", lambda e, tsub=tsub, oh=oh, t8=t8:
                              e.tensor_add(
                                  yacc[:, t8, tsub, oh * 512:(oh + 1) * 512],
                                  yacc[:, t8, tsub, oh * 512:(oh + 1) * 512],
                                  pb[2 * tsub + oh][:, 0:512]),
                              after=w, extra=[(S["sYB"], 1)])
            if q == 3:
                emit_epilogue(t8)
    while mm2_pend:
        emit_mm2(*mm2_pend.pop(0))

    pl.waits_only("gpsimd", [(S["sOUT"], 16 * NT8)])
    stC.close()

    with es, nc.Block() as block:
        for ename in ENGN:
            dec = getattr(block, ename)

            def body(eng, fns=pl.fns[ename]):
                for fn in fns:
                    fn(eng)
            dec(body)
    return nc


def own_tiles(c, seq):
    nt = seq // 256
    return [j for j in range(nt) if j % 8 in (c, 7 - c)]


def make_inmaps(seq, x_full, wq, wk, wv, w1, b1, w2, b2):
    f32 = np.float32
    shared = dict(
        wqT=np.ascontiguousarray(wq.T, dtype=f32),
        wkT=np.ascontiguousarray(wk.T, dtype=f32),
        wvT=np.ascontiguousarray(wv.T, dtype=f32),
        w1T=np.ascontiguousarray(w1.T, dtype=f32),
        w2T=np.ascontiguousarray(w2.T, dtype=f32),
        b1t=np.ascontiguousarray(b1.reshape(HID // P, P).T, dtype=f32),
        b2bc=np.ascontiguousarray(np.broadcast_to(b2[None, :], (P, D)),
                                  dtype=f32),
    )
    in_maps = []
    tiles_by_core = []
    for core in range(8):
        b, c = divmod(core, 4)
        tiles = own_tiles(c, seq)
        xb = np.ascontiguousarray(x_full[b], dtype=f32)
        x_own = np.ascontiguousarray(
            np.concatenate([xb[j * 256:(j + 1) * 256] for j in tiles], 0))
        qpos = np.broadcast_to(
            np.array([j * 256 for j in tiles], dtype=f32)[None, :],
            (P, len(tiles)))
        in_maps.append(dict(x=xb, x_own=x_own,
                            qpos=np.ascontiguousarray(qpos), **shared))
        tiles_by_core.append(tiles)
    return in_maps, tiles_by_core


def assemble(seq, results, tiles_by_core):
    B = 2
    out = np.zeros((B, seq, D), dtype=np.float32)
    for core in range(8):
        b = core // 4
        r = np.asarray(results[core]["out"])
        for s, j in enumerate(tiles_by_core[core]):
            out[b, j * 256:(j + 1) * 256] = r[s * 256:(s + 1) * 256]
    return out

from concourse.bass_utils import run_bass_kernel_spmd


def kernel(L0_attn_inputs, L0_attn_Wq, L0_attn_Wk, L0_attn_Wv,
           L0_ffn_w1, L0_ffn_b1, L0_ffn_w2, L0_ffn_b2):
    seq = 8192
    x = np.asarray(L0_attn_inputs, dtype=np.float32)
    in_maps, tiles = make_inmaps(seq, x,
                                 np.asarray(L0_attn_Wq, dtype=np.float32),
                                 np.asarray(L0_attn_Wk, dtype=np.float32),
                                 np.asarray(L0_attn_Wv, dtype=np.float32),
                                 np.asarray(L0_ffn_w1, dtype=np.float32),
                                 np.asarray(L0_ffn_b1, dtype=np.float32),
                                 np.asarray(L0_ffn_w2, dtype=np.float32),
                                 np.asarray(L0_ffn_b2, dtype=np.float32))
    nc = build(seq)
    res = run_bass_kernel_spmd(nc, in_maps, core_ids=list(range(8)))
    return assemble(seq, res.results, tiles)


# version 18
# speedup vs baseline: 2.5056x; 1.0116x over previous
import contextlib
import numpy as np

import concourse.bass as bass
import concourse.mybir as mybir

F32 = mybir.dt.float32
BF16 = mybir.dt.bfloat16
AF = mybir.ActivationFunctionType
ALU = mybir.AluOpType
AX = mybir.AxisListType

P = 128
D = 1024
HID = 4096
DC = D // 128
C32 = 1.5 * 2.0 ** 23
C52 = 1.5 * 2.0 ** 52
LN16 = float(np.log(16.0))
EPS = 1e-5
ISQ = 1.0 / 32.0
ENGN = ("sync", "gpsimd", "vector", "scalar", "tensor")


class _S:
    __slots__ = ("h", "v")

    def __init__(self, h):
        self.h = h
        self.v = 0


class Plan:
    def __init__(self, nc, es):
        self.nc = nc
        self.es = es
        self.fns = {e: [] for e in ENGN}
        self.sems = {}
        self.mu = {}
        for e in ENGN:
            self.mu[e] = self.sem("mu_" + e)

    def sem(self, name):
        s = _S(self.es.enter_context(self.nc.semaphore(name)))
        self.sems[name] = s
        return s

    vec_pending = None

    def emit(self, engine, build, waits=(), incs=()):
        for s, n in incs:
            s.v += n
        wl = [(s.h, int(v)) for s, v in waits if v > 0]
        il = [(s.h, n) for s, n in incs]

        def fn(eng, build=build, wl=wl, il=il):
            for h, v in wl:
                eng.wait_ge(h, v)
            instr = build(eng)
            for h, n in il:
                instr.then_inc(h, n)
        if engine == "vector" and self.vec_pending is not None:
            self.vec_pending.append(fn)
        else:
            self.fns[engine].append(fn)

    def pump(self, n):
        while n > 0 and self.vec_pending:
            self.fns["vector"].append(self.vec_pending.pop(0))
            n -= 1

    def pump_all(self):
        if self.vec_pending:
            self.fns["vector"] += self.vec_pending
            self.vec_pending = []

    def op(self, engine, build, after=(), extra=()):
        extra = list(extra)
        if extra:
            assert len(extra) == 1
            self.emit(engine, build, waits=after, incs=extra)
            return (extra[0][0], extra[0][0].v)
        m = self.mu[engine]
        self.emit(engine, build, waits=after, incs=[(m, 1)])
        return (m, m.v)

    def waits_only(self, engine, waits):
        wl = [(s.h, int(v)) for s, v in waits if v > 0]

        def fn(eng, wl=wl):
            for h, v in wl:
                eng.wait_ge(h, v)
        self.fns[engine].append(fn)

    def snapshot(self):
        return {k: s.v for k, s in self.sems.items()}

    def barrier(self):
        cur = self.snapshot()
        waits = [(self.sems[k], v) for k, v in cur.items() if v > 0]
        for e in ENGN:
            self.waits_only(e, waits)


def build(seq=8192):
    assert seq % 2048 == 0
    NSLOT = seq // 1024
    NOWN = NSLOT * 256
    NRNG = seq // 1024
    NORNG = max(1, NOWN // 1024)
    ORNG = NOWN // NORNG
    NHT = NOWN // P
    NT8 = NOWN // 256
    L = [8 * s + 8 for s in range(NSLOT)]
    W1TILES = DC
    W2TILES = HID // P
    NWT = W1TILES + W2TILES
    NHC = HID // 256

    nc = bass.Bass("TRN2", target_bir_lowering=False)

    ap_x = nc.dram_tensor("x", [seq, D], F32, kind="ExternalInput").ap()
    ap_xo = nc.dram_tensor("x_own", [NOWN, D], F32, kind="ExternalInput").ap()
    ap_qpos = nc.dram_tensor("qpos", [P, NSLOT], F32, kind="ExternalInput").ap()
    ap_wqT = nc.dram_tensor("wqT", [D, D], F32, kind="ExternalInput").ap()
    ap_wkT = nc.dram_tensor("wkT", [D, D], F32, kind="ExternalInput").ap()
    ap_wvT = nc.dram_tensor("wvT", [D, D], F32, kind="ExternalInput").ap()
    ap_w1T = nc.dram_tensor("w1T", [D, HID], F32, kind="ExternalInput").ap()
    ap_w2T = nc.dram_tensor("w2T", [HID, D], F32, kind="ExternalInput").ap()
    ap_b1t = nc.dram_tensor("b1t", [P, HID // P], F32, kind="ExternalInput").ap()
    ap_b2bc = nc.dram_tensor("b2bc", [P, D], F32, kind="ExternalInput").ap()
    s_lnx = nc.dram_tensor("s_lnx", [seq, D], BF16).ap()
    s_lnxo = nc.dram_tensor("s_lnxo", [NOWN, D], BF16).ap()
    s_kT = nc.dram_tensor("s_kT", [D, seq], BF16).ap()
    s_v = nc.dram_tensor("s_v", [seq, D], BF16).ap()
    s_lnat = nc.dram_tensor("s_lnat", [NOWN, D], BF16).ap()
    s_w1q = nc.dram_tensor("s_w1q", [HID // 256, P, DC, 256], BF16).ap()
    s_w2q = nc.dram_tensor("s_w2q", [HID // 256, P, 2, D], BF16).ap()
    s_xq = nc.dram_tensor("s_xq", [NOWN, D], BF16).ap()
    ap_out = nc.dram_tensor("out", [NOWN, D], F32, kind="ExternalOutput").ap()

    es = contextlib.ExitStack()
    pl = Plan(nc, es)

    def sb(st, name, shape, dtype=F32):
        return st.enter_context(nc.sbuf_tensor(name, shape, dtype))

    pb = [es.enter_context(nc.psum_tensor(f"pb{i}", [P, 512], F32))
          for i in range(8)]

    iotaD = sb(es, "iotaD", [P, 256], F32)
    iota64 = sb(es, "iota64", [P, NSLOT * 8], F32)
    thr = sb(es, "thr", [P, NSLOT, 8], F32)
    alpha = sb(es, "alpha", [P, 1], F32)
    betat = sb(es, "betat", [P, 1], F32)
    dsum = sb(es, "dsum", [P, 16], F32)
    epst = sb(es, "epst", [P, 1], F32)
    ln16t = sb(es, "ln16t", [P, 1], F32)
    onesc = sb(es, "onesc", [P, 1], BF16)
    qposb = sb(es, "qposb", [P, NSLOT], F32)
    b1q16 = sb(es, "b1q16", [P, HID // P], F32)
    b1raw = sb(es, "b1raw", [P, HID // P], F32)
    b2tile = sb(es, "b2tile", [P, D], F32)
    stat = sb(es, "stat", [P, 2, 32], F32)
    sq_scr = sb(es, "sq_scr", [P, 4, D], BF16)

    S = {n: pl.sem(n) for n in
         ["sP0", "sP0g", "sXa", "sXb", "sLN", "sLXa", "sLXb", "sTPa", "sTPb", "sKM", "sKE", "sKVw",
          "sKVLa", "sKVLb", "sKVF", "sS", "sE", "sMG", "sMSK", "sAV",
          "sAVE", "sOE", "sLAW", "sWL", "sWQw",
          "sX2", "sLG", "sX2c", "sXQW", "sXQT", "sWL2a", "sWL2b", "sWL2c",
          "sWL2d", "sWL2e", "sWL2f", "sHM",
          "sRL", "sRQ", "sY", "sYB", "sOUT", "sXE", "sLE"]}

    def ln_stats_group(xtiles, par, rs_bias, stt=None, sqt=None):
        nt = len(xtiles)
        st = (stt if stt is not None else stat)[:, par, :]
        sqb = sqt if sqt is not None else sq_scr
        red_mu = None
        for j, (xap, w) in enumerate(xtiles):
            red_mu = pl.op("vector", lambda e, xap=xap, j=j: e.reduce_sum(
                st[:, j:j + 1], xap, axis=AX.X), after=w)
        sq_mu = None
        for j, (xap, w) in enumerate(xtiles):
            sq_mu = pl.op("scalar", lambda e, xap=xap, j=j: e.activation(
                sqb[:, j, :], xap, AF.Square,
                accum_out=st[:, 4 + j:5 + j]), after=w)
        m1 = pl.op("vector", lambda e, nt=nt, st=st: e.tensor_scalar(
            st[:, 8:8 + nt], st[:, 0:nt], -1.0 / D, None, ALU.mult),
            after=[red_mu])
        m2 = pl.op("vector", lambda e, nt=nt, st=st: e.tensor_scalar(
            st[:, 12:12 + nt], st[:, 4:4 + nt], 1.0 / D, None, ALU.mult),
            after=[sq_mu, m1])
        m3 = pl.op("vector", lambda e, nt=nt, st=st: e.tensor_mul(
            st[:, 16:16 + nt], st[:, 8:8 + nt], st[:, 8:8 + nt]),
            after=[m2])
        m4 = pl.op("vector", lambda e, nt=nt, st=st: e.tensor_sub(
            st[:, 20:20 + nt], st[:, 12:12 + nt], st[:, 16:16 + nt]),
            after=[m3])
        s1 = pl.op("scalar", lambda e, nt=nt, st=st: e.activation(
            st[:, 24:24 + nt], st[:, 20:20 + nt], AF.Ln, bias=epst[:, 0:1]),
            after=[m4])
        rb = ln16t[:, 0:1] if rs_bias else 0.0
        s2 = pl.op("scalar", lambda e, nt=nt, st=st, rb=rb: e.activation(
            st[:, 28:28 + nt], st[:, 24:24 + nt], AF.Exp, bias=rb,
            scale=-0.5), after=[s1])
        return m1, s2

    def ln_norm(xap, par, j, out_ap, after, extra=(), stt=None):
        st = (stt if stt is not None else stat)[:, par, :]
        return pl.op("vector", lambda e, st=st, j=j: e.tensor_scalar(
            out_ap, xap, st[:, 8 + j:9 + j], st[:, 28 + j:29 + j],
            ALU.add, ALU.mult), after=after, extra=extra)

    def quant_chain(in_ap, out_ap, width, after, pre_scale=None,
                    post_scale=None, y1t=None, zt=None, extra=(),
                    eng="vector"):
        y1 = y1t[:, 0:width]
        z = zt[:, 0:width]
        if pre_scale is None:
            c = pl.op(eng, lambda e: e.tensor_scalar(
                y1, in_ap, C32, C32, ALU.add, ALU.subtract), after=after)
        else:
            c = pl.op(eng, lambda e: e.tensor_scalar(
                in_ap, in_ap, pre_scale, None, ALU.mult), after=after)
            c = pl.op(eng, lambda e: e.tensor_scalar(
                y1, in_ap, C32, C32, ALU.add, ALU.subtract), after=[c])
        c = pl.op(eng, lambda e: e.tensor_scalar(
            z, y1, C52, C52, ALU.add, ALU.subtract), after=[c])
        if eng == "gpsimd":
            c = pl.op(eng, lambda e: e.tensor_scalar(
                y1, y1, betat[:, 0:1], None, ALU.mult), after=[c])
            c = pl.op(eng, lambda e: e.tensor_add(z, z, y1), after=[c])
        else:
            c = pl.op(eng, lambda e: e.scalar_tensor_tensor(
                z, y1, betat[:, 0:1], z, ALU.mult, ALU.add), after=[c])
        if post_scale is None:
            c = pl.op(eng, lambda e: e.tensor_scalar(
                out_ap, z, -128.0, 127.0, ALU.max, ALU.min), after=[c],
                extra=extra)
        else:
            c = pl.op(eng, lambda e: e.tensor_scalar(
                z, z, -128.0, 127.0, ALU.max, ALU.min), after=[c])
            c = pl.op(eng, lambda e: e.tensor_scalar(
                out_ap, z, post_scale, None, ALU.mult), after=[c],
                extra=extra)
        return c

    pl.emit("sync", lambda e: e.dma_start(out=qposb[:], in_=ap_qpos[:, :]),
            incs=[(S["sP0"], 16)])
    pl.emit("sync", lambda e: e.dma_start(out=b1raw[:], in_=ap_b1t[:, :]),
            incs=[(S["sP0"], 16)])
    pl.emit("sync", lambda e: e.dma_start(out=b2tile[:], in_=ap_b2bc[:, :]),
            incs=[(S["sP0"], 16)])
    gi1 = pl.op("gpsimd", lambda e: e.iota(
        iotaD[:, :], [[1, 256]], channel_multiplier=-1,
        allow_small_or_imprecise_dtypes=True))
    gi2 = pl.op("gpsimd", lambda e: e.iota(
        iota64[:, :], [[128, NSLOT * 8]], channel_multiplier=0,
        allow_small_or_imprecise_dtypes=True), after=[gi1])
    a0 = pl.op("vector", lambda e: e.memset(alpha[:], 2.0 ** -30))
    a1 = pl.op("vector", lambda e: e.tensor_scalar(
        alpha[:], alpha[:], 1.0, 1.0, ALU.add, ALU.subtract), after=[a0])
    a2 = pl.op("vector", lambda e: e.tensor_scalar(
        alpha[:], alpha[:], 2.0 ** 30, None, ALU.mult), after=[a1])
    a3 = pl.op("vector", lambda e: e.tensor_scalar(
        betat[:], alpha[:], -1.0, 1.0, ALU.mult, ALU.add), after=[a2])
    a3 = pl.op("vector", lambda e: e.memset(dsum[:], 0.0), after=[a3])
    e0 = pl.op("vector", lambda e: e.memset(epst[:], EPS), after=[a3])
    e0 = pl.op("vector", lambda e: e.memset(ln16t[:], LN16), after=[e0])
    o0 = pl.op("vector", lambda e: e.memset(onesc[:], 1.0), after=[e0])
    tm = [o0, gi2, (S["sP0"], 48)]
    for s in range(NSLOT):
        tm = [pl.op("vector", lambda e, s=s: e.tensor_scalar(
            thr[:, s, :], iota64[:, 8 * s:8 * s + 8], qposb[:, s:s + 1],
            None, ALU.subtract), after=tm)]
    stAB = contextlib.ExitStack()
    qt = sb(stAB, "qt", [P, DC, NOWN], BF16)
    stA = contextlib.ExitStack()
    xbuf = sb(stA, "xbuf", [P, 8, D], F32)
    qcb = quant_chain(b1raw[:, :], b1q16[:, :], HID // P,
                      after=tm, pre_scale=64.0, post_scale=0.25,
                      y1t=xbuf[:, 0, :], zt=xbuf[:, 1, :])
    qcb = quant_chain(b2tile[:, :], b2tile[:, :], D,
                      after=[qcb], pre_scale=64.0, post_scale=1.0 / 64.0,
                      y1t=xbuf[:, 0, :], zt=xbuf[:, 1, :])
    lnstage = sb(stA, "lnstage", [P, 8, D], BF16)
    lnxT = sb(stA, "lnxT", [P, 2, DC, 1024], BF16)
    kstage = sb(stA, "kstage", [P, DC, 1024], BF16)
    vstage = sb(stA, "vstage", [P, 8, D], BF16)
    wT_sb = {}
    for nm, srcw in (("q", ap_wqT), ("k", ap_wkT), ("v", ap_wvT)):
        wT_sb[nm] = sb(stA, f"w{nm}T_sb", [P, DC, D], BF16)
        pl.emit("gpsimd", lambda e, dst=wT_sb[nm], srcw=srcw: e.dma_start(
            out=dst[:, :, :], in_=srcw.rearrange("(a p) o -> p a o", p=P)),
            incs=[(S["sP0g"], 16)])
    W0 = S["sP0"].v

    PA_RT = ([(r * 1024, 1024, False) for r in range(NRNG)]
             + [(r * ORNG, ORNG, True) for r in range(NORNG)])

    Tg = 0
    bank_gate = {0: 0, 1: 0, 2: 0, 3: 0}
    kv_store_done = 0
    km_before_prev = 0
    lx_after = {}
    for ri, (base, rtok, own) in enumerate(PA_RT):
        ntile = rtok // P
        srcx = ap_xo if own else ap_x
        km_before = S["sKM"].v
        for gi0 in range(0, ntile, 4):
            nt = min(4, ntile - gi0)
            G4 = Tg // 4
            sl = (G4 % 2) * 4
            w = [(S["sLN"], Tg - 4)] if Tg >= 8 else []
            if Tg == 0:
                w = [qcb]
            xsem = S["sXa"] if G4 % 2 == 0 else S["sXb"]
            pl.emit("sync", lambda e, gi0=gi0, nt=nt, base=base, srcx=srcx,
                    sl=sl: e.dma_start(
                        out=xbuf[:, sl:sl + nt, :],
                        in_=srcx.rearrange("(a p) d -> p a d", p=P)[
                            :, (base + gi0 * P) // P:
                            (base + gi0 * P) // P + nt, :]),
                    waits=w, incs=[(xsem, 16)])
            Tg += nt
        srcl = s_lnxo if own else s_lnx
        obase = base
        tp_after_half = {}
        for h in range(rtok // 512):
            hs = S["sLXa"] if h == 0 else S["sLXb"]
            ts_ = S["sTPa"] if h == 0 else S["sTPb"]
            for d in range(DC):
                w = [(hs, 16 * (ri + 1))]
                if d == 0:
                    w.append((ts_, ts_.v))
                if ri > 1 and h == 0 and d == 0:
                    w.append((S["sKM"], km_before_prev))
                pl.emit("sync", lambda e, d=d, h=h, obase=obase, srcl=srcl,
                        ri=ri: e.dma_start_transpose(
                            lnxT[:, ri % 2, d, 512 * h:512 * (h + 1)],
                            srcl[obase + 512 * h:obase + 512 * (h + 1),
                                 d * P:(d + 1) * P]),
                        waits=w, incs=[(ts_, 16)])
            tp_after_half[h] = (ts_, ts_.v)
        km_before_prev = km_before
        t0g = Tg - ntile
        for gi0 in range(0, ntile, 4):
            nt = min(4, ntile - gi0)
            G4 = (t0g + gi0) // 4
            par = G4 % 2
            sl = (G4 % 2) * 4
            xsem = S["sXa"] if G4 % 2 == 0 else S["sXb"]
            glw = (xsem, 16 * (G4 // 2 + 1))
            tiles = []
            for j in range(nt):
                T = t0g + gi0 + j
                w = [glw]
                if j == 0 and T >= 8:
                    w.append((S["sLN"], T - 4))
                tiles.append((xbuf[:, sl + j, :], w))
            mneg, srs = ln_stats_group(tiles, par, False)
            hh = gi0 // 4
            hs = S["sLXa"] if hh == 0 else S["sLXb"]
            lxw_gate = lx_after.get((ri - 1, hh), 0) if ri else 0
            for j in range(nt):
                ln_norm(xbuf[:, sl + j, :], par, j, lnstage[:, gi0 + j, :],
                        [srs, (hs, lxw_gate)],
                        extra=[(S["sLN"], 1)])
        dstl = s_lnxo if own else s_lnx
        for hh in range(0, ntile, 4):
            hn = min(4, ntile - hh)
            hs = S["sLXa"] if hh == 0 else S["sLXb"]
            pl.emit("gpsimd", lambda e, base=base, hh=hh, hn=hn, dstl=dstl:
                    e.dma_start(
                        out=dstl.rearrange("(a p) d -> p a d", p=P)[
                            :, base // P + hh:base // P + hh + hn, :],
                        in_=lnstage[:, hh:hh + hn, :]),
                    waits=[(S["sLN"], Tg - ntile + hh + hn)],
                    incs=[(hs, 16)])
            lx_after[(ri, hh // 4)] = hs.v
        if own:
            groups = [("q", oc, tsub) for tsub in range(rtok // 512)
                      for oc in range(DC)]
        else:
            groups = []
            for h in range(rtok // 512):
                kg = [("k", oc, h) for oc in range(DC)]
                vg = [("v", t8, oh) for oh in range(2)
                      for t8 in range(4 * h, 4 * h + 4)]
                groups += [g for pair in zip(kg, vg) for g in pair]
        for gidx, (kind, ga, gb) in enumerate(groups):
            ghalf = gb if kind != "v" else ga // 4
            bank = (0 if kind != "v" else 2) + gidx % 2
            wts = [tp_after_half[ghalf], (S["sKE"], bank_gate[bank])]
            if ri == 0 and gidx == 0:
                wts.append((S["sP0"], W0))
                wts.append((S["sP0g"], 48))
            for dc in range(DC):
                if kind == "v":
                    bf = (lambda e, dc=dc, t8=ga, oh=gb, bank=bank, ri=ri:
                          e.matmul(pb[bank][:, 0:512],
                                   lnxT[:, ri % 2, dc, t8 * P:(t8 + 1) * P],
                                   wT_sb["v"][:, dc, oh * 512:(oh + 1) * 512],
                                   start=(dc == 0), stop=(dc == DC - 1)))
                else:
                    wsb = wT_sb["k" if kind == "k" else "q"]
                    bf = (lambda e, dc=dc, oc=ga, tsub=gb, bank=bank, wsb=wsb,
                          ri=ri:
                          e.matmul(pb[bank][:, 0:512],
                                   wsb[:, dc, oc * P:(oc + 1) * P],
                                   lnxT[:, ri % 2, dc,
                                        tsub * 512:(tsub + 1) * 512],
                                   start=(dc == 0), stop=(dc == DC - 1)))
                pl.emit("tensor", bf, waits=wts if dc == 0 else (),
                        incs=[(S["sKM"], 1)] if dc == DC - 1 else ())
            kmv = S["sKM"].v
            if kind == "k":
                ev = (lambda e, oc=ga, tsub=gb, bank=bank: e.activation(
                    kstage[:, oc, tsub * 512:(tsub + 1) * 512],
                    pb[bank][:, 0:512], AF.Copy))
            elif kind == "v":
                ev = (lambda e, t8=ga, oh=gb, bank=bank: e.activation(
                    vstage[:, t8, oh * 512:(oh + 1) * 512],
                    pb[bank][:, 0:512], AF.Copy))
            else:
                ev = (lambda e, oc=ga, tsub=gb, bank=bank, base=base:
                      e.activation(
                          qt[:, oc, base + tsub * 512:base + (tsub + 1) * 512],
                          pb[bank][:, 0:512], AF.Copy))
            evw = [(S["sKM"], kmv)]
            if not own:
                evw.append((S["sKVw"], kv_store_done))
            pl.op("scalar", ev, after=evw, extra=[(S["sKE"], 1)])
            bank_gate[bank] = S["sKE"].v
        if not own:
            kev = S["sKE"].v
            pl.emit("gpsimd", lambda e, base=base, rtok=rtok: e.dma_start(
                out=s_kT.rearrange("(a p) t -> p a t", p=P)[
                    :, :, base:base + rtok],
                in_=kstage[:, :, 0:rtok]),
                waits=[(S["sKE"], kev)], incs=[(S["sKVw"], 16)])
            pl.emit("gpsimd", lambda e, base=base, ntile=ntile: e.dma_start(
                out=s_v.rearrange("(a p) d -> p a d", p=P)[
                    :, base // P:base // P + ntile, :],
                in_=vstage[:, 0:ntile, :]),
                waits=[(S["sKE"], kev)], incs=[(S["sKVw"], 16)])
            kv_store_done = S["sKVw"].v

    pl.barrier()
    stA.close()

    stB = contextlib.ExitStack()
    xbuf2 = sb(stB, "xbuf2", [P, 4, D], F32)
    latbuf = sb(stB, "latbuf", [P, 4, D], BF16)
    t16b = sb(stB, "t16b", [P, 2, D], F32)
    xqstage = sb(stB, "xqstage", [P, 4, D], BF16)
    stat2 = sb(stB, "stat2", [P, 2, 32], F32)
    sq2 = sb(stB, "sq2", [P, 4, D], BF16)
    qc1 = sb(stB, "qc1", [P, D], F32)
    qc2 = sb(stB, "qc2", [P, D], F32)
    kgrp = sb(stB, "kgrp", [P, 2, DC, 512], BF16)
    vgrp = sb(stB, "vgrp", [P, 2, 4, D], BF16)
    ptb = sb(stB, "ptb", [P, 8, 256], BF16)
    maskb = sb(stB, "maskb", [P, 8, 256], BF16)
    rcp = sb(stB, "rcp", [P, 16], F32)
    oacc = sb(stB, "oacc", [P, 4, 2, D], F32)
    lnatst = sb(stB, "lnatst", [P, 2, D], BF16)
    wstage = sb(stB, "wstage", [P, 2, HID // 2], F32)
    wqout = sb(stB, "wqout", [P, HID // 2], BF16)
    wy1 = sb(stB, "wy1", [P, HID // 2], BF16)
    wz = sb(stB, "wz", [P, HID // 2], BF16)

    chain_mu = {}

    def emit_part1_loads(gg):
        gi0 = gg * 4
        law_need = 16 * min(2 * gg + 2, NSLOT)
        w = [(S["sLAW"], law_need)]
        if gi0 >= 4:
            w.append((S["sX2c"], gi0))
        pl.emit("gpsimd", lambda e, gi0=gi0: e.dma_start(
            out=xbuf2[:, 0:4, :],
            in_=ap_xo.rearrange("(a p) d -> p a d", p=P)[:, gi0:gi0 + 4, :]),
            waits=w, incs=[(S["sX2"], 16)])
        pl.emit("gpsimd", lambda e, gi0=gi0: e.dma_start(
            out=latbuf[:, 0:4, :],
            in_=s_lnat.rearrange("(a p) d -> p a d", p=P)[:, gi0:gi0 + 4, :]),
            waits=w, incs=[(S["sLG"], 16)])

    def emit_part1_group(gg):
        gi0 = gg * 4
        tiles = []
        for j in range(4):
            ww = [(S["sX2"], 16 * (gg + 1)), (S["sLG"], 16 * (gg + 1))]
            xmu = pl.op("vector", lambda e, j=j: e.tensor_add(
                xbuf2[:, j, :], xbuf2[:, j, :], latbuf[:, j, :]), after=ww)
            tiles.append((xbuf2[:, j, :], [xmu]))
        par = gg % 2
        mneg, srs = ln_stats_group(tiles, par, True, stt=stat2, sqt=sq2)
        for j in range(4):
            i = gi0 + j
            after = [srs]
            if i >= 2 and (i - 2) in chain_mu:
                after.append(chain_mu[i - 2])
            nrm = ln_norm(xbuf2[:, j, :], par, j, t16b[:, i % 2, :],
                          after, extra=[(S["sX2c"], 1)], stt=stat2)
            cw = [nrm]
            if i >= 4:
                cw.append((S["sXQW"], 16 * (i // 4)))
            chain_mu[i] = quant_chain(
                t16b[:, i % 2, :], xqstage[:, i % 4, :], D, cw,
                y1t=qc1, zt=qc2)
        pl.emit("gpsimd", lambda e, gi0=gi0: e.dma_start(
            out=s_xq.rearrange("(a p) d -> p a d", p=P)[
                :, gi0:gi0 + 4, :],
            in_=xqstage[:, 0:4, :]),
            waits=[chain_mu[gi0 + 3]], incs=[(S["sXQW"], 16)])

    wq_units = []
    for wi in range(W1TILES):
        for h in range(2):
            src = ap_w1T[wi * P:(wi + 1) * P,
                         h * (HID // 2):(h + 1) * (HID // 2)]
            wout = s_w1q.rearrange("a p c m -> p a c m")[
                :, 8 * h:8 * (h + 1), wi, :]
            wq_units.append((src, HID // 2, wout, 256))
    for w2i in range(W2TILES):
        src = ap_w2T[w2i * P:(w2i + 1) * P, :]
        wout = s_w2q.rearrange("a p c o -> p a c o")[:, w2i // 2, w2i % 2, :]
        wq_units.append((src, D, wout, None))
    wq_mu = {}
    wqst = {"load": 0, "chain": 0}

    def emit_wq_unit():
        while wqst["load"] < min(len(wq_units), wqst["chain"] + 2):
            u = wqst["load"]
            src, width, _, _ = wq_units[u]
            lw = [wq_mu[u - 2]] if u >= 2 else []
            pl.emit("gpsimd", lambda e, src=src, width=width, u=u:
                    e.dma_start(out=wstage[:, u % 2, 0:width], in_=src),
                    waits=lw, incs=[(S["sWL"], 16)])
            wqst["load"] += 1
        u = wqst["chain"]
        if u >= len(wq_units):
            return
        src, width, wout, m = wq_units[u]
        ch = quant_chain(wstage[:, u % 2, 0:width], wqout[:, 0:width], width,
                         after=[(S["sWL"], 16 * (u + 1)),
                                (S["sWQw"], 16 * u)],
                         pre_scale=64.0, y1t=wy1, zt=wz)
        wq_mu[u] = ch
        wsrc = wqout[:, 0:width]
        if m is not None:
            wsrc = wsrc.rearrange("p (a m) -> p a m", m=m)
        pl.emit("gpsimd", lambda e, wout=wout, wsrc=wsrc: e.dma_start(
            out=wout, in_=wsrc), waits=[ch], incs=[(S["sWQw"], 16)])
        wqst["chain"] += 1

    pstate = {"par": 0}
    lnat_nrm = {}
    lane_free = {}

    def emit_slot_epilogue(s, lane, last_ev):
        sv = pl.vec_pending
        pl.vec_pending = None
        rcp_mu = pl.op("vector", lambda e, s=s: e.reciprocal(
            rcp[:, 2 * s:2 * s + 2], dsum[:, 2 * s:2 * s + 2]),
            after=[last_ev])
        ev_mus = []
        for qs2 in range(2):
            for oh in range(2):
                mu = pl.op("scalar", lambda e, qs2=qs2, oh=oh, lane=lane,
                           s=s: e.activation(
                               oacc[:, lane, qs2, oh * 512:(oh + 1) * 512],
                               oacc[:, lane, qs2, oh * 512:(oh + 1) * 512],
                               AF.Copy, scale=rcp[:, 2 * s + qs2:
                                                  2 * s + qs2 + 1]),
                           after=[rcp_mu, last_ev])
                ev_mus.append(mu)
        par = pstate["par"]
        tiles = [(oacc[:, lane, 0, :], [ev_mus[0], ev_mus[1]]),
                 (oacc[:, lane, 1, :], [ev_mus[2], ev_mus[3]])]
        if s >= 2:
            tiles[0][1].append(lnat_nrm[s - 2])
        mneg, srs = ln_stats_group(tiles, par, False)
        nrm_mu = None
        for j in range(2):
            after = [srs, (S["sLAW"], 16 * s)] if nrm_mu is None \
                else [srs, nrm_mu]
            nrm_mu = ln_norm(oacc[:, lane, j, :], par, j,
                             lnatst[:, j, :], after)
        lnat_nrm[s] = nrm_mu
        lane_free[lane] = nrm_mu
        pstate["par"] ^= 1
        pl.emit("gpsimd", lambda e, s=s: e.dma_start(
            out=s_lnat.rearrange("(a p) d -> p a d", p=P)[
                :, 2 * s:2 * s + 2, :],
            in_=lnatst[:, 0:2, :]),
            waits=[nrm_mu], incs=[(S["sLAW"], 16)])
        pl.vec_pending = sv

    I = 0
    B = 0
    G = 0
    msk = 0
    av_pend = []
    p1_after = {1: [0], 3: [1], 5: [2], 7: [3]}
    pl.vec_pending = []
    for q2 in range(2):
        GQ = 8 if q2 == 0 else 16
        slots4 = list(range(4 * q2, 4 * q2 + 4))
        for g in range(GQ):
            gw = [(S["sKVF"], G - 1)] if G >= 2 else []
            k0 = 512 * g
            kvsem = S["sKVLa"] if G % 2 == 0 else S["sKVLb"]
            gpar = G % 2
            pl.emit("sync", lambda e, k0=k0, gpar=gpar: e.dma_start(
                out=kgrp[:, gpar, :, :],
                in_=s_kT.rearrange("(a p) t -> p a t", p=P)[
                    :, :, k0:k0 + 512]),
                waits=gw, incs=[(kvsem, 16)])
            pl.emit("sync", lambda e, k0=k0, gpar=gpar: e.dma_start(
                out=vgrp[:, gpar, :, :],
                in_=s_v.rearrange("(a p) d -> p a d", p=P)[
                    :, k0 // P:k0 // P + 4, :]),
                incs=[(kvsem, 16)])
            G += 1
            kvl_wait = (kvsem, kvsem.v)
            act = [s for s in slots4 if g < 2 * s + 2]
            for si, s in enumerate(act):
                lane = s % 4
                last_of_group = (si == len(act) - 1)
                slot_last = (g == 2 * s + 1)
                first_burst = (g == 0)
                if B % 3 == 0:
                    emit_wq_unit()
                    emit_wq_unit()
                if g == 2 * s:
                    sv = pl.vec_pending
                    pl.vec_pending = None
                    for m in range(8):
                        w = [(S["sMSK"], 8 * s)]
                        if s == 0 and m == 0:
                            w.append(tm[0])
                        pl.op("vector", lambda e, s=s, m=m: e.tensor_scalar(
                            maskb[:, m, :], iotaD[:, :], thr[:, s, m:m + 1],
                            None, ALU.is_ge), after=w,
                            extra=[(S["sMG"], 1)])
                    pl.vec_pending = sv
                iter_info = []
                for kk in range(4):
                    kb = 4 * g + kk
                    sbank = 4 + I % 2
                    for dc in range(DC):
                        w = []
                        if dc == 0:
                            w = [kvl_wait]
                            if I >= 2:
                                w.append((S["sE"], I - 1))
                        pl.emit("tensor", lambda e, dc=dc, kk=kk, s=s,
                                gpar=gpar, sbank=sbank: e.matmul(
                                    pb[sbank][:, 0:256],
                                    kgrp[:, gpar, dc, kk * P:(kk + 1) * P],
                                    qt[:, dc, s * 256:(s + 1) * 256],
                                    start=(dc == 0), stop=(dc == DC - 1)),
                                waits=w if dc == 0 else (),
                                incs=[(S["sS"], 1)] if dc == DC - 1 else ())
                    w = [(S["sS"], I + 1)]
                    if I >= 8:
                        w.append((S["sAV"], I - 7))
                    pl.op("scalar", lambda e, I=I, sbank=sbank: e.activation(
                        ptb[:, I % 8, :], pb[sbank][:, 0:256], AF.Exp,
                        scale=ISQ), after=w, extra=[(S["sE"], 1)])
                    masked = kb >= 8 * s
                    if masked:
                        m = kb - 8 * s
                        sv = pl.vec_pending
                        pl.vec_pending = None
                        pl.op("vector", lambda e, I=I, m=m: e.tensor_mul(
                            ptb[:, I % 8, :], ptb[:, I % 8, :],
                            maskb[:, m, :]),
                            after=[(S["sE"], I + 1),
                                   (S["sMG"], 8 * s + m + 1)],
                            extra=[(S["sMSK"], 1)])
                        pl.vec_pending = sv
                        msk += 1
                    first_w = [(S["sMSK"], msk)] if masked \
                        else [(S["sE"], I + 1)]
                    iter_info.append((I, kb, tuple(first_w)))
                    pl.pump(3)
                    I += 1

                def make_av_block(s=s, lane=lane, B=B,
                                  iter_info=tuple(iter_info), gpar=gpar,
                                  first_burst=first_burst,
                                  last_of_group=last_of_group):
                    def emit_block():
                        for j, (I0, kb, first_w) in enumerate(iter_info):
                            kk = kb % 4
                            fw = list(first_w)
                            if j == 0 and B >= 1:
                                fw.append((S["sAVE"], 5 * B))
                            nmm = 0
                            for qs2 in range(2):
                                for oh in range(2):
                                    avincs = []
                                    if nmm == 3 and j == 3 and last_of_group:
                                        avincs.append((S["sKVF"], 1))
                                    pl.emit("tensor", lambda e, I0=I0,
                                            qs2=qs2, oh=oh, gpar=gpar, kk=kk:
                                            e.matmul(
                                                pb[2 * qs2 + oh][:, 0:512],
                                                ptb[:, I0 % 8,
                                                    qs2 * P:(qs2 + 1) * P],
                                                vgrp[:, gpar, kk,
                                                     oh * 512:(oh + 1) * 512],
                                                start=(kk == 0),
                                                stop=(kk == 3)),
                                            waits=fw if nmm == 0 else (),
                                            incs=avincs)
                                    nmm += 1
                                pl.emit("tensor", lambda e, I0=I0, qs2=qs2,
                                        j=j: e.matmul(
                                            pb[7][:, qs2:qs2 + 1],
                                            ptb[:, I0 % 8,
                                                qs2 * P:(qs2 + 1) * P],
                                            onesc[:, 0:1],
                                            start=(j == 0 and qs2 == 0),
                                            stop=(j == 3 and qs2 == 1)),
                                        incs=[(S["sAV"], 1)]
                                        if qs2 == 1 else ())
                        sAV_t = iter_info[-1][0] + 1
                        sv = pl.vec_pending
                        pl.vec_pending = None
                        ew = [(S["sAV"], sAV_t)]
                        if first_burst and lane in lane_free:
                            ew.append(lane_free[lane])
                        last = pl.op("vector", lambda e, s=s: e.tensor_add(
                            dsum[:, 2 * s:2 * s + 2],
                            dsum[:, 2 * s:2 * s + 2], pb[7][:, 0:2]),
                            after=ew, extra=[(S["sAVE"], 1)])
                        for qs2 in range(2):
                            for oh in range(2):
                                if first_burst:
                                    last = pl.op(
                                        "vector",
                                        lambda e, qs2=qs2, oh=oh, lane=lane:
                                        e.tensor_copy(
                                            oacc[:, lane, qs2,
                                                 oh * 512:(oh + 1) * 512],
                                            pb[2 * qs2 + oh][:, 0:512]),
                                        extra=[(S["sAVE"], 1)])
                                else:
                                    last = pl.op(
                                        "vector",
                                        lambda e, qs2=qs2, oh=oh, lane=lane:
                                        e.tensor_add(
                                            oacc[:, lane, qs2,
                                                 oh * 512:(oh + 1) * 512],
                                            oacc[:, lane, qs2,
                                                 oh * 512:(oh + 1) * 512],
                                            pb[2 * qs2 + oh][:, 0:512]),
                                        extra=[(S["sAVE"], 1)])
                        pl.vec_pending = sv
                        return last
                    return emit_block
                if av_pend:
                    av_pend.pop(0)()
                av_pend.append(make_av_block())
                B += 1
                if slot_last:
                    last_ev = None
                    while av_pend:
                        last_ev = av_pend.pop(0)()
                    emit_slot_epilogue(s, lane, last_ev)
                    for gg in p1_after.get(s, []):
                        emit_part1_loads(gg)
                        emit_part1_group(gg)
    while wqst["chain"] < len(wq_units):
        emit_wq_unit()
    pl.pump_all()
    pl.vec_pending = None
    pl.barrier()
    stB.close()
    stAB.close()

    stC = contextlib.ExitStack()
    xqT = sb(stC, "xqT", [P, DC, NOWN], BF16)
    for d in range(DC):
        pl.emit("sync", lambda e, d=d: e.dma_start_transpose(
            xqT[:, d, :], s_xq[:, d * P:(d + 1) * P]),
            waits=[(S["sXQW"], 16 * (NHT // 4))],
            incs=[(S["sXQT"], 16)])
    WRING = 6
    w1ch = sb(stC, "w1ch", [P, WRING, DC, 256], BF16)
    w2ch = sb(stC, "w2ch", [P, WRING, 2, D], BF16)
    yacc = sb(stC, "yacc", [P, NT8, 2, D], F32)
    usb = sb(stC, "usb", [P, 3, 512], F32)
    rbuf = sb(stC, "rbuf", [P, 3, 512], BF16)
    ysb = sb(stC, "ysb", [P, 2, D], F32)
    outst = sb(stC, "outst", [P, 2, D], F32)
    xet = sb(stC, "xet", [P, 2, D], F32)
    late = sb(stC, "late", [P, 2, D], F32)
    qc1c = sb(stC, "qc1c", [P, 512], F32)
    qc2c = sb(stC, "qc2c", [P, 512], F32)

    wlsems = [S["sWL2a"], S["sWL2b"], S["sWL2c"], S["sWL2d"],
              S["sWL2e"], S["sWL2f"]]
    for hc in range(NHC):
        lw = []
        if hc >= WRING:
            lw = [(S["sY"], 64 * ((hc - WRING) // 4 + 1))]
        wsem = wlsems[hc % WRING]
        pl.emit("sync", lambda e, hc=hc: e.dma_start(
            out=w1ch[:, hc % WRING, :, :],
            in_=s_w1q.rearrange("a p c m -> p a c m")[:, hc, :, :]),
            waits=lw, incs=[(wsem, 16)])
        pl.emit("sync", lambda e, hc=hc: e.dma_start(
            out=w2ch[:, hc % WRING, :, :],
            in_=s_w2q.rearrange("a p c o -> p a c o")[:, hc, :, :]),
            incs=[(wsem, 16)])

    idx = 0
    MM2_SKEW = 2
    mm2_pend = []

    def emit_mm2(i2, hc, pos, t8q):
        for h2 in range(2):
            step = 2 * pos + h2
            mw = []
            if h2 == 0:
                mw = [(S["sRQ"], i2 + 1)]
                if step == 0:
                    mw.append((S["sYB"], 4 * t8q))
            nmm = 0
            for tsub in range(2):
                for oh in range(2):
                    pl.emit("tensor", lambda e, i2=i2, tsub=tsub, oh=oh,
                            hc=hc, h2=h2, step=step: e.matmul(
                                pb[2 * tsub + oh][:, 0:512],
                                rbuf[:, i2 % 3,
                                     h2 * 256 + tsub * P:
                                     h2 * 256 + (tsub + 1) * P],
                                w2ch[:, hc % WRING, h2,
                                     oh * 512:(oh + 1) * 512],
                                start=(step == 0), stop=(step == 7)),
                            waits=mw if nmm == 0 else (),
                            incs=[(S["sY"], 1)] if nmm == 3 else ())
                    nmm += 1

    ep_state = {"lny_mu_prev": None, "out_mu_prev": None}
    lny_nrm = {}
    out_add_mu = {}

    def emit_epilogue(t8):
        ew = []
        if t8 >= 1:
            ew = [out_add_mu[t8 - 1]]
        pl.emit("sync", lambda e, t8=t8: e.dma_start(
            out=xet[:, 0:2, :],
            in_=ap_xo.rearrange("(a p) d -> p a d", p=P)[
                :, 2 * t8:2 * t8 + 2, :]),
            waits=ew, incs=[(S["sXE"], 16)])
        pl.emit("gpsimd", lambda e, t8=t8: e.dma_start(
            out=late[:, 0:2, :],
            in_=s_lnat.rearrange("(a p) d -> p a d", p=P)[
                :, 2 * t8:2 * t8 + 2, :]),
            waits=ew, incs=[(S["sLE"], 16)])
        yb_mu = None
        for j in range(2):
            w = [(S["sYB"], 4 * (24 + t8 + 1))]
            if ep_state["lny_mu_prev"] is not None and yb_mu is None:
                w.append(ep_state["lny_mu_prev"])
            yb_mu = pl.op("vector", lambda e, j=j, t8=t8:
                          e.scalar_tensor_tensor(
                              ysb[:, j, :], yacc[:, t8, j, :], 1.0 / 1024.0,
                              b2tile[:, :], ALU.mult, ALU.add),
                          after=w)
        tiles = [(ysb[:, j, :], [yb_mu]) for j in range(2)]
        if t8 >= 2:
            tiles[0][1].append(lny_nrm[t8 - 2])
        par = t8 % 2
        mneg, srs = ln_stats_group(tiles, par, False)
        x2a_mu = None
        for j in range(2):
            aw = [(S["sXE"], 16 * (t8 + 1)), (S["sLE"], 16 * (t8 + 1))]
            if x2a_mu is not None:
                aw = [x2a_mu]
            x2a_mu = pl.op("vector", lambda e, j=j: e.tensor_add(
                xet[:, j, :], xet[:, j, :], late[:, j, :]), after=aw)
        out_mu = None
        lny_mu_prev = None
        for j in range(2):
            after = [srs] if lny_mu_prev is None else [srs, lny_mu_prev]
            if j == 0 and ep_state["out_mu_prev"] is not None:
                after.append(ep_state["out_mu_prev"])
            nrm = ln_norm(ysb[:, j, :], par, j, ysb[:, j, :], after)
            lny_mu_prev = nrm
            ow = [nrm, (S["sOUT"], 16 * t8)]
            if j == 0:
                ow.append(x2a_mu)
            out_mu = pl.op("vector", lambda e, t8=t8, j=j: e.tensor_add(
                outst[:, j, :], xet[:, j, :], ysb[:, j, :]),
                after=ow)
        lny_nrm[t8] = lny_mu_prev
        ep_state["lny_mu_prev"] = lny_mu_prev
        ep_state["out_mu_prev"] = out_mu
        out_add_mu[t8] = out_mu
        pl.emit("gpsimd", lambda e, t8=t8: e.dma_start(
            out=ap_out.rearrange("(a p) d -> p a d", p=P)[
                :, 2 * t8:2 * t8 + 2, :],
            in_=outst[:, 0:2, :]),
            waits=[out_mu], incs=[(S["sOUT"], 16)])

    for q in range(4):
        for t8 in range(NT8):
            for hcq in range(4):
                hc = 4 * q + hcq
                i2 = idx
                for h2 in range(2):
                    hbank = 4 + 2 * (i2 % 2) + h2
                    w = [(wlsems[hc % WRING], 32 * (hc // WRING + 1)),
                         (S["sXQT"], 16 * DC)]
                    if 2 * i2 + h2 >= 4:
                        w.append((S["sRL"], 2 * i2 + h2 - 3))
                    for dc in range(DC):
                        pl.emit("tensor", lambda e, dc=dc, h2=h2, t8=t8,
                                hc=hc, hbank=hbank: e.matmul(
                                    pb[hbank][:, 0:256],
                                    w1ch[:, hc % WRING, dc,
                                         h2 * P:(h2 + 1) * P],
                                    xqT[:, dc, t8 * 256:(t8 + 1) * 256],
                                    start=(dc == 0), stop=(dc == DC - 1)),
                                waits=w if dc == 0 else (),
                                incs=[(S["sHM"], 1)] if dc == DC - 1 else ())
                    rw = [(S["sHM"], 2 * i2 + h2 + 1)]
                    if i2 >= 3:
                        rw.append((S["sRQ"], i2 - 2))
                    hcol = 2 * hc + h2
                    pl.op("scalar", lambda e, i2=i2, h2=h2, hcol=hcol,
                          hbank=hbank:
                          e.activation(usb[:, i2 % 3,
                                           h2 * 256:(h2 + 1) * 256],
                                       pb[hbank][:, 0:256],
                                       AF.Relu, bias=b1q16[:, hcol:hcol + 1],
                                       scale=1.0 / 64.0),
                          after=rw, extra=[(S["sRL"], 1)])
                cw = [(S["sRL"], 2 * i2 + 2)]
                if i2 >= 3:
                    cw.append((S["sY"], 2 * i2 - 4))
                quant_chain(usb[:, i2 % 3, :], rbuf[:, i2 % 3, :], 512,
                            cw, extra=[(S["sRQ"], 1)], y1t=qc1c, zt=qc2c)
                mm2_pend.append((i2, hc, i2 % 4, q * 8 + t8))
                if len(mm2_pend) > MM2_SKEW:
                    emit_mm2(*mm2_pend.pop(0))
                idx += 1
            for tsub in range(2):
                for oh in range(2):
                    w = [(S["sY"], 64 * q + 8 * (t8 + 1))]
                    if q >= 1:
                        w.append((S["sYB"], 4 * ((q - 1) * 8 + t8 + 1)))
                    if q == 0:
                        pl.op("scalar", lambda e, tsub=tsub, oh=oh, t8=t8:
                              e.activation(
                                  yacc[:, t8, tsub, oh * 512:(oh + 1) * 512],
                                  pb[2 * tsub + oh][:, 0:512], AF.Copy),
                              after=w, extra=[(S["sYB"], 1)])
                    else:
                        pl.op("vector", lambda e, tsub=tsub, oh=oh, t8=t8:
                              e.tensor_add(
                                  yacc[:, t8, tsub, oh * 512:(oh + 1) * 512],
                                  yacc[:, t8, tsub, oh * 512:(oh + 1) * 512],
                                  pb[2 * tsub + oh][:, 0:512]),
                              after=w, extra=[(S["sYB"], 1)])
            if q == 3:
                emit_epilogue(t8)
    while mm2_pend:
        emit_mm2(*mm2_pend.pop(0))

    pl.waits_only("gpsimd", [(S["sOUT"], 16 * NT8)])
    stC.close()

    with es, nc.Block() as block:
        for ename in ENGN:
            dec = getattr(block, ename)

            def body(eng, fns=pl.fns[ename]):
                for fn in fns:
                    fn(eng)
            dec(body)
    return nc


def own_tiles(c, seq):
    nt = seq // 256
    return [j for j in range(nt) if j % 8 in (c, 7 - c)]


def make_inmaps(seq, x_full, wq, wk, wv, w1, b1, w2, b2):
    f32 = np.float32
    shared = dict(
        wqT=np.ascontiguousarray(wq.T, dtype=f32),
        wkT=np.ascontiguousarray(wk.T, dtype=f32),
        wvT=np.ascontiguousarray(wv.T, dtype=f32),
        w1T=np.ascontiguousarray(w1.T, dtype=f32),
        w2T=np.ascontiguousarray(w2.T, dtype=f32),
        b1t=np.ascontiguousarray(b1.reshape(HID // P, P).T, dtype=f32),
        b2bc=np.ascontiguousarray(np.broadcast_to(b2[None, :], (P, D)),
                                  dtype=f32),
    )
    in_maps = []
    tiles_by_core = []
    for core in range(8):
        b, c = divmod(core, 4)
        tiles = own_tiles(c, seq)
        xb = np.ascontiguousarray(x_full[b], dtype=f32)
        x_own = np.ascontiguousarray(
            np.concatenate([xb[j * 256:(j + 1) * 256] for j in tiles], 0))
        qpos = np.broadcast_to(
            np.array([j * 256 for j in tiles], dtype=f32)[None, :],
            (P, len(tiles)))
        in_maps.append(dict(x=xb, x_own=x_own,
                            qpos=np.ascontiguousarray(qpos), **shared))
        tiles_by_core.append(tiles)
    return in_maps, tiles_by_core


def assemble(seq, results, tiles_by_core):
    B = 2
    out = np.zeros((B, seq, D), dtype=np.float32)
    for core in range(8):
        b = core // 4
        r = np.asarray(results[core]["out"])
        for s, j in enumerate(tiles_by_core[core]):
            out[b, j * 256:(j + 1) * 256] = r[s * 256:(s + 1) * 256]
    return out

from concourse.bass_utils import run_bass_kernel_spmd


def kernel(L0_attn_inputs, L0_attn_Wq, L0_attn_Wk, L0_attn_Wv,
           L0_ffn_w1, L0_ffn_b1, L0_ffn_w2, L0_ffn_b2):
    seq = 8192
    x = np.asarray(L0_attn_inputs, dtype=np.float32)
    in_maps, tiles = make_inmaps(seq, x,
                                 np.asarray(L0_attn_Wq, dtype=np.float32),
                                 np.asarray(L0_attn_Wk, dtype=np.float32),
                                 np.asarray(L0_attn_Wv, dtype=np.float32),
                                 np.asarray(L0_ffn_w1, dtype=np.float32),
                                 np.asarray(L0_ffn_b1, dtype=np.float32),
                                 np.asarray(L0_ffn_w2, dtype=np.float32),
                                 np.asarray(L0_ffn_b2, dtype=np.float32))
    nc = build(seq)
    res = run_bass_kernel_spmd(nc, in_maps, core_ids=list(range(8)))
    return assemble(seq, res.results, tiles)


# version 19
# speedup vs baseline: 2.6418x; 1.0544x over previous
import contextlib
import numpy as np

import concourse.bass as bass
import concourse.mybir as mybir

F32 = mybir.dt.float32
BF16 = mybir.dt.bfloat16
AF = mybir.ActivationFunctionType
ALU = mybir.AluOpType
AX = mybir.AxisListType

P = 128
D = 1024
HID = 4096
DC = D // 128
C32 = 1.5 * 2.0 ** 23
C52 = 1.5 * 2.0 ** 52
LN16 = float(np.log(16.0))
EPS = 1e-5
ISQ = 1.0 / 32.0
ENGN = ("sync", "gpsimd", "vector", "scalar", "tensor")


class _S:
    __slots__ = ("h", "v")

    def __init__(self, h):
        self.h = h
        self.v = 0


class Plan:
    def __init__(self, nc, es):
        self.nc = nc
        self.es = es
        self.fns = {e: [] for e in ENGN}
        self.sems = {}
        self.mu = {}
        for e in ENGN:
            self.mu[e] = self.sem("mu_" + e)

    def sem(self, name):
        s = _S(self.es.enter_context(self.nc.semaphore(name)))
        self.sems[name] = s
        return s

    vec_pending = None

    def emit(self, engine, build, waits=(), incs=()):
        for s, n in incs:
            s.v += n
        wl = [(s.h, int(v)) for s, v in waits if v > 0]
        il = [(s.h, n) for s, n in incs]

        def fn(eng, build=build, wl=wl, il=il):
            for h, v in wl:
                eng.wait_ge(h, v)
            instr = build(eng)
            for h, n in il:
                instr.then_inc(h, n)
        if engine == "vector" and self.vec_pending is not None:
            self.vec_pending.append(fn)
        else:
            self.fns[engine].append(fn)

    def pump(self, n):
        while n > 0 and self.vec_pending:
            self.fns["vector"].append(self.vec_pending.pop(0))
            n -= 1

    def pump_all(self):
        if self.vec_pending:
            self.fns["vector"] += self.vec_pending
            self.vec_pending = []

    def op(self, engine, build, after=(), extra=()):
        extra = list(extra)
        if extra:
            assert len(extra) == 1
            self.emit(engine, build, waits=after, incs=extra)
            return (extra[0][0], extra[0][0].v)
        m = self.mu[engine]
        self.emit(engine, build, waits=after, incs=[(m, 1)])
        return (m, m.v)

    def waits_only(self, engine, waits):
        wl = [(s.h, int(v)) for s, v in waits if v > 0]

        def fn(eng, wl=wl):
            for h, v in wl:
                eng.wait_ge(h, v)
        self.fns[engine].append(fn)

    def snapshot(self):
        return {k: s.v for k, s in self.sems.items()}

    def barrier(self):
        cur = self.snapshot()
        waits = [(self.sems[k], v) for k, v in cur.items() if v > 0]
        for e in ENGN:
            self.waits_only(e, waits)


def build(seq=8192):
    assert seq % 2048 == 0
    NSLOT = seq // 1024
    NOWN = NSLOT * 256
    NRNG = seq // 1024
    NORNG = max(1, NOWN // 1024)
    ORNG = NOWN // NORNG
    NHT = NOWN // P
    NT8 = NOWN // 256
    L = [8 * s + 8 for s in range(NSLOT)]
    W1TILES = DC
    W2TILES = HID // P
    NWT = W1TILES + W2TILES
    NHC = HID // 256

    nc = bass.Bass("TRN2", target_bir_lowering=False)

    ap_x = nc.dram_tensor("x", [seq, D], F32, kind="ExternalInput").ap()
    ap_xo = nc.dram_tensor("x_own", [NOWN, D], F32, kind="ExternalInput").ap()
    ap_qpos = nc.dram_tensor("qpos", [P, NSLOT], F32, kind="ExternalInput").ap()
    ap_wqT = nc.dram_tensor("wqT", [D, D], F32, kind="ExternalInput").ap()
    ap_wkT = nc.dram_tensor("wkT", [D, D], F32, kind="ExternalInput").ap()
    ap_wvT = nc.dram_tensor("wvT", [D, D], F32, kind="ExternalInput").ap()
    ap_w1T = nc.dram_tensor("w1T", [D, HID], F32, kind="ExternalInput").ap()
    ap_w2T = nc.dram_tensor("w2T", [HID, D], F32, kind="ExternalInput").ap()
    ap_b1t = nc.dram_tensor("b1t", [P, HID // P], F32, kind="ExternalInput").ap()
    ap_b2bc = nc.dram_tensor("b2bc", [P, D], F32, kind="ExternalInput").ap()
    s_lnx = nc.dram_tensor("s_lnx", [seq, D], BF16).ap()
    s_lnxo = nc.dram_tensor("s_lnxo", [NOWN, D], BF16).ap()
    s_kT = nc.dram_tensor("s_kT", [D, seq], BF16).ap()
    s_v = nc.dram_tensor("s_v", [seq, D], BF16).ap()
    s_lnat = nc.dram_tensor("s_lnat", [NOWN, D], BF16).ap()
    s_w1q = nc.dram_tensor("s_w1q", [HID // 256, P, DC, 256], BF16).ap()
    s_w2q = nc.dram_tensor("s_w2q", [HID // 256, P, 2, D], BF16).ap()
    s_xq = nc.dram_tensor("s_xq", [NOWN, D], BF16).ap()
    ap_out = nc.dram_tensor("out", [NOWN, D], F32, kind="ExternalOutput").ap()

    es = contextlib.ExitStack()
    pl = Plan(nc, es)

    def sb(st, name, shape, dtype=F32):
        return st.enter_context(nc.sbuf_tensor(name, shape, dtype))

    pb = [es.enter_context(nc.psum_tensor(f"pb{i}", [P, 512], F32))
          for i in range(8)]

    iotaD = sb(es, "iotaD", [P, 256], F32)
    iota64 = sb(es, "iota64", [P, NSLOT * 8], F32)
    thr = sb(es, "thr", [P, NSLOT, 8], F32)
    alpha = sb(es, "alpha", [P, 1], F32)
    betat = sb(es, "betat", [P, 1], F32)
    dsum = sb(es, "dsum", [P, 16], F32)
    epst = sb(es, "epst", [P, 1], F32)
    ln16t = sb(es, "ln16t", [P, 1], F32)
    onesc = sb(es, "onesc", [P, 1], BF16)
    qposb = sb(es, "qposb", [P, NSLOT], F32)
    b1q16 = sb(es, "b1q16", [P, HID // P], F32)
    b1raw = sb(es, "b1raw", [P, HID // P], F32)
    b2tile = sb(es, "b2tile", [P, D], F32)
    stat = sb(es, "stat", [P, 2, 32], F32)
    sq_scr = sb(es, "sq_scr", [P, 4, D], BF16)

    S = {n: pl.sem(n) for n in
         ["sP0", "sP0g", "sXa", "sXb", "sLN", "sLXa", "sLXb", "sTPa", "sTPb", "sKM", "sKE", "sKVw",
          "sKVLa", "sKVLb", "sKVF", "sS", "sE", "sMG", "sMSK", "sAV",
          "sAVE", "sOE", "sLAW", "sWL", "sWQw",
          "sX2", "sLG", "sX2c", "sXQW", "sXQT", "sWL2a", "sWL2b", "sWL2c",
          "sWL2d", "sWL2e", "sWL2f", "sHM",
          "sRL", "sRQ", "sY", "sYB", "sOUT", "sXE", "sLE"]}

    def ln_stats_group(xtiles, par, rs_bias, stt=None, sqt=None):
        nt = len(xtiles)
        st = (stt if stt is not None else stat)[:, par, :]
        sqb = sqt if sqt is not None else sq_scr
        red_mu = None
        for j, (xap, w) in enumerate(xtiles):
            red_mu = pl.op("vector", lambda e, xap=xap, j=j: e.reduce_sum(
                st[:, j:j + 1], xap, axis=AX.X), after=w)
        sq_mu = None
        for j, (xap, w) in enumerate(xtiles):
            sq_mu = pl.op("scalar", lambda e, xap=xap, j=j: e.activation(
                sqb[:, j, :], xap, AF.Square,
                accum_out=st[:, 4 + j:5 + j]), after=w)
        m1 = pl.op("vector", lambda e, nt=nt, st=st: e.tensor_scalar(
            st[:, 8:8 + nt], st[:, 0:nt], -1.0 / D, None, ALU.mult),
            after=[red_mu])
        m2 = pl.op("vector", lambda e, nt=nt, st=st: e.tensor_scalar(
            st[:, 12:12 + nt], st[:, 4:4 + nt], 1.0 / D, None, ALU.mult),
            after=[sq_mu, m1])
        m3 = pl.op("vector", lambda e, nt=nt, st=st: e.tensor_mul(
            st[:, 16:16 + nt], st[:, 8:8 + nt], st[:, 8:8 + nt]),
            after=[m2])
        m4 = pl.op("vector", lambda e, nt=nt, st=st: e.tensor_sub(
            st[:, 20:20 + nt], st[:, 12:12 + nt], st[:, 16:16 + nt]),
            after=[m3])
        s1 = pl.op("scalar", lambda e, nt=nt, st=st: e.activation(
            st[:, 24:24 + nt], st[:, 20:20 + nt], AF.Ln, bias=epst[:, 0:1]),
            after=[m4])
        rb = ln16t[:, 0:1] if rs_bias else 0.0
        s2 = pl.op("scalar", lambda e, nt=nt, st=st, rb=rb: e.activation(
            st[:, 28:28 + nt], st[:, 24:24 + nt], AF.Exp, bias=rb,
            scale=-0.5), after=[s1])
        return m1, s2

    def ln_norm(xap, par, j, out_ap, after, extra=(), stt=None):
        st = (stt if stt is not None else stat)[:, par, :]
        return pl.op("vector", lambda e, st=st, j=j: e.tensor_scalar(
            out_ap, xap, st[:, 8 + j:9 + j], st[:, 28 + j:29 + j],
            ALU.add, ALU.mult), after=after, extra=extra)

    def quant_chain(in_ap, out_ap, width, after, pre_scale=None,
                    post_scale=None, y1t=None, zt=None, extra=(),
                    eng="vector"):
        y1 = y1t[:, 0:width]
        z = zt[:, 0:width]
        if pre_scale is None:
            c = pl.op(eng, lambda e: e.tensor_scalar(
                y1, in_ap, C32, C32, ALU.add, ALU.subtract), after=after)
        else:
            c = pl.op(eng, lambda e: e.tensor_scalar(
                in_ap, in_ap, pre_scale, None, ALU.mult), after=after)
            c = pl.op(eng, lambda e: e.tensor_scalar(
                y1, in_ap, C32, C32, ALU.add, ALU.subtract), after=[c])
        c = pl.op(eng, lambda e: e.tensor_scalar(
            z, y1, C52, C52, ALU.add, ALU.subtract), after=[c])
        if eng == "gpsimd":
            c = pl.op(eng, lambda e: e.tensor_scalar(
                y1, y1, betat[:, 0:1], None, ALU.mult), after=[c])
            c = pl.op(eng, lambda e: e.tensor_add(z, z, y1), after=[c])
        else:
            c = pl.op(eng, lambda e: e.scalar_tensor_tensor(
                z, y1, betat[:, 0:1], z, ALU.mult, ALU.add), after=[c])
        if post_scale is None:
            c = pl.op(eng, lambda e: e.tensor_scalar(
                out_ap, z, -128.0, 127.0, ALU.max, ALU.min), after=[c],
                extra=extra)
        else:
            c = pl.op(eng, lambda e: e.tensor_scalar(
                z, z, -128.0, 127.0, ALU.max, ALU.min), after=[c])
            c = pl.op(eng, lambda e: e.tensor_scalar(
                out_ap, z, post_scale, None, ALU.mult), after=[c],
                extra=extra)
        return c

    pl.emit("sync", lambda e: e.dma_start(out=qposb[:], in_=ap_qpos[:, :]),
            incs=[(S["sP0"], 16)])
    pl.emit("sync", lambda e: e.dma_start(out=b1raw[:], in_=ap_b1t[:, :]),
            incs=[(S["sP0"], 16)])
    pl.emit("sync", lambda e: e.dma_start(out=b2tile[:], in_=ap_b2bc[:, :]),
            incs=[(S["sP0"], 16)])
    gi1 = pl.op("gpsimd", lambda e: e.iota(
        iotaD[:, :], [[1, 256]], channel_multiplier=-1,
        allow_small_or_imprecise_dtypes=True))
    gi2 = pl.op("gpsimd", lambda e: e.iota(
        iota64[:, :], [[128, NSLOT * 8]], channel_multiplier=0,
        allow_small_or_imprecise_dtypes=True), after=[gi1])
    a0 = pl.op("vector", lambda e: e.memset(alpha[:], 2.0 ** -30))
    a1 = pl.op("vector", lambda e: e.tensor_scalar(
        alpha[:], alpha[:], 1.0, 1.0, ALU.add, ALU.subtract), after=[a0])
    a2 = pl.op("vector", lambda e: e.tensor_scalar(
        alpha[:], alpha[:], 2.0 ** 30, None, ALU.mult), after=[a1])
    a3 = pl.op("vector", lambda e: e.tensor_scalar(
        betat[:], alpha[:], -1.0, 1.0, ALU.mult, ALU.add), after=[a2])
    a3 = pl.op("vector", lambda e: e.memset(dsum[:], 0.0), after=[a3])
    e0 = pl.op("vector", lambda e: e.memset(epst[:], EPS), after=[a3])
    e0 = pl.op("vector", lambda e: e.memset(ln16t[:], LN16), after=[e0])
    o0 = pl.op("vector", lambda e: e.memset(onesc[:], 1.0), after=[e0])
    tm = [o0, gi2, (S["sP0"], 48)]
    for s in range(NSLOT):
        tm = [pl.op("vector", lambda e, s=s: e.tensor_scalar(
            thr[:, s, :], iota64[:, 8 * s:8 * s + 8], qposb[:, s:s + 1],
            None, ALU.subtract), after=tm)]
    stAB = contextlib.ExitStack()
    qt = sb(stAB, "qt", [P, DC, NOWN], BF16)
    stA = contextlib.ExitStack()
    xbuf = sb(stA, "xbuf", [P, 8, D], F32)
    qcb = quant_chain(b1raw[:, :], b1q16[:, :], HID // P,
                      after=tm, pre_scale=64.0, post_scale=0.25,
                      y1t=xbuf[:, 0, :], zt=xbuf[:, 1, :])
    qcb = quant_chain(b2tile[:, :], b2tile[:, :], D,
                      after=[qcb], pre_scale=64.0, post_scale=1.0 / 64.0,
                      y1t=xbuf[:, 0, :], zt=xbuf[:, 1, :])
    lnstage = sb(stA, "lnstage", [P, 8, D], BF16)
    lnxT = sb(stA, "lnxT", [P, 2, DC, 1024], BF16)
    kstage = sb(stA, "kstage", [P, DC, 1024], BF16)
    vstage = sb(stA, "vstage", [P, 8, D], BF16)
    wT_sb = {}
    for nm, srcw in (("q", ap_wqT), ("k", ap_wkT), ("v", ap_wvT)):
        wT_sb[nm] = sb(stA, f"w{nm}T_sb", [P, DC, D], BF16)
        pl.emit("gpsimd", lambda e, dst=wT_sb[nm], srcw=srcw: e.dma_start(
            out=dst[:, :, :], in_=srcw.rearrange("(a p) o -> p a o", p=P)),
            incs=[(S["sP0g"], 16)])
    W0 = S["sP0"].v

    PA_RT = ([(r * 1024, 1024, False) for r in range(NRNG)]
             + [(r * ORNG, ORNG, True) for r in range(NORNG)])

    Tg = 0
    bank_gate = {0: 0, 1: 0, 2: 0, 3: 0}
    kv_store_done = 0
    km_before_prev = 0
    lx_after = {}
    for ri, (base, rtok, own) in enumerate(PA_RT):
        ntile = rtok // P
        srcx = ap_xo if own else ap_x
        km_before = S["sKM"].v
        for gi0 in range(0, ntile, 4):
            nt = min(4, ntile - gi0)
            G4 = Tg // 4
            sl = (G4 % 2) * 4
            w = [(S["sLN"], Tg - 4)] if Tg >= 8 else []
            if Tg == 0:
                w = [qcb]
            xsem = S["sXa"] if G4 % 2 == 0 else S["sXb"]
            pl.emit("sync", lambda e, gi0=gi0, nt=nt, base=base, srcx=srcx,
                    sl=sl: e.dma_start(
                        out=xbuf[:, sl:sl + nt, :],
                        in_=srcx.rearrange("(a p) d -> p a d", p=P)[
                            :, (base + gi0 * P) // P:
                            (base + gi0 * P) // P + nt, :]),
                    waits=w, incs=[(xsem, 16)])
            Tg += nt
        srcl = s_lnxo if own else s_lnx
        obase = base
        tp_after_half = {}
        for h in range(rtok // 512):
            hs = S["sLXa"] if h == 0 else S["sLXb"]
            ts_ = S["sTPa"] if h == 0 else S["sTPb"]
            for d in range(DC):
                w = [(hs, 16 * (ri + 1))]
                if d == 0:
                    w.append((ts_, ts_.v))
                if ri > 1 and h == 0 and d == 0:
                    w.append((S["sKM"], km_before_prev))
                pl.emit("sync", lambda e, d=d, h=h, obase=obase, srcl=srcl,
                        ri=ri: e.dma_start_transpose(
                            lnxT[:, ri % 2, d, 512 * h:512 * (h + 1)],
                            srcl[obase + 512 * h:obase + 512 * (h + 1),
                                 d * P:(d + 1) * P]),
                        waits=w, incs=[(ts_, 16)])
            tp_after_half[h] = (ts_, ts_.v)
        km_before_prev = km_before
        t0g = Tg - ntile
        for gi0 in range(0, ntile, 4):
            nt = min(4, ntile - gi0)
            G4 = (t0g + gi0) // 4
            par = G4 % 2
            sl = (G4 % 2) * 4
            xsem = S["sXa"] if G4 % 2 == 0 else S["sXb"]
            glw = (xsem, 16 * (G4 // 2 + 1))
            tiles = []
            for j in range(nt):
                T = t0g + gi0 + j
                w = [glw]
                if j == 0 and T >= 8:
                    w.append((S["sLN"], T - 4))
                tiles.append((xbuf[:, sl + j, :], w))
            mneg, srs = ln_stats_group(tiles, par, False)
            hh = gi0 // 4
            hs = S["sLXa"] if hh == 0 else S["sLXb"]
            lxw_gate = lx_after.get((ri - 1, hh), 0) if ri else 0
            for j in range(nt):
                ln_norm(xbuf[:, sl + j, :], par, j, lnstage[:, gi0 + j, :],
                        [srs, (hs, lxw_gate)],
                        extra=[(S["sLN"], 1)])
        dstl = s_lnxo if own else s_lnx
        for hh in range(0, ntile, 4):
            hn = min(4, ntile - hh)
            hs = S["sLXa"] if hh == 0 else S["sLXb"]
            pl.emit("gpsimd", lambda e, base=base, hh=hh, hn=hn, dstl=dstl:
                    e.dma_start(
                        out=dstl.rearrange("(a p) d -> p a d", p=P)[
                            :, base // P + hh:base // P + hh + hn, :],
                        in_=lnstage[:, hh:hh + hn, :]),
                    waits=[(S["sLN"], Tg - ntile + hh + hn)],
                    incs=[(hs, 16)])
            lx_after[(ri, hh // 4)] = hs.v
        if own:
            groups = [("q", oc, tsub) for tsub in range(rtok // 512)
                      for oc in range(DC)]
        else:
            groups = []
            for h in range(rtok // 512):
                kg = [("k", oc, h) for oc in range(DC)]
                vg = [("v", t8, oh) for oh in range(2)
                      for t8 in range(4 * h, 4 * h + 4)]
                groups += [g for pair in zip(kg, vg) for g in pair]
        for gidx, (kind, ga, gb) in enumerate(groups):
            ghalf = gb if kind != "v" else ga // 4
            bank = (0 if kind != "v" else 2) + gidx % 2
            wts = [tp_after_half[ghalf], (S["sKE"], bank_gate[bank])]
            if ri == 0 and gidx == 0:
                wts.append((S["sP0"], W0))
                wts.append((S["sP0g"], 48))
            for dc in range(DC):
                if kind == "v":
                    bf = (lambda e, dc=dc, t8=ga, oh=gb, bank=bank, ri=ri:
                          e.matmul(pb[bank][:, 0:512],
                                   lnxT[:, ri % 2, dc, t8 * P:(t8 + 1) * P],
                                   wT_sb["v"][:, dc, oh * 512:(oh + 1) * 512],
                                   start=(dc == 0), stop=(dc == DC - 1)))
                else:
                    wsb = wT_sb["k" if kind == "k" else "q"]
                    bf = (lambda e, dc=dc, oc=ga, tsub=gb, bank=bank, wsb=wsb,
                          ri=ri:
                          e.matmul(pb[bank][:, 0:512],
                                   wsb[:, dc, oc * P:(oc + 1) * P],
                                   lnxT[:, ri % 2, dc,
                                        tsub * 512:(tsub + 1) * 512],
                                   start=(dc == 0), stop=(dc == DC - 1)))
                pl.emit("tensor", bf, waits=wts if dc == 0 else (),
                        incs=[(S["sKM"], 1)] if dc == DC - 1 else ())
            kmv = S["sKM"].v
            if kind == "k":
                ev = (lambda e, oc=ga, tsub=gb, bank=bank: e.activation(
                    kstage[:, oc, tsub * 512:(tsub + 1) * 512],
                    pb[bank][:, 0:512], AF.Copy))
            elif kind == "v":
                ev = (lambda e, t8=ga, oh=gb, bank=bank: e.activation(
                    vstage[:, t8, oh * 512:(oh + 1) * 512],
                    pb[bank][:, 0:512], AF.Copy))
            else:
                ev = (lambda e, oc=ga, tsub=gb, bank=bank, base=base:
                      e.activation(
                          qt[:, oc, base + tsub * 512:base + (tsub + 1) * 512],
                          pb[bank][:, 0:512], AF.Copy))
            evw = [(S["sKM"], kmv)]
            if not own:
                evw.append((S["sKVw"], kv_store_done))
            pl.op("scalar", ev, after=evw, extra=[(S["sKE"], 1)])
            bank_gate[bank] = S["sKE"].v
        if not own:
            kev = S["sKE"].v
            pl.emit("gpsimd", lambda e, base=base, rtok=rtok: e.dma_start(
                out=s_kT.rearrange("(a p) t -> p a t", p=P)[
                    :, :, base:base + rtok],
                in_=kstage[:, :, 0:rtok]),
                waits=[(S["sKE"], kev)], incs=[(S["sKVw"], 16)])
            pl.emit("gpsimd", lambda e, base=base, ntile=ntile: e.dma_start(
                out=s_v.rearrange("(a p) d -> p a d", p=P)[
                    :, base // P:base // P + ntile, :],
                in_=vstage[:, 0:ntile, :]),
                waits=[(S["sKE"], kev)], incs=[(S["sKVw"], 16)])
            kv_store_done = S["sKVw"].v

    pl.barrier()
    stA.close()

    stB = contextlib.ExitStack()
    xbuf2 = sb(stB, "xbuf2", [P, 4, D], F32)
    latbuf = sb(stB, "latbuf", [P, 4, D], BF16)
    t16b = sb(stB, "t16b", [P, 2, D], F32)
    xqstage = sb(stB, "xqstage", [P, 4, D], BF16)
    stat2 = sb(stB, "stat2", [P, 2, 32], F32)
    sq2 = sb(stB, "sq2", [P, 4, D], BF16)
    qc1 = sb(stB, "qc1", [P, D], F32)
    qc2 = sb(stB, "qc2", [P, D], F32)
    kgrp = sb(stB, "kgrp", [P, 2, DC, 512], BF16)
    vgrp = sb(stB, "vgrp", [P, 2, 4, D], BF16)
    ptb = sb(stB, "ptb", [P, 8, 256], BF16)
    maskb = sb(stB, "maskb", [P, 8, 256], BF16)
    rcp = sb(stB, "rcp", [P, 16], F32)
    oacc = sb(stB, "oacc", [P, 4, 2, D], F32)
    lnatst = sb(stB, "lnatst", [P, 2, D], BF16)
    wstage = sb(stB, "wstage", [P, 2, D], F32)
    wqout = sb(stB, "wqout", [P, D], BF16)
    wy1 = sb(stB, "wy1", [P, D], BF16)
    wz = sb(stB, "wz", [P, D], BF16)

    chain_mu = {}

    def emit_part1_loads(gg):
        gi0 = gg * 4
        law_need = 16 * min(2 * gg + 2, NSLOT)
        w = [(S["sLAW"], law_need)]
        if gi0 >= 4:
            w.append((S["sX2c"], gi0))
        pl.emit("gpsimd", lambda e, gi0=gi0: e.dma_start(
            out=xbuf2[:, 0:4, :],
            in_=ap_xo.rearrange("(a p) d -> p a d", p=P)[:, gi0:gi0 + 4, :]),
            waits=w, incs=[(S["sX2"], 16)])
        pl.emit("gpsimd", lambda e, gi0=gi0: e.dma_start(
            out=latbuf[:, 0:4, :],
            in_=s_lnat.rearrange("(a p) d -> p a d", p=P)[:, gi0:gi0 + 4, :]),
            waits=w, incs=[(S["sLG"], 16)])

    def emit_part1_group(gg):
        gi0 = gg * 4
        tiles = []
        for j in range(4):
            ww = [(S["sX2"], 16 * (gg + 1)), (S["sLG"], 16 * (gg + 1))]
            xmu = pl.op("vector", lambda e, j=j: e.tensor_add(
                xbuf2[:, j, :], xbuf2[:, j, :], latbuf[:, j, :]), after=ww)
            tiles.append((xbuf2[:, j, :], [xmu]))
        par = gg % 2
        mneg, srs = ln_stats_group(tiles, par, True, stt=stat2, sqt=sq2)
        for j in range(4):
            i = gi0 + j
            after = [srs]
            if i >= 2 and (i - 2) in chain_mu:
                after.append(chain_mu[i - 2])
            nrm = ln_norm(xbuf2[:, j, :], par, j, t16b[:, i % 2, :],
                          after, extra=[(S["sX2c"], 1)], stt=stat2)
            cw = [nrm]
            if i >= 4:
                cw.append((S["sXQW"], 16 * (i // 4)))
            chain_mu[i] = quant_chain(
                t16b[:, i % 2, :], xqstage[:, i % 4, :], D, cw,
                y1t=qc1, zt=qc2)
        pl.emit("gpsimd", lambda e, gi0=gi0: e.dma_start(
            out=s_xq.rearrange("(a p) d -> p a d", p=P)[
                :, gi0:gi0 + 4, :],
            in_=xqstage[:, 0:4, :]),
            waits=[chain_mu[gi0 + 3]], incs=[(S["sXQW"], 16)])

    wq_units = []
    for wi in range(W1TILES):
        for hh in range(4):
            src = ap_w1T[wi * P:(wi + 1) * P, hh * D:(hh + 1) * D]
            wout = s_w1q.rearrange("a p c m -> p a c m")[
                :, 4 * hh:4 * (hh + 1), wi, :]
            wq_units.append((src, D, wout, 256))
    for w2i in range(W2TILES):
        src = ap_w2T[w2i * P:(w2i + 1) * P, :]
        wout = s_w2q.rearrange("a p c o -> p a c o")[:, w2i // 2, w2i % 2, :]
        wq_units.append((src, D, wout, None))
    wq_mu = {}
    wqst = {"load": 0, "chain": 0}

    def emit_wq_unit():
        while wqst["load"] < min(len(wq_units), wqst["chain"] + 2):
            u = wqst["load"]
            src, width, _, _ = wq_units[u]
            lw = [wq_mu[u - 2]] if u >= 2 else []
            pl.emit("gpsimd", lambda e, src=src, width=width, u=u:
                    e.dma_start(out=wstage[:, u % 2, 0:width], in_=src),
                    waits=lw, incs=[(S["sWL"], 16)])
            wqst["load"] += 1
        u = wqst["chain"]
        if u >= len(wq_units):
            return
        src, width, wout, m = wq_units[u]
        ch = quant_chain(wstage[:, u % 2, 0:width], wqout[:, 0:width], width,
                         after=[(S["sWL"], 16 * (u + 1)),
                                (S["sWQw"], 16 * u)],
                         pre_scale=64.0, y1t=wy1, zt=wz)
        wq_mu[u] = ch
        wsrc = wqout[:, 0:width]
        if m is not None:
            wsrc = wsrc.rearrange("p (a m) -> p a m", m=m)
        pl.emit("gpsimd", lambda e, wout=wout, wsrc=wsrc: e.dma_start(
            out=wout, in_=wsrc), waits=[ch], incs=[(S["sWQw"], 16)])
        wqst["chain"] += 1

    pstate = {"par": 0}
    lnat_nrm = {}
    lane_free = {}

    def emit_slot_epilogue(s, lane, last_ev):
        sv = pl.vec_pending
        pl.vec_pending = None
        rcp_mu = pl.op("vector", lambda e, s=s: e.reciprocal(
            rcp[:, 2 * s:2 * s + 2], dsum[:, 2 * s:2 * s + 2]),
            after=[last_ev])
        ev_mus = []
        for qs2 in range(2):
            for oh in range(2):
                mu = pl.op("scalar", lambda e, qs2=qs2, oh=oh, lane=lane,
                           s=s: e.activation(
                               oacc[:, lane, qs2, oh * 512:(oh + 1) * 512],
                               oacc[:, lane, qs2, oh * 512:(oh + 1) * 512],
                               AF.Copy, scale=rcp[:, 2 * s + qs2:
                                                  2 * s + qs2 + 1]),
                           after=[rcp_mu, last_ev])
                ev_mus.append(mu)
        par = pstate["par"]
        tiles = [(oacc[:, lane, 0, :], [ev_mus[0], ev_mus[1]]),
                 (oacc[:, lane, 1, :], [ev_mus[2], ev_mus[3]])]
        if s >= 2:
            tiles[0][1].append(lnat_nrm[s - 2])
        mneg, srs = ln_stats_group(tiles, par, False)
        nrm_mu = None
        for j in range(2):
            after = [srs, (S["sLAW"], 16 * s)] if nrm_mu is None \
                else [srs, nrm_mu]
            nrm_mu = ln_norm(oacc[:, lane, j, :], par, j,
                             lnatst[:, j, :], after)
        lnat_nrm[s] = nrm_mu
        lane_free[lane] = nrm_mu
        pstate["par"] ^= 1
        pl.emit("gpsimd", lambda e, s=s: e.dma_start(
            out=s_lnat.rearrange("(a p) d -> p a d", p=P)[
                :, 2 * s:2 * s + 2, :],
            in_=lnatst[:, 0:2, :]),
            waits=[nrm_mu], incs=[(S["sLAW"], 16)])
        pl.vec_pending = sv

    I = 0
    B = 0
    G = 0
    msk = 0
    av_pend = []
    p1_after = {1: [0], 3: [1], 5: [2], 7: [3]}
    pl.vec_pending = []
    for q2 in range(2):
        GQ = 8 if q2 == 0 else 16
        slots4 = list(range(4 * q2, 4 * q2 + 4))
        for g in range(GQ):
            gw = [(S["sKVF"], G - 1)] if G >= 2 else []
            k0 = 512 * g
            kvsem = S["sKVLa"] if G % 2 == 0 else S["sKVLb"]
            gpar = G % 2
            pl.emit("sync", lambda e, k0=k0, gpar=gpar: e.dma_start(
                out=kgrp[:, gpar, :, :],
                in_=s_kT.rearrange("(a p) t -> p a t", p=P)[
                    :, :, k0:k0 + 512]),
                waits=gw, incs=[(kvsem, 16)])
            pl.emit("sync", lambda e, k0=k0, gpar=gpar: e.dma_start(
                out=vgrp[:, gpar, :, :],
                in_=s_v.rearrange("(a p) d -> p a d", p=P)[
                    :, k0 // P:k0 // P + 4, :]),
                incs=[(kvsem, 16)])
            G += 1
            kvl_wait = (kvsem, kvsem.v)
            act = [s for s in slots4 if g < 2 * s + 2]
            for si, s in enumerate(act):
                lane = s % 4
                last_of_group = (si == len(act) - 1)
                slot_last = (g == 2 * s + 1)
                first_burst = (g == 0)
                emit_wq_unit()
                if g == 2 * s:
                    sv = pl.vec_pending
                    pl.vec_pending = None
                    for m in range(8):
                        w = [(S["sMSK"], 8 * s)]
                        if s == 0 and m == 0:
                            w.append(tm[0])
                        pl.op("vector", lambda e, s=s, m=m: e.tensor_scalar(
                            maskb[:, m, :], iotaD[:, :], thr[:, s, m:m + 1],
                            None, ALU.is_ge), after=w,
                            extra=[(S["sMG"], 1)])
                    pl.vec_pending = sv
                iter_info = []
                for kk in range(4):
                    kb = 4 * g + kk
                    sbank = 4 + I % 2
                    for dc in range(DC):
                        w = []
                        if dc == 0:
                            w = [kvl_wait]
                            if I >= 2:
                                w.append((S["sE"], I - 1))
                        pl.emit("tensor", lambda e, dc=dc, kk=kk, s=s,
                                gpar=gpar, sbank=sbank: e.matmul(
                                    pb[sbank][:, 0:256],
                                    kgrp[:, gpar, dc, kk * P:(kk + 1) * P],
                                    qt[:, dc, s * 256:(s + 1) * 256],
                                    start=(dc == 0), stop=(dc == DC - 1)),
                                waits=w if dc == 0 else (),
                                incs=[(S["sS"], 1)] if dc == DC - 1 else ())
                    w = [(S["sS"], I + 1)]
                    if I >= 8:
                        w.append((S["sAV"], I - 7))
                    pl.op("scalar", lambda e, I=I, sbank=sbank: e.activation(
                        ptb[:, I % 8, :], pb[sbank][:, 0:256], AF.Exp,
                        scale=ISQ), after=w, extra=[(S["sE"], 1)])
                    masked = kb >= 8 * s
                    if masked:
                        m = kb - 8 * s
                        sv = pl.vec_pending
                        pl.vec_pending = None
                        pl.op("vector", lambda e, I=I, m=m: e.tensor_mul(
                            ptb[:, I % 8, :], ptb[:, I % 8, :],
                            maskb[:, m, :]),
                            after=[(S["sE"], I + 1),
                                   (S["sMG"], 8 * s + m + 1)],
                            extra=[(S["sMSK"], 1)])
                        pl.vec_pending = sv
                        msk += 1
                    first_w = [(S["sMSK"], msk)] if masked \
                        else [(S["sE"], I + 1)]
                    iter_info.append((I, kb, tuple(first_w)))
                    pl.pump(3)
                    I += 1

                def make_av_block(s=s, lane=lane, B=B,
                                  iter_info=tuple(iter_info), gpar=gpar,
                                  first_burst=first_burst,
                                  last_of_group=last_of_group):
                    def emit_block():
                        for j, (I0, kb, first_w) in enumerate(iter_info):
                            kk = kb % 4
                            fw = list(first_w)
                            if j == 0 and B >= 1:
                                fw.append((S["sAVE"], 5 * B))
                            nmm = 0
                            for qs2 in range(2):
                                for oh in range(2):
                                    avincs = []
                                    if nmm == 3 and j == 3 and last_of_group:
                                        avincs.append((S["sKVF"], 1))
                                    pl.emit("tensor", lambda e, I0=I0,
                                            qs2=qs2, oh=oh, gpar=gpar, kk=kk:
                                            e.matmul(
                                                pb[2 * qs2 + oh][:, 0:512],
                                                ptb[:, I0 % 8,
                                                    qs2 * P:(qs2 + 1) * P],
                                                vgrp[:, gpar, kk,
                                                     oh * 512:(oh + 1) * 512],
                                                start=(kk == 0),
                                                stop=(kk == 3)),
                                            waits=fw if nmm == 0 else (),
                                            incs=avincs)
                                    nmm += 1
                                pl.emit("tensor", lambda e, I0=I0, qs2=qs2,
                                        j=j: e.matmul(
                                            pb[7][:, qs2:qs2 + 1],
                                            ptb[:, I0 % 8,
                                                qs2 * P:(qs2 + 1) * P],
                                            onesc[:, 0:1],
                                            start=(j == 0 and qs2 == 0),
                                            stop=(j == 3 and qs2 == 1)),
                                        incs=[(S["sAV"], 1)]
                                        if qs2 == 1 else ())
                        sAV_t = iter_info[-1][0] + 1
                        sv = pl.vec_pending
                        pl.vec_pending = None
                        ew = [(S["sAV"], sAV_t)]
                        if first_burst and lane in lane_free:
                            ew.append(lane_free[lane])
                        last = pl.op("vector", lambda e, s=s: e.tensor_add(
                            dsum[:, 2 * s:2 * s + 2],
                            dsum[:, 2 * s:2 * s + 2], pb[7][:, 0:2]),
                            after=ew, extra=[(S["sAVE"], 1)])
                        for qs2 in range(2):
                            for oh in range(2):
                                if first_burst:
                                    last = pl.op(
                                        "vector",
                                        lambda e, qs2=qs2, oh=oh, lane=lane:
                                        e.tensor_copy(
                                            oacc[:, lane, qs2,
                                                 oh * 512:(oh + 1) * 512],
                                            pb[2 * qs2 + oh][:, 0:512]),
                                        extra=[(S["sAVE"], 1)])
                                else:
                                    last = pl.op(
                                        "vector",
                                        lambda e, qs2=qs2, oh=oh, lane=lane:
                                        e.tensor_add(
                                            oacc[:, lane, qs2,
                                                 oh * 512:(oh + 1) * 512],
                                            oacc[:, lane, qs2,
                                                 oh * 512:(oh + 1) * 512],
                                            pb[2 * qs2 + oh][:, 0:512]),
                                        extra=[(S["sAVE"], 1)])
                        pl.vec_pending = sv
                        return last
                    return emit_block
                if av_pend:
                    av_pend.pop(0)()
                av_pend.append(make_av_block())
                B += 1
                if slot_last:
                    last_ev = None
                    while av_pend:
                        last_ev = av_pend.pop(0)()
                    emit_slot_epilogue(s, lane, last_ev)
                    for gg in p1_after.get(s, []):
                        emit_part1_loads(gg)
                        emit_part1_group(gg)
    while wqst["chain"] < len(wq_units):
        emit_wq_unit()
    pl.pump_all()
    pl.vec_pending = None
    pl.barrier()
    stB.close()
    stAB.close()

    stC = contextlib.ExitStack()
    xqT = sb(stC, "xqT", [P, DC, NOWN], BF16)
    for d in range(DC):
        pl.emit("sync", lambda e, d=d: e.dma_start_transpose(
            xqT[:, d, :], s_xq[:, d * P:(d + 1) * P]),
            waits=[(S["sXQW"], 16 * (NHT // 4))],
            incs=[(S["sXQT"], 16)])
    WRING = 6
    w1ch = sb(stC, "w1ch", [P, WRING, DC, 256], BF16)
    w2ch = sb(stC, "w2ch", [P, WRING, 2, D], BF16)
    yacc = sb(stC, "yacc", [P, NT8, 2, D], F32)
    usb = sb(stC, "usb", [P, 3, 512], F32)
    rbuf = sb(stC, "rbuf", [P, 3, 512], BF16)
    ysb = sb(stC, "ysb", [P, 2, D], F32)
    outst = sb(stC, "outst", [P, 2, D], F32)
    xet = sb(stC, "xet", [P, 2, D], F32)
    late = sb(stC, "late", [P, 2, D], F32)
    qc1c = sb(stC, "qc1c", [P, 512], F32)
    qc2c = sb(stC, "qc2c", [P, 512], F32)

    wlsems = [S["sWL2a"], S["sWL2b"], S["sWL2c"], S["sWL2d"],
              S["sWL2e"], S["sWL2f"]]
    for hc in range(NHC):
        lw = []
        if hc >= WRING:
            lw = [(S["sY"], 64 * ((hc - WRING) // 4 + 1))]
        wsem = wlsems[hc % WRING]
        pl.emit("sync", lambda e, hc=hc: e.dma_start(
            out=w1ch[:, hc % WRING, :, :],
            in_=s_w1q.rearrange("a p c m -> p a c m")[:, hc, :, :]),
            waits=lw, incs=[(wsem, 16)])
        pl.emit("sync", lambda e, hc=hc: e.dma_start(
            out=w2ch[:, hc % WRING, :, :],
            in_=s_w2q.rearrange("a p c o -> p a c o")[:, hc, :, :]),
            incs=[(wsem, 16)])

    idx = 0
    MM2_SKEW = 2
    mm2_pend = []

    def emit_mm2(i2, hc, pos, t8q):
        for h2 in range(2):
            step = 2 * pos + h2
            mw = []
            if h2 == 0:
                mw = [(S["sRQ"], i2 + 1)]
                if step == 0:
                    mw.append((S["sYB"], 4 * t8q))
            nmm = 0
            for tsub in range(2):
                for oh in range(2):
                    pl.emit("tensor", lambda e, i2=i2, tsub=tsub, oh=oh,
                            hc=hc, h2=h2, step=step: e.matmul(
                                pb[2 * tsub + oh][:, 0:512],
                                rbuf[:, i2 % 3,
                                     h2 * 256 + tsub * P:
                                     h2 * 256 + (tsub + 1) * P],
                                w2ch[:, hc % WRING, h2,
                                     oh * 512:(oh + 1) * 512],
                                start=(step == 0), stop=(step == 7)),
                            waits=mw if nmm == 0 else (),
                            incs=[(S["sY"], 1)] if nmm == 3 else ())
                    nmm += 1

    ep_state = {"lny_mu_prev": None, "out_mu_prev": None}
    lny_nrm = {}
    out_add_mu = {}

    def emit_epilogue(t8):
        ew = []
        if t8 >= 1:
            ew = [out_add_mu[t8 - 1]]
        pl.emit("sync", lambda e, t8=t8: e.dma_start(
            out=xet[:, 0:2, :],
            in_=ap_xo.rearrange("(a p) d -> p a d", p=P)[
                :, 2 * t8:2 * t8 + 2, :]),
            waits=ew, incs=[(S["sXE"], 16)])
        pl.emit("gpsimd", lambda e, t8=t8: e.dma_start(
            out=late[:, 0:2, :],
            in_=s_lnat.rearrange("(a p) d -> p a d", p=P)[
                :, 2 * t8:2 * t8 + 2, :]),
            waits=ew, incs=[(S["sLE"], 16)])
        yb_mu = None
        for j in range(2):
            w = [(S["sYB"], 4 * (24 + t8 + 1))]
            if ep_state["lny_mu_prev"] is not None and yb_mu is None:
                w.append(ep_state["lny_mu_prev"])
            yb_mu = pl.op("vector", lambda e, j=j, t8=t8:
                          e.scalar_tensor_tensor(
                              ysb[:, j, :], yacc[:, t8, j, :], 1.0 / 1024.0,
                              b2tile[:, :], ALU.mult, ALU.add),
                          after=w)
        tiles = [(ysb[:, j, :], [yb_mu]) for j in range(2)]
        if t8 >= 2:
            tiles[0][1].append(lny_nrm[t8 - 2])
        par = t8 % 2
        mneg, srs = ln_stats_group(tiles, par, False)
        x2a_mu = None
        for j in range(2):
            aw = [(S["sXE"], 16 * (t8 + 1)), (S["sLE"], 16 * (t8 + 1))]
            if x2a_mu is not None:
                aw = [x2a_mu]
            x2a_mu = pl.op("vector", lambda e, j=j: e.tensor_add(
                xet[:, j, :], xet[:, j, :], late[:, j, :]), after=aw)
        out_mu = None
        lny_mu_prev = None
        for j in range(2):
            after = [srs] if lny_mu_prev is None else [srs, lny_mu_prev]
            if j == 0 and ep_state["out_mu_prev"] is not None:
                after.append(ep_state["out_mu_prev"])
            nrm = ln_norm(ysb[:, j, :], par, j, ysb[:, j, :], after)
            lny_mu_prev = nrm
            ow = [nrm, (S["sOUT"], 16 * t8)]
            if j == 0:
                ow.append(x2a_mu)
            out_mu = pl.op("vector", lambda e, t8=t8, j=j: e.tensor_add(
                outst[:, j, :], xet[:, j, :], ysb[:, j, :]),
                after=ow)
        lny_nrm[t8] = lny_mu_prev
        ep_state["lny_mu_prev"] = lny_mu_prev
        ep_state["out_mu_prev"] = out_mu
        out_add_mu[t8] = out_mu
        pl.emit("gpsimd", lambda e, t8=t8: e.dma_start(
            out=ap_out.rearrange("(a p) d -> p a d", p=P)[
                :, 2 * t8:2 * t8 + 2, :],
            in_=outst[:, 0:2, :]),
            waits=[out_mu], incs=[(S["sOUT"], 16)])

    for q in range(4):
        for t8 in range(NT8):
            for hcq in range(4):
                hc = 4 * q + hcq
                i2 = idx
                for h2 in range(2):
                    hbank = 4 + 2 * (i2 % 2) + h2
                    w = [(wlsems[hc % WRING], 32 * (hc // WRING + 1)),
                         (S["sXQT"], 16 * DC)]
                    if 2 * i2 + h2 >= 4:
                        w.append((S["sRL"], 2 * i2 + h2 - 3))
                    for dc in range(DC):
                        pl.emit("tensor", lambda e, dc=dc, h2=h2, t8=t8,
                                hc=hc, hbank=hbank: e.matmul(
                                    pb[hbank][:, 0:256],
                                    w1ch[:, hc % WRING, dc,
                                         h2 * P:(h2 + 1) * P],
                                    xqT[:, dc, t8 * 256:(t8 + 1) * 256],
                                    start=(dc == 0), stop=(dc == DC - 1)),
                                waits=w if dc == 0 else (),
                                incs=[(S["sHM"], 1)] if dc == DC - 1 else ())
                    rw = [(S["sHM"], 2 * i2 + h2 + 1)]
                    if i2 >= 3:
                        rw.append((S["sRQ"], i2 - 2))
                    hcol = 2 * hc + h2
                    pl.op("scalar", lambda e, i2=i2, h2=h2, hcol=hcol,
                          hbank=hbank:
                          e.activation(usb[:, i2 % 3,
                                           h2 * 256:(h2 + 1) * 256],
                                       pb[hbank][:, 0:256],
                                       AF.Relu, bias=b1q16[:, hcol:hcol + 1],
                                       scale=1.0 / 64.0),
                          after=rw, extra=[(S["sRL"], 1)])
                cw = [(S["sRL"], 2 * i2 + 2)]
                if i2 >= 3:
                    cw.append((S["sY"], 2 * i2 - 4))
                quant_chain(usb[:, i2 % 3, :], rbuf[:, i2 % 3, :], 512,
                            cw, extra=[(S["sRQ"], 1)], y1t=qc1c, zt=qc2c)
                mm2_pend.append((i2, hc, i2 % 4, q * 8 + t8))
                if len(mm2_pend) > MM2_SKEW:
                    emit_mm2(*mm2_pend.pop(0))
                idx += 1
            for tsub in range(2):
                for oh in range(2):
                    w = [(S["sY"], 64 * q + 8 * (t8 + 1))]
                    if q >= 1:
                        w.append((S["sYB"], 4 * ((q - 1) * 8 + t8 + 1)))
                    if q == 0:
                        pl.op("scalar", lambda e, tsub=tsub, oh=oh, t8=t8:
                              e.activation(
                                  yacc[:, t8, tsub, oh * 512:(oh + 1) * 512],
                                  pb[2 * tsub + oh][:, 0:512], AF.Copy),
                              after=w, extra=[(S["sYB"], 1)])
                    else:
                        pl.op("vector", lambda e, tsub=tsub, oh=oh, t8=t8:
                              e.tensor_add(
                                  yacc[:, t8, tsub, oh * 512:(oh + 1) * 512],
                                  yacc[:, t8, tsub, oh * 512:(oh + 1) * 512],
                                  pb[2 * tsub + oh][:, 0:512]),
                              after=w, extra=[(S["sYB"], 1)])
            if q == 3:
                emit_epilogue(t8)
    while mm2_pend:
        emit_mm2(*mm2_pend.pop(0))

    pl.waits_only("gpsimd", [(S["sOUT"], 16 * NT8)])
    stC.close()

    with es, nc.Block() as block:
        for ename in ENGN:
            dec = getattr(block, ename)

            def body(eng, fns=pl.fns[ename]):
                for fn in fns:
                    fn(eng)
            dec(body)
    return nc


def own_tiles(c, seq):
    nt = seq // 256
    return [j for j in range(nt) if j % 8 in (c, 7 - c)]


def make_inmaps(seq, x_full, wq, wk, wv, w1, b1, w2, b2):
    f32 = np.float32
    shared = dict(
        wqT=np.ascontiguousarray(wq.T, dtype=f32),
        wkT=np.ascontiguousarray(wk.T, dtype=f32),
        wvT=np.ascontiguousarray(wv.T, dtype=f32),
        w1T=np.ascontiguousarray(w1.T, dtype=f32),
        w2T=np.ascontiguousarray(w2.T, dtype=f32),
        b1t=np.ascontiguousarray(b1.reshape(HID // P, P).T, dtype=f32),
        b2bc=np.ascontiguousarray(np.broadcast_to(b2[None, :], (P, D)),
                                  dtype=f32),
    )
    in_maps = []
    tiles_by_core = []
    for core in range(8):
        b, c = divmod(core, 4)
        tiles = own_tiles(c, seq)
        xb = np.ascontiguousarray(x_full[b], dtype=f32)
        x_own = np.ascontiguousarray(
            np.concatenate([xb[j * 256:(j + 1) * 256] for j in tiles], 0))
        qpos = np.broadcast_to(
            np.array([j * 256 for j in tiles], dtype=f32)[None, :],
            (P, len(tiles)))
        in_maps.append(dict(x=xb, x_own=x_own,
                            qpos=np.ascontiguousarray(qpos), **shared))
        tiles_by_core.append(tiles)
    return in_maps, tiles_by_core


def assemble(seq, results, tiles_by_core):
    B = 2
    out = np.zeros((B, seq, D), dtype=np.float32)
    for core in range(8):
        b = core // 4
        r = np.asarray(results[core]["out"])
        for s, j in enumerate(tiles_by_core[core]):
            out[b, j * 256:(j + 1) * 256] = r[s * 256:(s + 1) * 256]
    return out

from concourse.bass_utils import run_bass_kernel_spmd


def kernel(L0_attn_inputs, L0_attn_Wq, L0_attn_Wk, L0_attn_Wv,
           L0_ffn_w1, L0_ffn_b1, L0_ffn_w2, L0_ffn_b2):
    seq = 8192
    x = np.asarray(L0_attn_inputs, dtype=np.float32)
    in_maps, tiles = make_inmaps(seq, x,
                                 np.asarray(L0_attn_Wq, dtype=np.float32),
                                 np.asarray(L0_attn_Wk, dtype=np.float32),
                                 np.asarray(L0_attn_Wv, dtype=np.float32),
                                 np.asarray(L0_ffn_w1, dtype=np.float32),
                                 np.asarray(L0_ffn_b1, dtype=np.float32),
                                 np.asarray(L0_ffn_w2, dtype=np.float32),
                                 np.asarray(L0_ffn_b2, dtype=np.float32))
    nc = build(seq)
    res = run_bass_kernel_spmd(nc, in_maps, core_ids=list(range(8)))
    return assemble(seq, res.results, tiles)
